# version 08
# baseline (speedup 1.0000x reference)
from contextlib import ExitStack

import numpy as np
import concourse.bass as bass
import concourse.mybir as mybir
from concourse.bass_utils import run_bass_kernel_spmd

F32 = mybir.dt.float32
BF16 = mybir.dt.bfloat16
I32 = mybir.dt.int32
AF = mybir.ActivationFunctionType
ALU = mybir.AluOpType
AX = mybir.AxisListType

D = 1024
NC8 = 8
TFULL = 2048
NEXP = 8
CAP = 2560.0
NHEAD = 16
NKV = 4
HDIM = 64
IR = 4096
ISH = 8192
DH = 4096
F32EPS = float(np.finfo(np.float32).eps)


class Ctr:
    def __init__(self, nc, name, es):
        self.sem = es.enter_context(nc.semaphore(name))
        self.val = 0
        self.name = name

    def inc(self, instr, n=1):
        instr.then_inc(self.sem, n)
        self.val += n
        return self.val


class Kb:
    def __init__(self, nc, es):
        self.nc = nc
        self.es = es
        self.T = Ctr(nc, "cT", es)
        self.V = Ctr(nc, "cV", es)
        self.S = Ctr(nc, "cS", es)
        self.P = Ctr(nc, "cP", es)
        self.Y = Ctr(nc, "dY", es)
        self.G = Ctr(nc, "dG", es)
        self.GW = [Ctr(nc, "dGW0", es), Ctr(nc, "dGW1", es)]
        self.GX = [Ctr(nc, "dGX0", es), Ctr(nc, "dGX1", es)]
        self.YW = [Ctr(nc, "dYW0", es), Ctr(nc, "dYW1", es)]
        self.seen = {}
        self.ctrs = (self.T, self.V, self.S, self.P, self.Y, self.G,
                     self.GW[0], self.GW[1], self.GX[0], self.GX[1],
                     self.YW[0], self.YW[1])

    def eng(self, which):
        return {"T": self.nc.tensor, "V": self.nc.vector, "S": self.nc.scalar,
                "P": self.nc.gpsimd, "Y": self.nc.sync, "G": self.nc.gpsimd}[which]

    def wait(self, ek, ctr, val=None):
        if val is None:
            val = ctr.val
        if val <= 0:
            return
        key = (ek, ctr.name)
        if self.seen.get(key, -1) >= val:
            return
        self.eng(ek).wait_ge(ctr.sem, val)
        self.seen[key] = val

    def cp(self, i, p=128):
        return self.cpool[0:p, i:i + 1]

    def dma(self, qkey, out, in_, ctr=None, **kw):
        ctr = ctr or (self.Y if qkey == "Y" else self.G)
        self.wait(qkey, ctr, ctr.val)
        ins = self.eng(qkey).dma_start(out=out, in_=in_, **kw)
        return ctr.inc(ins, 16)

    def barrier(self, engines="TVSPYG"):
        for e in engines:
            for c in self.ctrs:
                self.wait(e, c)


def emit_consts(k, d):
    nc, es = k.nc, k.es
    cpool = es.enter_context(nc.sbuf_tensor("cpool_sb", [128, 8], F32))
    k.dma("Y", cpool[:, :], d["cpool"])
    k.cpool = cpool
    k.wait("S", k.Y, k.Y.val)
    k.wait("V", k.Y, k.Y.val)
    ones128 = es.enter_context(nc.sbuf_tensor("ones128", [128, 1], F32))
    ones128b = es.enter_context(nc.sbuf_tensor("ones128b", [128, 1], BF16))
    ones1 = es.enter_context(nc.sbuf_tensor("ones1", [1, 128], F32))
    idf = es.enter_context(nc.sbuf_tensor("idf", [128, 128], F32))
    k.ones128b = ones128b
    k.V.inc(nc.vector.memset(ones128b[:, :], 1.0))
    k.V.inc(nc.vector.memset(ones128[:, :], 1.0))
    k.V.inc(nc.vector.memset(ones1[:, :], 1.0))
    k.P.inc(nc.gpsimd.memset(idf[:, :], 1.0))
    k.wait("P", k.P, k.P.val)
    ins = nc.gpsimd.affine_select(idf[:, :], idf[:, :], pattern=[[-1, 128]],
                                  compare_op=ALU.is_equal, fill=0.0, base=0,
                                  channel_multiplier=1)
    pv = k.P.inc(ins)
    idb = es.enter_context(nc.sbuf_tensor("idb", [128, 128], BF16))
    k.wait("V", k.P, pv)
    k.V.inc(nc.vector.tensor_copy(idb[:, :], idf[:, :]))
    k.idb = idb
    return ones128, ones1, idf


def emit_ln(k, tag, X, Xb, ones128, ones1, T, eps=1e-5, out_bf=True,
            rstd_out=None):
    nc, es = k.nc, k.es
    ncol = T // 512
    k.barrier()
    with ExitStack() as ps:
        psAr = ps.enter_context(nc.psum_tensor(f"lnps{tag}", [128, 4096], F32))
        SQ = ps.enter_context(nc.sbuf_tensor(f"lnsq{tag}", [128, T], F32))
        stm = ps.enter_context(nc.sbuf_tensor(f"lnm{tag}", [1, T], F32))
        str_ = ps.enter_context(nc.sbuf_tensor(f"lnr{tag}", [1, T], F32))
        xv = k.V.val
        for c in range(NC8):
            k.wait("S", k.V, xv)
            k.wait("S", k.S, k.S.val)
            k.wait("S", k.T, k.T.val)
            ins = nc.scalar.activation(SQ[:, 0:T], X[:, c * T:(c + 1) * T], AF.Square)
            sv = k.S.inc(ins)
            k.wait("T", k.S, sv)
            k.wait("T", k.V, xv)
            for t in range(ncol):
                sl = slice(t * 512, (t + 1) * 512)
                nc.tensor.matmul(psAr[0:1, sl], ones128[:, 0:1],
                                 X[:, c * T + t * 512:c * T + (t + 1) * 512],
                                 start=(c == 0), stop=(c == NC8 - 1),
                                 skip_group_check=True)
                ins2 = nc.tensor.matmul(
                    psAr[0:1, 2048 + t * 512:2048 + (t + 1) * 512],
                    ones128[:, 0:1], SQ[:, sl],
                    start=(c == 0), stop=(c == NC8 - 1), skip_group_check=True)
            k.T.inc(ins2)
        tv = k.T.val
        k.wait("V", k.T, tv)
        ins = nc.vector.tensor_scalar_mul(stm[0:1, :], psAr[0:1, 0:T], 1.0 / D)
        mv = k.V.inc(ins)
        k.wait("S", k.V, mv)
        ins = nc.scalar.activation(str_[0:1, :], stm[0:1, :], AF.Square)
        sv = k.S.inc(ins)
        k.wait("V", k.S, sv)
        ins = nc.vector.scalar_tensor_tensor(str_[0:1, :], psAr[0:1, 2048:2048 + T],
                                             1.0 / D, str_[0:1, :], ALU.mult,
                                             ALU.subtract)
        vv = k.V.inc(ins)
        k.wait("S", k.V, vv)
        ins = nc.scalar.activation(str_[0:1, :], str_[0:1, :], AF.Ln,
                                   bias=k.cp(0, 1))
        sv = k.S.inc(ins)
        k.wait("S", k.S, sv)
        ins = nc.scalar.activation(str_[0:1, :], str_[0:1, :], AF.Exp,
                                   scale=k.cp(1, 1))
        sv = k.S.inc(ins)
        if rstd_out is not None:
            k.wait("V", k.S, sv)
            k.V.inc(nc.vector.tensor_copy(rstd_out[0:1, :], str_[0:1, :]))
        k.wait("T", k.V, mv)
        k.wait("T", k.S, sv)
        for t in range(ncol):
            sl = slice(t * 512, (t + 1) * 512)
            nc.tensor.matmul(psAr[:, sl], ones1[0:1, :], stm[0:1, sl],
                             start=True, stop=True)
            ins = nc.tensor.matmul(psAr[:, 2048 + t * 512:2048 + (t + 1) * 512],
                                   ones1[0:1, :], str_[0:1, sl], start=True,
                                   stop=True)
        tv = k.T.inc(ins)
        k.wait("V", k.T, tv)
        for c in range(NC8):
            ins = nc.vector.tensor_sub(X[:, c * T:(c + 1) * T],
                                       X[:, c * T:(c + 1) * T], psAr[:, 0:T])
            vv = k.V.inc(ins)
            k.wait("V", k.V, vv)
            ins = nc.vector.tensor_mul(X[:, c * T:(c + 1) * T],
                                       X[:, c * T:(c + 1) * T],
                                       psAr[:, 2048:2048 + T])
            vv = k.V.inc(ins)
            if out_bf:
                k.wait("S", k.V, vv)
                ins = nc.scalar.copy(Xb[:, c * T:(c + 1) * T],
                                     X[:, c * T:(c + 1) * T])
                k.S.inc(ins)
        k.barrier()


def emit_mixer_core(k, Xsrc, G, w, T):
    nc = k.nc
    ncol = T // 512
    wfull, bz, bbcd, alog = w["wfull"], w["bz"], w["bbcd"], w["alog"]
    sB, sC, sDel, sTmp = w["sB"], w["sC"], w["sDel"], w["sTmp"]
    yb, ones16 = w["yb"], w["ones16"]
    psZ, psZ2, psBCD, psy = w["psZ"], w["psZ2"], w["psBCD"], w["psy"]
    ones1 = w["ones1"]

    gelu_last = 0
    for zc in range(NC8):
        for t in range(ncol):
            ps_ = psZ if (zc * ncol + t) % 2 == 0 else psZ2
            for kc in range(8):
                ins = nc.tensor.matmul(
                    ps_[:, :],
                    wfull[:, kc * 1072 + zc * 128:kc * 1072 + (zc + 1) * 128],
                    Xsrc[:, kc * T + t * 512:kc * T + (t + 1) * 512],
                    start=(kc == 0), stop=(kc == 7))
            tv = k.T.inc(ins)
            k.wait("S", k.T, tv)
            ins = nc.scalar.activation(
                G[:, zc * T + t * 512:zc * T + (t + 1) * 512], ps_[:, :],
                AF.Gelu, bias=bz[:, zc:zc + 1])
            gelu_last = k.S.inc(ins)
            k.wait("T", k.S, gelu_last)
    for t in range(ncol):
        for j, c0 in enumerate((1024, 1040, 1056)):
            for kc in range(8):
                ins = nc.tensor.matmul(
                    psBCD[:, j * 512:j * 512 + 512],
                    wfull[:, kc * 1072 + c0:kc * 1072 + c0 + 16],
                    Xsrc[:, kc * T + t * 512:kc * T + (t + 1) * 512],
                    start=(kc == 0), stop=(kc == 7))
        tv = k.T.inc(ins)
        k.wait("V", k.T, tv)
        k.wait("V", k.V, k.V.val)
        for j, dst in enumerate((sB, sC, sDel)):
            ins = nc.vector.tensor_scalar_add(dst[:, t * 512:(t + 1) * 512],
                                              psBCD[:, j * 512:j * 512 + 512],
                                              bbcd[:, j:j + 1])
            k.V.inc(ins)
        k.wait("T", k.V, k.V.val)

    k.wait("S", k.V, k.V.val)
    sv = k.S.inc(nc.scalar.activation(sTmp[:, :], sDel[:, :], AF.Abs))
    k.wait("S", k.S, sv)
    sv = k.S.inc(nc.scalar.activation(sTmp[:, :], sTmp[:, :], AF.Exp,
                                      scale=k.cp(4, 16)))
    k.wait("S", k.S, sv)
    sv = k.S.inc(nc.scalar.activation(sTmp[:, :], sTmp[:, :], AF.Ln,
                                      bias=k.cp(5, 16)))
    k.wait("V", k.S, sv)
    vv = k.V.inc(nc.vector.tensor_scalar_max(sDel[:, :], sDel[:, :], 0.0))
    k.wait("V", k.V, vv)
    vv = k.V.inc(nc.vector.tensor_add(sDel[:, :], sDel[:, :], sTmp[:, :]))
    k.wait("V", k.V, vv)
    vv = k.V.inc(nc.vector.tensor_scalar_mul(sDel[:, :], sDel[:, :],
                                             alog[:, 0:1]))
    k.wait("S", k.V, vv)
    sv = k.S.inc(nc.scalar.activation(sDel[:, :], sDel[:, :], AF.Exp,
                                      scale=k.cp(4, 16)))
    k.wait("V", k.S, sv)
    vv = k.V.inc(nc.vector.tensor_tensor_scan(sTmp[:, :], sDel[:, :], sB[:, :],
                                              0.0, ALU.mult, ALU.add))
    k.wait("V", k.V, vv)
    vv = k.V.inc(nc.vector.tensor_mul(sTmp[:, :], sTmp[:, :], sC[:, :]))
    k.wait("T", k.V, vv)
    k.wait("T", k.S, k.S.val)
    for t in range(ncol):
        tv = k.T.inc(nc.tensor.matmul(psy[0:1, :], ones16[:, 0:1],
                                      sTmp[:, t * 512:(t + 1) * 512],
                                      start=True, stop=True))
        k.wait("V", k.T, tv)
        vv = k.V.inc(nc.vector.tensor_copy(sTmp[0:1, t * 512:(t + 1) * 512],
                                           psy[0:1, :]))
        k.wait("T", k.V, vv)
    for t in range(ncol):
        ps_ = psZ if t % 2 == 0 else psZ2
        tv = k.T.inc(nc.tensor.matmul(ps_[:, :], ones1[0:1, :],
                                      sTmp[0:1, t * 512:(t + 1) * 512],
                                      start=True, stop=True))
        k.wait("V", k.T, tv)
        vv = k.V.inc(nc.vector.tensor_copy(yb[:, t * 512:(t + 1) * 512],
                                           ps_[:, :]))
        k.wait("T", k.V, vv)
    k.wait("V", k.S, gelu_last)
    for c in range(NC8):
        k.wait("V", k.V, k.V.val)
        k.V.inc(nc.vector.tensor_mul(G[:, c * T:(c + 1) * T],
                                     G[:, c * T:(c + 1) * T], yb[:, 0:T]))


def alloc_mixer_scratch(k, ls, tag, T, ones1):
    nc = k.nc
    w = {}
    w["wfull"] = ls.enter_context(nc.sbuf_tensor(f"mx_wf{tag}", [128, 8 * 1072],
                                                 BF16))
    w["bz"] = ls.enter_context(nc.sbuf_tensor(f"mx_bz{tag}", [128, 8], F32))
    w["bbcd"] = ls.enter_context(nc.sbuf_tensor(f"mx_bb{tag}", [16, 3], F32))
    w["alog"] = ls.enter_context(nc.sbuf_tensor(f"mx_al{tag}", [16, 1], F32))
    w["sB"] = ls.enter_context(nc.sbuf_tensor(f"mx_B{tag}", [16, T], F32))
    w["sC"] = ls.enter_context(nc.sbuf_tensor(f"mx_C{tag}", [16, T], F32))
    w["sDel"] = ls.enter_context(nc.sbuf_tensor(f"mx_D{tag}", [16, T], F32))
    w["sTmp"] = ls.enter_context(nc.sbuf_tensor(f"mx_t{tag}", [16, T], F32))
    w["yb"] = ls.enter_context(nc.sbuf_tensor(f"mx_y{tag}", [128, T], BF16))
    w["ones16"] = ls.enter_context(nc.sbuf_tensor(f"mx_o16{tag}", [16, 1], F32))
    w["psZ"] = ls.enter_context(nc.psum_tensor(f"mx_pZ{tag}", [128, 512], F32))
    w["psZ2"] = ls.enter_context(nc.psum_tensor(f"mx_pZ2{tag}", [128, 512], F32))
    w["psBCD"] = ls.enter_context(nc.psum_tensor(f"mx_pB{tag}", [16, 3 * 512],
                                                 F32))
    w["psy"] = ls.enter_context(nc.psum_tensor(f"mx_py{tag}", [1, 512], F32))
    w["ones1"] = ones1
    return w


def load_mixer_weights(k, w, d):
    nc = k.nc
    for name, key in (("bz", "bz"), ("bbcd", "bbcd"), ("alog", "alog")):
        k.dma("Y", w[key][:, :], d[name])
    for kc in range(8):
        k.dma("G", w["wfull"][:, kc * 1072:(kc + 1) * 1072],
              d["w_in"][kc * 128:(kc + 1) * 128, :])
    ldy, ldg = k.Y.val, k.G.val
    k.wait("V", k.Y, ldy)
    k.V.inc(nc.vector.memset(w["ones16"][:, :], 1.0))
    k.wait("S", k.Y, ldy)
    sv = k.S.inc(nc.scalar.activation(w["alog"][:, :], w["alog"][:, :], AF.Exp))
    k.wait("V", k.S, sv)
    k.wait("T", k.G, ldg)
    k.wait("T", k.Y, ldy)
    k.wait("T", k.V, k.V.val)


def emit_l0_attn(k, X, Xb, d, ones128, ones1, T):
    nc, es = k.nc, k.es
    ncol = T // 512
    k.barrier()
    with ExitStack() as ls:
        w = alloc_mixer_scratch(k, ls, "a", T, ones1)
        wob = [ls.enter_context(nc.sbuf_tensor(f"l0wo{i}", [128, 1024], BF16))
               for i in range(2)]
        G = ls.enter_context(nc.sbuf_tensor("l0G", [128, 8 * T], BF16))
        bout = ls.enter_context(nc.sbuf_tensor("l0bout", [128, 8], F32))
        k.dma("Y", bout[:, :], d["bout"])
        load_mixer_weights(k, w, d)
        emit_mixer_core(k, Xb, G, w, T)
        uv = k.V.val
        k.wait("T", k.V, uv)
        k.wait("V", k.Y, k.Y.val)
        psZ, psZ2 = w["psZ"], w["psZ2"]
        for oc in range(NC8):
            b = oc % 2
            k.wait("G", k.T, k.T.val)
            k.dma("G", wob[b][:, :], d["wo_t"][oc])
            k.wait("T", k.G, k.G.val)
            for t in range(ncol):
                ps_ = psZ if (oc * ncol + t) % 2 == 0 else psZ2
                for kc in range(8):
                    ins = nc.tensor.matmul(
                        ps_[:, :], wob[b][:, kc * 128:(kc + 1) * 128],
                        G[:, kc * T + t * 512:kc * T + (t + 1) * 512],
                        start=(kc == 0), stop=(kc == 7))
                tv = k.T.inc(ins)
                k.wait("V", k.T, tv)
                sl = slice(oc * T + t * 512, oc * T + (t + 1) * 512)
                ins = nc.vector.scalar_tensor_tensor(X[:, sl], ps_[:, :],
                                                     bout[:, oc:oc + 1], X[:, sl],
                                                     ALU.add, ALU.add)
                vv = k.V.inc(ins)
                k.wait("T", k.V, vv)
        k.barrier()



def emit_phase0(k, d, ones128, ones1, idf, T, art):
    nc, es = k.nc, k.es
    ncol = T // 512
    ntile = T // 128
    k.barrier()
    with ExitStack() as ls:
        Xf = ls.enter_context(nc.sbuf_tensor("p_x", [128, 8 * T], BF16))
        G = ls.enter_context(nc.sbuf_tensor("p_g", [128, 8 * T], BF16))
        rwtb = ls.enter_context(nc.sbuf_tensor("p_rw", [128, 8 * 8], BF16))
        LX = ls.enter_context(nc.sbuf_tensor("p_lx", [128, 8 * 9], BF16))
        LU = ls.enter_context(nc.sbuf_tensor("p_lu", [128, 8 * 9], BF16))
        boutb = ls.enter_context(nc.sbuf_tensor("p_bo", [128, 8], BF16))
        cfull = ls.enter_context(nc.sbuf_tensor("p_cf", [128, 9], F32))
        swefull = ls.enter_context(nc.sbuf_tensor("p_sw", [128, 9], F32))
        c91 = ls.enter_context(nc.sbuf_tensor("p_c91", [9, 1], F32))
        c19 = ls.enter_context(nc.sbuf_tensor("p_c19", [1, 9], F32))
        PR = ls.enter_context(nc.sbuf_tensor("p_pr", [16, T], F32))
        ptile = ls.enter_context(nc.sbuf_tensor("p_pt", [128, 9], F32))
        pf = ls.enter_context(nc.sbuf_tensor("p_pf", [128, 8], F32))
        pf2 = ls.enter_context(nc.sbuf_tensor("p_pf2", [128, 8], F32))
        scr8 = ls.enter_context(nc.sbuf_tensor("p_s8", [128, 8], F32))
        mm1 = ls.enter_context(nc.sbuf_tensor("p_m1", [128, 1], F32))
        mm2 = ls.enter_context(nc.sbuf_tensor("p_m2", [128, 1], F32))
        mneg = ls.enter_context(nc.sbuf_tensor("p_mn", [128, 1], F32))
        selT = ls.enter_context(nc.sbuf_tensor("p_st", [128, 8 * ntile], F32))
        MSK1b = ls.enter_context(nc.sbuf_tensor("p_k1", [128, 8 * ntile], F32))
        MSK2b = ls.enter_context(nc.sbuf_tensor("p_k2", [128, 8 * ntile], F32))
        dpb = ls.enter_context(nc.sbuf_tensor("p_dp", [128, ntile], F32))
        selEb = ls.enter_context(nc.sbuf_tensor("p_se", [8, T], F32))
        CAc = ls.enter_context(nc.sbuf_tensor("p_ca", [8, 8], F32))
        CAsb = ls.enter_context(nc.sbuf_tensor("p_cas", [8, 8], F32))
        im8 = ls.enter_context(nc.sbuf_tensor("p_im8", [8, 8], F32))
        im128 = ls.enter_context(nc.sbuf_tensor("p_imc", [128, 8], F32))
        pmsb = ls.enter_context(nc.sbuf_tensor("p_pm", [8, 1], F32))

        for kc in range(8):
            k.dma("G", rwtb[:, kc * 8:(kc + 1) * 8],
                  d["rwT"][kc * 128:(kc + 1) * 128, :])
        k.dma("Y", im8[:, :], d["im8"])
        k.dma("Y", im128[:, :], d["im128"])
        k.dma("Y", pmsb[:, :], d["prevmask"])
        ldy, ldg = k.Y.val, k.G.val
        k.wait("V", k.Y, ldy)
        k.wait("V", k.G, ldg)
        k.V.inc(nc.vector.memset(art["selE"][:, :], 0.0))
        k.V.inc(nc.vector.memset(art["msk1"][:, :], 0.0))
        k.V.inc(nc.vector.memset(art["msk2"][:, :], 0.0))
        k.V.inc(nc.vector.memset(art["dp"][:, :], 0.0))
        for kc in range(8):
            k.wait("V", k.V, k.V.val)
            k.V.inc(nc.vector.tensor_copy(LX[:, kc * 9:kc * 9 + 8],
                                          rwtb[:, kc * 8:(kc + 1) * 8]))
            k.V.inc(nc.vector.tensor_scalar_mul(LX[:, kc * 9 + 8:kc * 9 + 9],
                                                ones128[:, 0:1], 1.0 / D))
        lxv = k.V.val

        with ExitStack() as pp:
            wout = pp.enter_context(nc.sbuf_tensor("p_wo", [128, 8 * 1024], BF16))
            WoT = pp.enter_context(nc.sbuf_tensor("p_wot", [128, 8 * 1024], BF16))
            bout = pp.enter_context(nc.sbuf_tensor("p_bof", [128, 8], F32))
            psTb = pp.enter_context(nc.psum_tensor("p_ptb", [128, 128], BF16))
            psLU = pp.enter_context(nc.psum_tensor("p_plu", [128, 9], F32))
            psc = pp.enter_context(nc.psum_tensor("p_pc", [9, 1], F32))
            psT9 = pp.enter_context(nc.psum_tensor("p_pt9", [128, 9], F32))
            for kc in range(8):
                k.dma("G", wout[:, kc * 1024:(kc + 1) * 1024],
                      d["w_out"][kc * 128:(kc + 1) * 128, :])
            k.dma("Y", bout[:, :], d["bout"])
            k.wait("V", k.Y, k.Y.val)
            vv = k.V.inc(nc.vector.tensor_copy(boutb[:, :], bout[:, :]))
            ldg = k.G.val
            k.wait("T", k.G, ldg)
            k.wait("T", k.P, k.P.val)
            k.wait("T", k.V, k.V.val)
            for jc in range(8):
                for dc in range(8):
                    k.wait("T", k.S, k.S.val)
                    tv = k.T.inc(nc.tensor.transpose(
                        psTb[:, :],
                        wout[:, dc * 1024 + jc * 128:dc * 1024 + (jc + 1) * 128],
                        k.idb[:, :]))
                    k.wait("S", k.T, tv)
                    k.S.inc(nc.scalar.copy(
                        WoT[:, jc * 1024 + dc * 128:jc * 1024 + (dc + 1) * 128],
                        psTb[:, :]))
            k.wait("T", k.S, k.S.val)
            k.wait("T", k.V, lxv)
            for dc in range(8):
                for jc in range(8):
                    ins = nc.tensor.matmul(psLU[:, :],
                                           WoT[:, jc * 1024 + dc * 128:
                                               jc * 1024 + (dc + 1) * 128],
                                           LX[:, jc * 9:(jc + 1) * 9],
                                           start=(jc == 0), stop=(jc == 7))
                tv = k.T.inc(ins)
                k.wait("S", k.T, tv)
                sv = k.S.inc(nc.scalar.copy(LU[:, dc * 9:(dc + 1) * 9],
                                            psLU[:, :]))
                k.wait("T", k.S, sv)
            for name in ("c", "s"):
                for dc in range(8):
                    rhs_ap = boutb[:, dc:dc + 1] if name == "c" \
                        else k.ones128b[:, 0:1]
                    ins = nc.tensor.matmul(psc[:, :], LX[:, dc * 9:(dc + 1) * 9],
                                           rhs_ap, start=(dc == 0),
                                           stop=(dc == 7))
                tv = k.T.inc(ins)
                k.wait("V", k.T, tv)
                vv = k.V.inc(nc.vector.tensor_copy(c91[:, :], psc[:, :]))
                k.wait("T", k.V, vv)
                tv = k.T.inc(nc.tensor.transpose(psT9[0:1, 0:9], c91[:, :],
                                                 idf[0:9, 0:9]))
                k.wait("V", k.T, tv)
                vv = k.V.inc(nc.vector.tensor_copy(c19[:, :], psT9[0:1, 0:9]))
                k.wait("T", k.V, vv)
                tv = k.T.inc(nc.tensor.matmul(psT9[:, :], ones1[0:1, :],
                                              c19[0:1, :], start=True,
                                              stop=True))
                k.wait("V", k.T, tv)
                dst = cfull if name == "c" else swefull
                vv = k.V.inc(nc.vector.tensor_copy(dst[:, :], psT9[:, :]))
                k.wait("T", k.V, vv)

        k.barrier()
        w = alloc_mixer_scratch(k, ls, "p", T, ones1)
        load_mixer_weights(k, w, d)
        with ExitStack() as pp:
            ps9 = pp.enter_context(nc.psum_tensor("p_p9", [9, 512], F32))
            psT = pp.enter_context(nc.psum_tensor("p_pT", [128, 128], F32))

            for b8 in range(NEXP):
                k.barrier()
                for c in range(NC8):
                    k.dma("G", Xf[:, c * T:(c + 1) * T],
                          d["x_all"][b8 * D + c * 128:b8 * D + (c + 1) * 128, :])
                k.wait("T", k.G, k.G.val)
                emit_mixer_core(k, Xf, G, w, T)
                uv = k.V.val
                k.wait("T", k.V, uv)
                for t in range(ncol):
                    ts = slice(t * 512, (t + 1) * 512)
                    k.wait("T", k.V, k.V.val)
                    for kc in range(8):
                        nc.tensor.matmul(ps9[:, :], LX[:, kc * 9:(kc + 1) * 9],
                                         Xf[:, kc * T + t * 512:
                                            kc * T + (t + 1) * 512],
                                         start=(kc == 0), stop=False)
                    for kc in range(8):
                        ins = nc.tensor.matmul(ps9[:, :],
                                               LU[:, kc * 9:(kc + 1) * 9],
                                               G[:, kc * T + t * 512:
                                                  kc * T + (t + 1) * 512],
                                               start=False, stop=(kc == 7))
                    tv = k.T.inc(ins)
                    k.wait("V", k.T, tv)
                    vv = k.V.inc(nc.vector.tensor_copy(PR[0:9, ts], ps9[:, :]))
                    k.wait("T", k.V, vv)
                for tt in range(ntile):
                    k.wait("T", k.V, k.V.val)
                    tv = k.T.inc(nc.tensor.transpose(
                        psT[:, 0:9], PR[0:9, tt * 128:(tt + 1) * 128],
                        idf[0:9, 0:9]))
                    k.wait("V", k.T, tv)
                    vv = k.V.inc(nc.vector.tensor_copy(ptile[:, :], psT[:, 0:9]))
                    k.wait("V", k.V, vv)
                    k.wait("T", k.V, vv)
                    vv = k.V.inc(nc.vector.tensor_scalar(
                        mneg[:, :], ptile[:, 8:9], cfull[:, 8:9], -1.0,
                        op0=ALU.add, op1=ALU.mult))
                    k.wait("V", k.V, vv)
                    vv = k.V.inc(nc.vector.tensor_add(scr8[:, :], ptile[:, 0:8],
                                                      cfull[:, 0:8]))
                    k.wait("V", k.V, vv)
                    vv = k.V.inc(nc.vector.scalar_tensor_tensor(
                        pf[:, :], swefull[:, 0:8], mneg[:, 0:1], scr8[:, :],
                        ALU.mult, ALU.add))
                    k.wait("V", k.V, vv)
                    vv = k.V.inc(nc.vector.tensor_reduce(mm1[:, :], pf[:, :],
                                                         axis=AX.X, op=ALU.max))
                    k.wait("V", k.V, vv)
                    vv = k.V.inc(nc.vector.tensor_scalar(
                        MSK1b[:, tt * 8:(tt + 1) * 8], pf[:, :], mm1[:, 0:1],
                        None, op0=ALU.is_equal))
                    k.wait("V", k.V, vv)
                    vv = k.V.inc(nc.vector.scalar_tensor_tensor(
                        pf2[:, :], MSK1b[:, tt * 8:(tt + 1) * 8], -1e30,
                        pf[:, :], ALU.mult, ALU.add))
                    k.wait("V", k.V, vv)
                    vv = k.V.inc(nc.vector.tensor_reduce(mm2[:, :], pf2[:, :],
                                                         axis=AX.X, op=ALU.max))
                    k.wait("V", k.V, vv)
                    vv = k.V.inc(nc.vector.tensor_scalar(
                        MSK2b[:, tt * 8:(tt + 1) * 8], pf2[:, :], mm2[:, 0:1],
                        None, op0=ALU.is_equal))
                    k.wait("V", k.V, vv)
                    vv = k.V.inc(nc.vector.tensor_sub(dpb[:, tt:tt + 1],
                                                      mm1[:, :], mm2[:, :]))
                    k.wait("V", k.V, vv)
                    vv = k.V.inc(nc.vector.tensor_add(
                        selT[:, tt * 8:(tt + 1) * 8],
                        MSK1b[:, tt * 8:(tt + 1) * 8],
                        MSK2b[:, tt * 8:(tt + 1) * 8]))
                k.wait("T", k.V, k.V.val)
                for tt in range(ntile):
                    tv = k.T.inc(nc.tensor.transpose(
                        psT[0:8, :], selT[:, tt * 8:(tt + 1) * 8], idf[:, :]))
                    k.wait("V", k.T, tv)
                    vv = k.V.inc(nc.vector.tensor_copy(
                        selEb[:, tt * 128:(tt + 1) * 128], psT[0:8, :]))
                    k.wait("T", k.V, vv)
                k.wait("V", k.V, k.V.val)
                vv = k.V.inc(nc.vector.tensor_reduce(CAc[:, b8:b8 + 1],
                                                     selEb[:, :], axis=AX.X,
                                                     op=ALU.add))
                k.wait("V", k.V, vv)
                vv = k.V.inc(nc.vector.scalar_tensor_tensor(
                    art["selE"][:, :], selEb[:, :], im8[:, b8:b8 + 1],
                    art["selE"][:, :], ALU.mult, ALU.add))
                k.wait("V", k.V, vv)
                vv = k.V.inc(nc.vector.scalar_tensor_tensor(
                    art["msk1"][:, :], MSK1b[:, :], im128[:, b8:b8 + 1],
                    art["msk1"][:, :], ALU.mult, ALU.add))
                k.wait("V", k.V, vv)
                vv = k.V.inc(nc.vector.scalar_tensor_tensor(
                    art["msk2"][:, :], MSK2b[:, :], im128[:, b8:b8 + 1],
                    art["msk2"][:, :], ALU.mult, ALU.add))
                k.wait("V", k.V, vv)
                vv = k.V.inc(nc.vector.scalar_tensor_tensor(
                    art["dp"][:, :], dpb[:, :], im128[:, b8:b8 + 1],
                    art["dp"][:, :], ALU.mult, ALU.add))
                k.wait("V", k.V, vv)

            k.wait("T", k.V, k.V.val)
            tv = k.T.inc(nc.tensor.transpose(psT[0:8, 0:8], CAc[:, :],
                                             idf[0:8, 0:8]))
            k.wait("V", k.T, tv)
            vv = k.V.inc(nc.vector.tensor_copy(CAsb[:, :], psT[0:8, 0:8]))
            k.wait("T", k.V, vv)
            tv = k.T.inc(nc.tensor.matmul(psT[0:8, 0:1], CAsb[:, :],
                                          pmsb[:, :], start=True, stop=True))
            k.wait("V", k.T, tv)
            k.V.inc(nc.vector.tensor_copy(art["off"][:, :], psT[0:8, 0:1]))
    k.barrier()


def emit_routing2(k, X, Xb, d, ones128, ones1, idf, T, art, rstd, weff,
                  rankI=None, weffT=None):
    nc, es = k.nc, k.es
    ncol = T // 512
    ntile = T // 128
    d_sc = d["scratch1"]
    k.barrier()
    with ExitStack() as ls:
        SQ = ls.enter_context(nc.sbuf_tensor("r2_sq", [128, T], F32))
        rms = ls.enter_context(nc.sbuf_tensor("r2_rms", [1, T], F32))
        rr_tm = ls.enter_context(nc.sbuf_tensor("r2_rmt", [128, ntile], F32))
        rw1 = ls.enter_context(nc.sbuf_tensor("r2_rw", [128, ntile], F32))
        wdT = ls.enter_context(nc.sbuf_tensor("r2_wd", [128, 8 * ntile], F32))
        wdE = ls.enter_context(nc.sbuf_tensor("r2_wde", [8, T], F32))
        csum = ls.enter_context(nc.sbuf_tensor("r2_cs", [8, T], F32))
        ones8 = ls.enter_context(nc.sbuf_tensor("r2_o8", [8, T], F32))
        psS = ls.enter_context(nc.psum_tensor("r2_psS", [1, 2048], F32))
        psT = ls.enter_context(nc.psum_tensor("r2_psT", [8, 128], F32))
        psT2 = ls.enter_context(nc.psum_tensor("r2_psT2", [128, 8], F32))
        k.V.inc(nc.vector.memset(ones8[:, :], 1.0))

        xv = k.V.val
        for c in range(NC8):
            k.wait("S", k.V, xv)
            k.wait("S", k.S, k.S.val)
            k.wait("S", k.T, k.T.val)
            sv = k.S.inc(nc.scalar.activation(SQ[:, 0:T],
                                              X[:, c * T:(c + 1) * T],
                                              AF.Square))
            k.wait("T", k.S, sv)
            for t in range(ncol):
                ins2 = nc.tensor.matmul(psS[0:1, t * 512:(t + 1) * 512],
                                        ones128[:, 0:1],
                                        SQ[:, t * 512:(t + 1) * 512],
                                        start=(c == 0), stop=(c == NC8 - 1),
                                        skip_group_check=True)
            k.T.inc(ins2)
        tv = k.T.val
        k.wait("S", k.T, tv)
        sv = k.S.inc(nc.scalar.activation(rms[0:1, :], psS[0:1, 0:T], AF.Ln,
                                          scale=k.cp(3, 1), bias=k.cp(2, 1)))
        k.wait("S", k.S, sv)
        sv = k.S.inc(nc.scalar.activation(rms[0:1, :], rms[0:1, :], AF.Exp,
                                          scale=k.cp(1, 1)))
        k.wait("V", k.S, sv)
        vv = k.V.inc(nc.vector.tensor_mul(rms[0:1, :], rms[0:1, :],
                                          rstd[0:1, :]))
        k.wait("G", k.V, vv)
        k.dma("G", d_sc[0:1, 0:T], rms[0:1, :])
        k.dma("G", rr_tm[:, 0:ntile],
              d_sc[0:1, 0:T].rearrange("o (a b) -> b (o a)", b=128),
              allow_slow_non_contiguous=True)
        rrv = k.G.val
        k.wait("V", k.G, rrv)
        vv = k.V.inc(nc.vector.tensor_mul(rw1[:, :], art["dp"][:, :],
                                          rr_tm[:, :]))
        k.wait("S", k.V, vv)
        sv = k.S.inc(nc.scalar.activation(rw1[:, :], rw1[:, :], AF.Exp,
                                          scale=k.cp(4)))
        k.wait("V", k.S, sv)
        vv = k.V.inc(nc.vector.tensor_scalar_add(rw1[:, :], rw1[:, :], 1.0))
        k.wait("V", k.V, vv)
        vv = k.V.inc(nc.vector.reciprocal(rw1[:, :], rw1[:, :]))
        k.wait("V", k.V, vv)
        for tt in range(ntile):
            s8 = slice(tt * 8, (tt + 1) * 8)
            vv = k.V.inc(nc.vector.tensor_sub(wdT[:, s8], art["msk1"][:, s8],
                                              art["msk2"][:, s8]))
            k.wait("V", k.V, vv)
            vv = k.V.inc(nc.vector.tensor_scalar_mul(wdT[:, s8], wdT[:, s8],
                                                     rw1[:, tt:tt + 1]))
            k.wait("V", k.V, vv)
            vv = k.V.inc(nc.vector.tensor_add(wdT[:, s8], wdT[:, s8],
                                              art["msk2"][:, s8]))
            k.wait("V", k.V, vv)
        k.wait("T", k.V, k.V.val)
        for tt in range(ntile):
            tv = k.T.inc(nc.tensor.transpose(psT[:, :],
                                             wdT[:, tt * 8:(tt + 1) * 8],
                                             idf[:, :]))
            k.wait("V", k.T, tv)
            vv = k.V.inc(nc.vector.tensor_copy(wdE[:, tt * 128:(tt + 1) * 128],
                                               psT[:, :]))
            k.wait("T", k.V, vv)
        k.wait("V", k.V, k.V.val)
        vv = k.V.inc(nc.vector.tensor_tensor_scan(csum[:, :], ones8[:, :],
                                                  art["selE"][:, :], 0.0,
                                                  ALU.mult, ALU.add))
        if rankI is not None:
            ntl = T // 128
            rkE = ls.enter_context(nc.sbuf_tensor("r2_rk", [8, T], F32))
            rkT = ls.enter_context(nc.sbuf_tensor("r2_rkt", [128, 8 * ntl], F32))
            k.wait("V", k.V, k.V.val)
            vv = k.V.inc(nc.vector.tensor_sub(rkE[:, :], csum[:, :],
                                              art["selE"][:, :]))
            k.wait("V", k.V, vv)
            vv = k.V.inc(nc.vector.tensor_scalar_add(rkE[:, :], rkE[:, :],
                                                     1e6))
            k.wait("V", k.V, vv)
            vv = k.V.inc(nc.vector.scalar_tensor_tensor(
                rkE[:, :], art["selE"][:, :], -1e6, rkE[:, :], ALU.mult,
                ALU.add))
            k.wait("T", k.V, vv)
            for tt in range(ntl):
                tv = k.T.inc(nc.tensor.transpose(
                    psT2[:, :], rkE[:, tt * 128:(tt + 1) * 128], idf[0:8, 0:8]))
                k.wait("V", k.T, tv)
                vv = k.V.inc(nc.vector.tensor_copy(
                    rkT[:, tt * 8:(tt + 1) * 8], psT2[:, :]))
                k.wait("T", k.V, vv)
            k.wait("V", k.V, k.V.val)
            vv = k.V.inc(nc.vector.tensor_copy(rankI[:, :], rkT[:, :]))
        k.wait("V", k.V, k.V.val)
        vv = k.V.inc(nc.vector.tensor_sub(weff[:, :], csum[:, :],
                                          art["selE"][:, :]))
        k.wait("V", k.V, vv)
        vv = k.V.inc(nc.vector.tensor_scalar_add(weff[:, :], weff[:, :],
                                                 art["off"][:, 0:1]))
        k.wait("V", k.V, vv)
        vv = k.V.inc(nc.vector.tensor_scalar(weff[:, :], weff[:, :], CAP,
                                             None, op0=ALU.is_lt))
        k.wait("V", k.V, vv)
        vv = k.V.inc(nc.vector.tensor_mul(weff[:, :], weff[:, :],
                                          art["selE"][:, :]))
        k.wait("V", k.V, vv)
        vv = k.V.inc(nc.vector.tensor_mul(weff[:, :], weff[:, :], wdE[:, :]))
        if weffT is not None:
            k.wait("T", k.V, vv)
            for tt in range(ntile):
                tv = k.T.inc(nc.tensor.transpose(
                    psT2[:, :], weff[:, tt * 128:(tt + 1) * 128],
                    idf[0:8, 0:8]))
                k.wait("V", k.T, tv)
                vv = k.V.inc(nc.vector.tensor_copy(
                    weffT[:, tt * 8:(tt + 1) * 8], psT2[:, :]))
                k.wait("T", k.V, vv)
    k.barrier()


def emit_routing(k, X, Xb, d, ones128, ones1, idf, T, route_sb):
    nc, es = k.nc, k.es
    ncol = T // 512
    ntile = T // 128
    k.barrier()
    selE, wdE, csum, counts = (route_sb["selE"], route_sb["wdE"],
                               route_sb["csum"], route_sb["counts"])
    d_sc = d["scratch1"]
    with ExitStack() as ls:
        rwt = ls.enter_context(nc.sbuf_tensor("rt_w", [128, 8 * 8], F32))
        SQ = ls.enter_context(nc.sbuf_tensor("rt_sq", [128, T], F32))
        rms = ls.enter_context(nc.sbuf_tensor("rt_rms", [1, T], F32))
        rms_tm = ls.enter_context(nc.sbuf_tensor("rt_rmt", [128, ntile], F32))
        lgT = ls.enter_context(nc.sbuf_tensor("rt_lg", [128, 8 * ntile], F32))
        m1 = ls.enter_context(nc.sbuf_tensor("rt_m1", [128, ntile], F32))
        m2 = ls.enter_context(nc.sbuf_tensor("rt_m2", [128, ntile], F32))
        msk1 = ls.enter_context(nc.sbuf_tensor("rt_k1", [128, 8 * ntile], F32))
        msk2 = ls.enter_context(nc.sbuf_tensor("rt_k2", [128, 8 * ntile], F32))
        rw1 = ls.enter_context(nc.sbuf_tensor("rt_rw1", [128, ntile], F32))
        selT = ls.enter_context(nc.sbuf_tensor("rt_selT", [128, 8 * ntile], F32))
        wdT = ls.enter_context(nc.sbuf_tensor("rt_wdT", [128, 8 * ntile], F32))
        ones8 = ls.enter_context(nc.sbuf_tensor("rt_o8", [8, T], F32))
        psS = ls.enter_context(nc.psum_tensor("rt_psS", [1, 2048], F32))
        pslg = ls.enter_context(nc.psum_tensor("pslg", [128, 8], F32))
        pslg2 = ls.enter_context(nc.psum_tensor("pslg2", [128, 8], F32))
        pstr = ls.enter_context(nc.psum_tensor("pstr", [8, 128], F32))
        pstr2 = ls.enter_context(nc.psum_tensor("pstr2", [8, 128], F32))

        for kc in range(8):
            k.dma("G", rwt[:, kc * 8:(kc + 1) * 8],
                                        d["rwT"][kc * 128:(kc + 1) * 128, :])
        ldg = k.G.val
        k.V.inc(nc.vector.memset(ones8[:, :], 1.0))

        xv = k.V.val
        for c in range(NC8):
            k.wait("S", k.V, xv)
            k.wait("S", k.S, k.S.val)
            k.wait("S", k.T, k.T.val)
            ins = nc.scalar.activation(SQ[:, 0:T], X[:, c * T:(c + 1) * T],
                                       AF.Square)
            sv = k.S.inc(ins)
            k.wait("T", k.S, sv)
            for t in range(ncol):
                ins2 = nc.tensor.matmul(psS[0:1, t * 512:(t + 1) * 512],
                                        ones128[:, 0:1],
                                        SQ[:, t * 512:(t + 1) * 512],
                                        start=(c == 0), stop=(c == NC8 - 1),
                                        skip_group_check=True)
            k.T.inc(ins2)
        tv = k.T.val
        k.wait("S", k.T, tv)
        ins = nc.scalar.activation(rms[0:1, :], psS[0:1, 0:T], AF.Ln,
                                   scale=k.cp(3, 1), bias=k.cp(2, 1))
        sv = k.S.inc(ins)
        k.wait("S", k.S, sv)
        sv = k.S.inc(nc.scalar.activation(rms[0:1, :], rms[0:1, :], AF.Exp,
                                          scale=k.cp(1, 1)))
        k.wait("G", k.S, sv)
        k.dma("G", d_sc[0:1, 0:T], rms[0:1, :])
        k.wait("G", k.G, k.G.val)
        k.dma("G", 
            rms_tm[:, 0:ntile],
            d_sc[0:1, 0:T].rearrange("o (a b) -> b (o a)", b=128),
            allow_slow_non_contiguous=True)
        rmsv = k.G.val

        k.wait("T", k.G, ldg)
        k.wait("T", k.Y, k.Y.val)
        for tt in range(ntile):
            ps_ = pslg if tt % 2 == 0 else pslg2
            for kc in range(8):
                ins = nc.tensor.matmul(
                    ps_[:, :], X[:, kc * T + tt * 128:kc * T + (tt + 1) * 128],
                    rwt[:, kc * 8:(kc + 1) * 8], start=(kc == 0), stop=(kc == 7))
            tv = k.T.inc(ins)
            k.wait("V", k.T, tv)
            vv = k.V.inc(nc.vector.tensor_copy(lgT[:, tt * 8:(tt + 1) * 8],
                                               ps_[:, :]))
            k.wait("T", k.V, vv)
        for tt in range(ntile):
            s8 = slice(tt * 8, (tt + 1) * 8)
            s1_ = slice(tt, tt + 1)
            k.wait("V", k.V, k.V.val)
            vv = k.V.inc(nc.vector.tensor_reduce(m1[:, s1_], lgT[:, s8],
                                                 axis=AX.X, op=ALU.max))
            k.wait("V", k.V, vv)
            vv = k.V.inc(nc.vector.tensor_scalar(msk1[:, s8], lgT[:, s8],
                                                 m1[:, s1_], None,
                                                 op0=ALU.is_equal))
            k.wait("V", k.V, vv)
            vv = k.V.inc(nc.vector.scalar_tensor_tensor(msk2[:, s8], msk1[:, s8],
                                                        -1e30, lgT[:, s8],
                                                        ALU.mult, ALU.add))
            k.wait("V", k.V, vv)
            vv = k.V.inc(nc.vector.tensor_reduce(m2[:, s1_], msk2[:, s8],
                                                 axis=AX.X, op=ALU.max))
            k.wait("V", k.V, vv)
            vv = k.V.inc(nc.vector.tensor_scalar(msk2[:, s8], msk2[:, s8],
                                                 m2[:, s1_], None,
                                                 op0=ALU.is_equal))
            k.wait("V", k.V, vv)
            vv = k.V.inc(nc.vector.tensor_sub(m1[:, s1_], m1[:, s1_], m2[:, s1_]))
            k.wait("V", k.G, rmsv)
            k.wait("V", k.V, vv)
            vv = k.V.inc(nc.vector.tensor_mul(m1[:, s1_], m1[:, s1_],
                                              rms_tm[:, s1_]))
            k.wait("S", k.V, vv)
            sv = k.S.inc(nc.scalar.activation(m2[:, s1_], m1[:, s1_], AF.Exp,
                                              scale=k.cp(4)))
            k.wait("V", k.S, sv)
            vv = k.V.inc(nc.vector.tensor_scalar_add(m2[:, s1_], m2[:, s1_], 1.0))
            k.wait("V", k.V, vv)
            vv = k.V.inc(nc.vector.reciprocal(rw1[:, s1_], m2[:, s1_]))
            k.wait("V", k.V, vv)
            k.V.inc(nc.vector.tensor_add(selT[:, s8], msk1[:, s8], msk2[:, s8]))
            vv = k.V.inc(nc.vector.tensor_sub(wdT[:, s8], msk1[:, s8],
                                              msk2[:, s8]))
            k.wait("V", k.V, vv)
            vv = k.V.inc(nc.vector.tensor_scalar_mul(wdT[:, s8], wdT[:, s8],
                                                     rw1[:, s1_]))
            k.wait("V", k.V, vv)
            vv = k.V.inc(nc.vector.tensor_add(wdT[:, s8], wdT[:, s8],
                                              msk2[:, s8]))
        topv = k.V.val
        k.wait("T", k.V, topv)
        for tt in range(ntile):
            tv = k.T.inc(nc.tensor.transpose(pstr[:, :],
                                             selT[:, tt * 8:(tt + 1) * 8],
                                             idf[:, :]))
            k.wait("V", k.T, tv)
            vv = k.V.inc(nc.vector.tensor_copy(selE[:, tt * 128:(tt + 1) * 128],
                                               pstr[:, :]))
            k.wait("T", k.V, vv)
            tv = k.T.inc(nc.tensor.transpose(pstr2[:, :],
                                             wdT[:, tt * 8:(tt + 1) * 8],
                                             idf[:, :]))
            k.wait("V", k.T, tv)
            vv = k.V.inc(nc.vector.tensor_copy(wdE[:, tt * 128:(tt + 1) * 128],
                                               pstr2[:, :]))
            k.wait("T", k.V, vv)
        k.wait("V", k.V, k.V.val)
        vv = k.V.inc(nc.vector.tensor_tensor_scan(csum[:, :], ones8[:, :],
                                                  selE[:, :], 0.0, ALU.mult,
                                                  ALU.add))
        k.wait("V", k.V, vv)
        k.V.inc(nc.vector.tensor_copy(counts[:, 0:1], csum[:, T - 1:T]))
        k.barrier()


def emit_weff(k, d, route_sb, T):
    nc, es = k.nc, k.es
    weff = route_sb["weff"]
    selE, wdE, csum = route_sb["selE"], route_sb["wdE"], route_sb["csum"]
    with ExitStack() as ls:
        ca = ls.enter_context(nc.sbuf_tensor("wf_ca", [8, 8], F32))
        pm = ls.enter_context(nc.sbuf_tensor("wf_pm", [8, 1], F32))
        off = ls.enter_context(nc.sbuf_tensor("wf_off", [8, 1], F32))
        psoff = ls.enter_context(nc.psum_tensor("psoff", [8, 1], F32))
        k.dma("Y", ca[:, :], d["counts_all"])
        k.dma("Y", pm[:, :], d["prevmask"])
        ldy = k.Y.val
        k.wait("T", k.Y, ldy)
        tv = k.T.inc(nc.tensor.matmul(psoff[:, :], ca[:, :], pm[:, :],
                                      start=True, stop=True))
        k.wait("V", k.T, tv)
        vv = k.V.inc(nc.vector.tensor_copy(off[:, 0:1], psoff[:, :]))
        k.wait("V", k.V, vv)
        vv = k.V.inc(nc.vector.tensor_sub(weff[:, :], csum[:, :], selE[:, :]))
        k.wait("V", k.V, vv)
        vv = k.V.inc(nc.vector.tensor_scalar_add(weff[:, :], weff[:, :],
                                                 off[:, 0:1]))
        k.wait("V", k.V, vv)
        vv = k.V.inc(nc.vector.tensor_scalar(weff[:, :], weff[:, :], CAP, None,
                                             op0=ALU.is_lt))
        k.wait("V", k.V, vv)
        vv = k.V.inc(nc.vector.tensor_mul(weff[:, :], weff[:, :], selE[:, :]))
        k.wait("V", k.V, vv)
        k.V.inc(nc.vector.tensor_mul(weff[:, :], weff[:, :], wdE[:, :]))
        k.barrier()


def emit_mm_phase(k, d_wt, n_ob, K, rhs_sb, rhs_T, wb, psP, evac,
                  kc0=0, tok0=0, n_tcol=None):
    nc = k.nc
    nkc = K // 128
    if n_tcol is None:
        n_tcol = rhs_T // 512
    nbuf = len(wb)
    k.wait("T", k.V, k.V.val)
    k.wait("T", k.S, k.S.val)
    war = [(k.T, k.T.val)] * nbuf
    ps_war = [None, None]
    for ob in range(n_ob):
        b = ob % nbuf
        if war[b] is not None:
            ctr, v = war[b]
            k.wait("G", ctr, v)
        ld = k.dma("G", wb[b][:, 0:K], d_wt[ob][:, kc0 * 128:kc0 * 128 + K],
                   ctr=k.GW[b % 2])
        k.wait("T", k.GW[b % 2], ld)
        for t in range(n_tcol):
            pb = (ob * n_tcol + t) % 2
            ps_ = psP[pb]
            if ps_war[pb] is not None:
                ctr, v = ps_war[pb]
                k.wait("T", ctr, v)
            for kc in range(nkc):
                ins = nc.tensor.matmul(
                    ps_[:, :], wb[b][:, kc * 128:(kc + 1) * 128],
                    rhs_sb[:, kc * rhs_T + tok0 + t * 512:
                           kc * rhs_T + tok0 + (t + 1) * 512],
                    start=(kc == 0), stop=(kc == nkc - 1))
            tv = k.T.inc(ins)
            ps_war[pb] = evac(ob, t, ps_, tv)
        war[b] = (k.T, k.T.val)


CLOC = 1024


def emit_moe_routed(k, Xb, d, ones128, ones1, idf, rankI, weffT, T):
    nc, es = k.nc, k.es
    ncol = T // 512
    ntile = T // 128
    k.barrier()
    bregT = nc.gpsimd.to_reg(T - 1)
    bregC = nc.gpsimd.to_reg(CLOC - 1)
    with ExitStack() as ls:
        wgu = [ls.enter_context(nc.sbuf_tensor(f"m_wgu{i}", [128, D], BF16))
               for i in range(2)]
        wdn = [ls.enter_context(nc.sbuf_tensor(f"m_wdn{i}", [128, 2048], BF16))
               for i in range(2)]
        iotaF = ls.enter_context(nc.sbuf_tensor("m_iota", [128, ntile], F32))
        bigF = ls.enter_context(nc.sbuf_tensor("m_big", [128, 8], F32))
        zero1k = ls.enter_context(nc.sbuf_tensor("m_z", [128, 1024], F32))
        pairt = [ls.enter_context(nc.sbuf_tensor(f"m_pt{i}", [128, 2], F32))
                 for i in range(2)]
        dispbF = ls.enter_context(nc.sbuf_tensor("m_dbf", [128, 1], F32))
        xrow = [ls.enter_context(nc.sbuf_tensor(f"m_xr{i}", [128, 1024], BF16))
                for i in range(2)]
        eo = [ls.enter_context(nc.sbuf_tensor(f"m_eo{i}", [128, 1024], F32))
              for i in range(2)]
        psP = [ls.enter_context(nc.psum_tensor(f"m_ps{i}", [128, 512], F32))
               for i in range(2)]
        psTb = ls.enter_context(nc.psum_tensor("m_ptb", [128, 128], BF16))

        k.P.inc(nc.gpsimd.iota(iotaF[:, :], pattern=[[128, ntile]], base=0,
                               channel_multiplier=1,
                               allow_small_or_imprecise_dtypes=True))
        k.P.inc(nc.gpsimd.iota(bigF[:, :], pattern=[[0, 8]], base=1000000,
                               channel_multiplier=0,
                               allow_small_or_imprecise_dtypes=True))
        k.V.inc(nc.vector.memset(zero1k[:, :], 0.0))
        k.barrier()

        for tt in range(ntile):
            b = tt % 2
            k.wait("S", k.YW[b], k.YW[b].val)
            for c in range(NC8):
                k.wait("T", k.S, k.S.val)
                tv = k.T.inc(nc.tensor.transpose(
                    psTb[:, :], Xb[:, c * T + tt * 128:c * T + (tt + 1) * 128],
                    k.idb[:, :]))
                k.wait("S", k.T, tv)
                k.S.inc(nc.scalar.copy(xrow[b][:, c * 128:(c + 1) * 128],
                                       psTb[:, :]))
            k.wait("Y", k.S, k.S.val)
            k.wait("Y", k.V, k.V.val)
            k.dma("Y", d["x2tm"][tt * 128:(tt + 1) * 128, :], xrow[b][:, :],
                  ctr=k.YW[b])
            k.dma("Y", d["routedtm"][tt * 128:(tt + 1) * 128, :], zero1k[:, :])
        k.barrier()

        for ex in range(NEXP):
            with ExitStack() as se:
                dispb = se.enter_context(nc.sbuf_tensor(f"m_db{ex}", [128, 8],
                                                        I32))
                wdsp = se.enter_context(nc.sbuf_tensor(f"m_wd{ex}", [128, 8],
                                                       F32))
                XD = se.enter_context(nc.sbuf_tensor(f"m_xd{ex}",
                                                     [128, 8 * CLOC], BF16))
                k.wait("G", k.P, k.P.val)
                k.wait("G", k.V, k.V.val)
                k.dma("G", d["paird"][ex * CLOC:(ex + 1) * CLOC, 0:1],
                      bigF[:, :], allow_slow_non_contiguous=True)
                k.dma("G", d["paird"][ex * CLOC:(ex + 1) * CLOC, 1:2],
                      zero1k[:, 0:8], allow_slow_non_contiguous=True)
                k.wait("G", k.G, k.G.val)
                for tt in range(ntile):
                    bx = tt % 2
                    k.wait("V", k.GX[0], k.GX[0].val)
                    k.wait("V", k.GX[1], k.GX[1].val)
                    k.V.inc(nc.vector.tensor_copy(pairt[bx][:, 0:1],
                                                  iotaF[:, tt:tt + 1]))
                    vv = k.V.inc(nc.vector.tensor_copy(
                        pairt[bx][:, 1:2],
                        weffT[:, tt * 8 + ex:tt * 8 + ex + 1]))
                    k.wait("G", k.V, vv)
                    ins = nc.gpsimd.indirect_dma_start(
                        out=d["paird"],
                        out_offset=bass.IndirectOffsetOnAxis(
                            ap=rankI[:, tt * 8 + ex:tt * 8 + ex + 1], axis=0),
                        in_=pairt[bx][:, :], in_offset=None,
                        element_offset=ex * CLOC * 2,
                        bounds_check=bregC, oob_is_err=False)
                    k.GX[bx].inc(ins, 16)
                k.wait("Y", k.GX[0], k.GX[0].val)
                k.wait("Y", k.GX[1], k.GX[1].val)
                for st in range(8):
                    k.wait("Y", k.V, k.V.val)
                    k.dma("Y", dispbF[:, 0:1],
                          d["paird"][ex * CLOC + st * 128:
                                     ex * CLOC + (st + 1) * 128, 0:1],
                          allow_slow_non_contiguous=True)
                    k.wait("V", k.Y, k.Y.val)
                    vv = k.V.inc(nc.vector.tensor_copy(dispb[:, st:st + 1],
                                                       dispbF[:, 0:1]))
                    k.dma("Y", wdsp[:, st:st + 1],
                          d["paird"][ex * CLOC + st * 128:
                                     ex * CLOC + (st + 1) * 128, 1:2],
                          allow_slow_non_contiguous=True)
                with ExitStack() as sg_:
                    xg = sg_.enter_context(nc.sbuf_tensor(f"m_xg{ex}",
                                                          [128, 8 * 1024],
                                                          BF16))
                    if ex == 0:
                        k.wait("V", k.G, k.G.val)
                        k.V.inc(nc.vector.memset(xg[:, :], 0.0))
                        k.wait("G", k.V, k.V.val)
                    k.wait("G", k.V, k.V.val)
                    k.wait("G", k.T, k.T.val)
                    gx_evt = [0, 0]
                    for st in range(8):
                        bx = st % 2
                        ins = nc.gpsimd.indirect_dma_start(
                            out=xg[:, st * 1024:(st + 1) * 1024],
                            out_offset=None, in_=d["x2tm"],
                            in_offset=bass.IndirectOffsetOnAxis(
                                ap=dispb[:, st:st + 1], axis=0),
                            bounds_check=bregT, oob_is_err=False)
                        gx_evt[bx] = k.GX[bx].inc(ins, 16)
                    k.wait("T", k.GX[0], gx_evt[0])
                    k.wait("T", k.GX[1], gx_evt[1])
                    for st in range(8):
                        for c in range(NC8):
                            k.wait("T", k.S, k.S.val)
                            tv = k.T.inc(nc.tensor.transpose(
                                psTb[:, :],
                                xg[:, st * 1024 + c * 128:
                                   st * 1024 + (c + 1) * 128],
                                k.idb[:, :]))
                            k.wait("S", k.T, tv)
                            k.S.inc(nc.scalar.copy(
                                XD[:, c * CLOC + st * 128:
                                   c * CLOC + (st + 1) * 128], psTb[:, :]))
                k.barrier()
                with ExitStack() as sh:
                    He = sh.enter_context(nc.sbuf_tensor(f"m_he{ex}",
                                                         [128, 16 * CLOC],
                                                         BF16))
                    rdres = sh.enter_context(nc.sbuf_tensor(f"m_rd{ex}",
                                                            [128, 16 * 1024],
                                                            BF16))
                    for half in range(2):
                        def evac_gate(ob, t, ps_, tv):
                            k.wait("S", k.T, tv)
                            ins = nc.scalar.activation(
                                He[:, ob * CLOC + t * 512:
                                   ob * CLOC + (t + 1) * 512],
                                ps_[:, :], AF.Silu)
                            return (k.S, k.S.inc(ins))

                        def evac_up(ob, t, ps_, tv):
                            sl = slice(ob * CLOC + t * 512,
                                       ob * CLOC + (t + 1) * 512)
                            k.wait("V", k.T, tv)
                            k.wait("V", k.S, k.S.val)
                            k.wait("V", k.V, k.V.val)
                            ins = nc.vector.tensor_mul(He[:, sl], He[:, sl],
                                                       ps_[:, :])
                            return (k.V, k.V.inc(ins))
                        emit_mm_phase(k, d["rg"][ex][half * 16:(half + 1) * 16],
                                      16, D, XD, CLOC, wgu, psP, evac_gate)
                        emit_mm_phase(k, d["ru"][ex][half * 16:(half + 1) * 16],
                                      16, D, XD, CLOC, wgu, psP, evac_up)
                        k.wait("G", k.T, k.T.val)
                        ldrd = k.dma("G",
                                     rdres.rearrange("p (a n) -> p a n", a=16),
                                     d["rdk"][ex][half * 16:(half + 1) * 16]
                                     .rearrange("a p n -> p a n"),
                                     ctr=k.GW[0])
                        k.wait("T", k.GW[0], ldrd)
                        k.wait("T", k.S, k.S.val)
                        k.wait("T", k.V, k.V.val)
                        for st in range(8):
                            b = st % 2
                            k.wait("V", k.GX[b], k.GX[b].val)
                            k.wait("T", k.V, k.V.val)
                            for oc2 in range(2):
                                for kc in range(16):
                                    ins = nc.tensor.matmul(
                                        psP[oc2][:, :],
                                        He[:, kc * CLOC + st * 128:
                                           kc * CLOC + (st + 1) * 128],
                                        rdres[:, kc * 1024 + oc2 * 512:
                                              kc * 1024 + oc2 * 512 + 512],
                                        start=(kc == 0), stop=(kc == 15))
                                tv = k.T.inc(ins)
                                k.wait("V", k.T, tv)
                                vv = k.V.inc(nc.vector.tensor_scalar_mul(
                                    eo[b][:, oc2 * 512:(oc2 + 1) * 512],
                                    psP[oc2][:, :], wdsp[:, st:st + 1]))
                            k.wait("G", k.V, vv)
                            ins = nc.gpsimd.indirect_dma_start(
                                out=d["routedtm"],
                                out_offset=bass.IndirectOffsetOnAxis(
                                    ap=dispb[:, st:st + 1], axis=0),
                                in_=eo[b][:, :], in_offset=None,
                                bounds_check=bregT, oob_is_err=False,
                                compute_op=ALU.add)
                            k.GX[b].inc(ins, 16)
                k.barrier()

    k.barrier()


def emit_moe_shared(k, X, Xb, H, d, ones128, ones1, idf, T):
    nc, es = k.nc, k.es
    ntile = T // 128
    k.barrier()
    with ExitStack() as ls:
        wgu = [ls.enter_context(nc.sbuf_tensor(f"s_wgu{i}", [128, D], BF16))
               for i in range(2)]
        wdn = [ls.enter_context(nc.sbuf_tensor(f"s_wdn{i}", [128, 2048], BF16))
               for i in range(2)]
        psP = [ls.enter_context(nc.psum_tensor(f"s_ps{i}", [128, 512], F32))
               for i in range(2)]
        with ExitStack() as sr:
            rin = [sr.enter_context(nc.sbuf_tensor(f"m_ri{i}", [128, 1024],
                                                   F32)) for i in range(2)]
            psTf = sr.enter_context(nc.psum_tensor("m_ptf", [128, 128], F32))
            for tt in range(ntile):
                b = tt % 2
                k.wait("G", k.T, k.T.val)
                k.dma("G", rin[b][:, :], d["routedtm"][tt * 128:(tt + 1) * 128, :])
                k.wait("T", k.G, k.G.val)
                for c in range(NC8):
                    k.wait("T", k.V, k.V.val)
                    tv = k.T.inc(nc.tensor.transpose(
                        psTf[:, :], rin[b][:, c * 128:(c + 1) * 128],
                        idf[:, :]))
                    k.wait("V", k.T, tv)
                    sl = slice(c * T + tt * 128, c * T + (tt + 1) * 128)
                    vv = k.V.inc(nc.vector.tensor_add(X[:, sl], X[:, sl],
                                                      psTf[:, :]))
        k.barrier()

        def evac_gate_s(ob, t, ps_, tv):
            k.wait("S", k.T, tv)
            ins = nc.scalar.activation(
                H[:, ob * T + t * 512:ob * T + (t + 1) * 512], ps_[:, :],
                AF.Silu)
            return (k.S, k.S.inc(ins))

        def evac_up_s(ob, t, ps_, tv):
            sl = slice(ob * T + t * 512, ob * T + (t + 1) * 512)
            k.wait("V", k.T, tv)
            k.wait("V", k.S, k.S.val)
            k.wait("V", k.V, k.V.val)
            ins = nc.vector.tensor_mul(H[:, sl], H[:, sl], ps_[:, :])
            return (k.V, k.V.inc(ins))

        def evac_down_s(ob, t, ps_, tv):
            sl = slice(ob * T + t * 512, ob * T + (t + 1) * 512)
            k.wait("V", k.T, tv)
            k.wait("V", k.V, k.V.val)
            ins = nc.vector.tensor_add(X[:, sl], X[:, sl], ps_[:, :])
            return (k.V, k.V.inc(ins))

        for q in range(4):
            emit_mm_phase(k, d["sg"][q * 16:(q + 1) * 16], 16, D, Xb, T,
                          wgu, psP, evac_gate_s)
            emit_mm_phase(k, d["su"][q * 16:(q + 1) * 16], 16, D, Xb, T,
                          wgu, psP, evac_up_s)
            emit_mm_phase(k, d["sd_h"][q // 2], NC8, 2048, H, T, wdn, psP,
                          evac_down_s, kc0=(q % 2) * 16)
    k.barrier()



def emit_l1_attn(k, X, Xb, d, ones128, ones1, T):
    nc, es = k.nc, k.es
    ncol = T // 512
    ntile = T // 128
    k.barrier()

    def norm_pipeline(src64, sl, ps1, nq1, psB, SQ64):
        k.wait("S", k.S, k.S.val)
        k.wait("S", k.T, k.T.val)
        k.wait("S", k.V, k.V.val)
        sv = k.S.inc(nc.scalar.activation(SQ64[:, :], src64[:, sl], AF.Square))
        k.wait("T", k.S, sv)
        tv = k.T.inc(nc.tensor.matmul(ps1[0:1, :], ones128[0:64, 0:1],
                                      SQ64[:, :], start=True, stop=True))
        k.wait("V", k.T, tv)
        vv = k.V.inc(nc.vector.tensor_copy(nq1[0:1, :], ps1[0:1, :]))
        k.wait("S", k.V, vv)
        sv = k.S.inc(nc.scalar.activation(nq1[0:1, :], nq1[0:1, :], AF.Ln))
        k.wait("S", k.S, sv)
        sv = k.S.inc(nc.scalar.activation(nq1[0:1, :], nq1[0:1, :], AF.Exp,
                                          scale=k.cp(7, 1)))
        k.wait("V", k.S, sv)
        vv = k.V.inc(nc.vector.tensor_scalar_add(nq1[0:1, :], nq1[0:1, :], 1e-6))
        k.wait("V", k.V, vv)
        vv = k.V.inc(nc.vector.reciprocal(nq1[0:1, :], nq1[0:1, :]))
        k.wait("T", k.V, vv)
        k.wait("T", k.T, k.T.val)
        tv = k.T.inc(nc.tensor.matmul(psB[:, :], ones1[0:1, 0:64], nq1[0:1, :],
                                      start=True, stop=True))
        k.wait("V", k.T, tv)
        vv = k.V.inc(nc.vector.tensor_mul(src64[:, sl], src64[:, sl], psB[:, :]))
        k.wait("T", k.V, vv)

    def rope_pipeline(src64, sl, rot, cosb, sinb, psR, tbf):
        tv = k.T.inc(nc.tensor.matmul(psR[0:64, :], rot[:, :], src64[:, sl],
                                      start=True, stop=True))
        k.wait("V", k.T, tv)
        vv = k.V.inc(nc.vector.tensor_mul(tbf[:, :], src64[:, sl], cosb[:, :]))
        k.wait("V", k.V, vv)
        vv = k.V.inc(nc.vector.tensor_mul(src64[:, sl], psR[0:64, :], sinb[:, :]))
        k.wait("V", k.V, vv)
        vv = k.V.inc(nc.vector.tensor_add(src64[:, sl], src64[:, sl], tbf[:, :]))
        k.wait("T", k.V, vv)

    with ExitStack() as ls:
        wkv = ls.enter_context(nc.sbuf_tensor("a_wkv", [128, 8 * 512], BF16))
        wbuf = [ls.enter_context(nc.sbuf_tensor(f"a_wb{i}", [128, 1024], BF16))
                for i in range(2)]
        K64 = ls.enter_context(nc.sbuf_tensor("a_k", [64, 4 * T], BF16))
        Vt = ls.enter_context(nc.sbuf_tensor("a_v", [128, ntile * 256], BF16))
        Q64 = ls.enter_context(nc.sbuf_tensor("a_q", [64, 16 * 512], BF16))
        st = ls.enter_context(nc.sbuf_tensor("a_st", [128, 8 * 512], BF16))
        SQ64 = ls.enter_context(nc.sbuf_tensor("a_sq", [64, 512], F32))
        nq1 = ls.enter_context(nc.sbuf_tensor("a_nq1", [1, 512], F32))
        rdn = ls.enter_context(nc.sbuf_tensor("a_rd", [1, 512], F32))
        cosb = ls.enter_context(nc.sbuf_tensor("a_cos", [64, 512], BF16))
        sinb = ls.enter_context(nc.sbuf_tensor("a_sin", [64, 512], BF16))
        rot = ls.enter_context(nc.sbuf_tensor("a_rot", [64, 64], BF16))
        tbf = ls.enter_context(nc.sbuf_tensor("a_tbf", [64, 512], BF16))
        ybf = ls.enter_context(nc.sbuf_tensor("a_ybf", [64, 512], BF16))

        for kc in range(8):
            k.dma("G", wkv[:, kc * 512:(kc + 1) * 512],
                  d["wkvT"][kc * 128:(kc + 1) * 128, :])
        k.dma("G", rot[:, :], d["rot64"])
        ldg = k.G.val
        k.wait("T", k.G, ldg)
        k.wait("T", k.V, k.V.val)
        k.wait("T", k.S, k.S.val)

        with ExitStack() as hs:
            psP = [hs.enter_context(nc.psum_tensor(f"atA{i}", [128, 512], F32))
                   for i in range(2)]
            ps1 = hs.enter_context(nc.psum_tensor("at1", [1, 512], F32))
            psD = hs.enter_context(nc.psum_tensor("atD", [1, 512], F32))
            psB = hs.enter_context(nc.psum_tensor("atB", [64, 512], F32))
            psPV = hs.enter_context(nc.psum_tensor("atPV", [64, 512], F32))

            for g in range(NKV):
                for t in range(ncol):
                    pb = (g * ncol + t) % 2
                    k.wait("T", k.S, k.S.val)
                    for kc in range(8):
                        ins = nc.tensor.matmul(
                            psP[pb][0:64, :],
                            wkv[:, kc * 512 + g * 64:kc * 512 + (g + 1) * 64],
                            Xb[:, kc * T + t * 512:kc * T + (t + 1) * 512],
                            start=(kc == 0), stop=(kc == 7))
                    tv = k.T.inc(ins)
                    k.wait("S", k.T, tv)
                    k.S.inc(nc.scalar.copy(
                        K64[:, g * T + t * 512:g * T + (t + 1) * 512],
                        psP[pb][0:64, :]))
            for tt in range(ntile):
                pb = tt % 2
                k.wait("T", k.V, k.V.val)
                k.wait("T", k.S, k.S.val)
                for kc in range(8):
                    ins = nc.tensor.matmul(
                        psP[pb][:, 0:256],
                        Xb[:, kc * T + tt * 128:kc * T + (tt + 1) * 128],
                        wkv[:, kc * 512 + 256:kc * 512 + 512],
                        start=(kc == 0), stop=(kc == 7))
                tv = k.T.inc(ins)
                k.wait("V", k.T, tv)
                k.V.inc(nc.vector.tensor_copy(Vt[:, tt * 256:(tt + 1) * 256],
                                              psP[pb][:, 0:256]))

            for t in range(ncol):
                k.wait("G", k.V, k.V.val)
                k.wait("G", k.T, k.T.val)
                k.dma("G", cosb[:, :], d["cos64"][:, t * 512:(t + 1) * 512])
                k.dma("G", sinb[:, :], d["sin64"][:, t * 512:(t + 1) * 512])
                k.wait("V", k.G, k.G.val)
                k.wait("T", k.G, k.G.val)
                for g in range(NKV):
                    sl = slice(g * T + t * 512, g * T + (t + 1) * 512)
                    norm_pipeline(K64, sl, ps1, nq1, psB, SQ64)
                    rope_pipeline(K64, sl, rot, cosb, sinb, psP[g % 2], tbf)

            for tq in range(ncol):
                ts = slice(tq * 512, (tq + 1) * 512)
                k.wait("G", k.V, k.V.val)
                k.wait("G", k.T, k.T.val)
                k.dma("G", cosb[:, :], d["cos64"][:, ts])
                k.dma("G", sinb[:, :], d["sin64"][:, ts])
                k.wait("V", k.G, k.G.val)
                k.wait("T", k.G, k.G.val)
                for oc in range(NC8):
                    b = oc % 2
                    k.wait("G", k.T, k.T.val)
                    k.dma("G", wbuf[b][:, :], d["wqT_t"][oc])
                    k.wait("T", k.G, k.G.val)
                    for par in range(2):
                        h = 2 * oc + par
                        pb = h % 2
                        k.wait("T", k.S, k.S.val)
                        for kc in range(8):
                            ins = nc.tensor.matmul(
                                psP[pb][0:64, :],
                                wbuf[b][:, kc * 128 + par * 64:
                                        kc * 128 + par * 64 + 64],
                                Xb[:, kc * T + tq * 512:kc * T + (tq + 1) * 512],
                                start=(kc == 0), stop=(kc == 7))
                        tv = k.T.inc(ins)
                        k.wait("S", k.T, tv)
                        k.S.inc(nc.scalar.copy(Q64[:, h * 512:(h + 1) * 512],
                                               psP[pb][0:64, :]))
                for h in range(NHEAD):
                    sl = slice(h * 512, (h + 1) * 512)
                    norm_pipeline(Q64, sl, ps1, nq1, psB, SQ64)
                    rope_pipeline(Q64, sl, rot, cosb, sinb, psP[h % 2], tbf)

                for h in range(NHEAD):
                    g = h // 4
                    hT = k.T.val
                    k.wait("S", k.T, hT)
                    k.wait("P", k.T, hT)
                    subs = [(j, tq * 512 - 512 + j * 128) for j in range(8)
                            if tq * 512 - 512 + j * 128 >= 0]
                    for j, tkb in subs:
                        pb = j % 2
                        k.wait("T", k.S, k.S.val)
                        ins = nc.tensor.matmul(
                            psP[pb][:, :],
                            K64[:, g * T + tkb:g * T + tkb + 128],
                            Q64[:, h * 512:(h + 1) * 512],
                            start=True, stop=True)
                        tv = k.T.inc(ins)
                        k.wait("S", k.T, tv)
                        ins = nc.scalar.activation(st[:, j * 512:(j + 1) * 512],
                                                   psP[pb][:, :], AF.Exp,
                                                   scale=k.cp(6))
                        sv = k.S.inc(ins)
                        k.wait("P", k.S, sv)
                        if j < 4:
                            ins = nc.gpsimd.affine_select(
                                st[:, j * 512:(j + 1) * 512],
                                st[:, j * 512:(j + 1) * 512],
                                pattern=[[-1, 512]], compare_op=ALU.is_ge,
                                fill=0.0, base=511 - tq * 512 + tkb,
                                channel_multiplier=1)
                        else:
                            ins = nc.gpsimd.affine_select(
                                st[:, j * 512:(j + 1) * 512],
                                st[:, j * 512:(j + 1) * 512],
                                pattern=[[1, 512]], compare_op=ALU.is_ge,
                                fill=0.0, base=tq * 512 - tkb,
                                channel_multiplier=-1)
                        k.P.inc(ins)
                    mv = k.P.val
                    k.wait("T", k.P, mv)
                    for i, (j, tkb) in enumerate(subs):
                        ins = nc.tensor.matmul(psD[0:1, :], k.ones128b[:, 0:1],
                                               st[:, j * 512:(j + 1) * 512],
                                               start=(i == 0),
                                               stop=(i == len(subs) - 1))
                    tv = k.T.inc(ins)
                    k.wait("V", k.T, tv)
                    vv = k.V.inc(nc.vector.reciprocal(rdn[0:1, :], psD[0:1, :]))
                    k.wait("T", k.V, vv)
                    bv = k.T.inc(nc.tensor.matmul(psB[:, :], ones1[0:1, 0:64],
                                                  rdn[0:1, :], start=True,
                                                  stop=True))
                    for i, (j, tkb) in enumerate(subs):
                        tt = tkb // 128
                        ins = nc.tensor.matmul(
                            psPV[:, :],
                            Vt[:, tt * 256 + g * 64:tt * 256 + (g + 1) * 64],
                            st[:, j * 512:(j + 1) * 512], start=(i == 0),
                            stop=(i == len(subs) - 1))
                    tv = k.T.inc(ins)
                    k.wait("S", k.T, tv)
                    sv = k.S.inc(nc.scalar.copy(ybf[:, :], psPV[:, :]))
                    k.wait("V", k.S, sv)
                    k.wait("V", k.T, bv)
                    vv = k.V.inc(nc.vector.tensor_mul(ybf[:, :], ybf[:, :],
                                                      psB[:, :]))
                    k.wait("G", k.V, vv)
                    k.dma("G", d["yf"][h * 64:(h + 1) * 64, ts], ybf[:, :])
                    k.wait("T", k.G, k.G.val)
                    k.wait("S", k.G, k.G.val)
                    k.wait("V", k.G, k.G.val)
        k.barrier()

    with ExitStack() as os_:
        ybin = [os_.enter_context(nc.sbuf_tensor(f"a_yb{i}", [64, 512], BF16))
                for i in range(2)]
        wp8 = os_.enter_context(nc.sbuf_tensor("a_wp8", [64, 8 * 2048], BF16))
        psO = os_.enter_context(nc.psum_tensor("a_psO", [128, 4096], F32))
        for oc in range(NC8):
            k.dma("G", wp8[:, oc * 2048:(oc + 1) * 2048], d["wp64"][oc])
        k.wait("T", k.G, k.G.val)
        for t in range(ncol):
            for h in range(NHEAD):
                b = h % 2
                k.wait("G", k.T, k.T.val)
                k.dma("G", ybin[b][:, :],
                      d["yf"][h * 64:(h + 1) * 64, t * 512:(t + 1) * 512])
                gv = k.G.val
                k.wait("T", k.G, gv)
                if h == 0:
                    k.wait("T", k.V, k.V.val)
                for oc in range(NC8):
                    ins = nc.tensor.matmul(
                        psO[:, oc * 512:(oc + 1) * 512],
                        wp8[:, oc * 2048 + h * 128:oc * 2048 + (h + 1) * 128],
                        ybin[b][:, :],
                        start=(h == 0), stop=(h == NHEAD - 1))
                k.T.inc(ins)
            tv = k.T.val
            k.wait("V", k.T, tv)
            for oc in range(NC8):
                sl = slice(oc * T + t * 512, oc * T + (t + 1) * 512)
                k.V.inc(nc.vector.tensor_add(X[:, sl], X[:, sl],
                                             psO[:, oc * 512:(oc + 1) * 512]))
    k.barrier()


def emit_l1_ffn_w12(k, Xb, H, d, wbig, psP, T, tok0, Th):
    nc = k.nc
    s1 = k.ffn_scales["s1"]
    s2 = k.ffn_scales["s2"]

    def evac_w1(ob, t, ps_, tv):
        k.wait("S", k.T, tv)
        ins = nc.scalar.activation(H[:, ob * Th + t * 512:ob * Th + (t + 1) * 512],
                                   ps_[:, :], AF.Silu, scale=s1[:, ob:ob + 1])
        return (k.S, k.S.inc(ins))
    emit_mm_phase(k, d["w1T"], 32, D, Xb, T, wbig, psP, evac_w1, tok0=tok0,
                  n_tcol=Th // 512)

    def evac_w2(ob, t, ps_, tv):
        sl = slice(ob * Th + t * 512, ob * Th + (t + 1) * 512)
        k.wait("V", k.T, tv)
        k.wait("V", k.S, k.S.val)
        k.wait("V", k.V, k.V.val)
        ins = nc.vector.tensor_scalar_mul(k.ffn_sq[:, 0:512], ps_[:, :],
                                          s2[:, ob:ob + 1])
        vv = k.V.inc(ins)
        k.wait("V", k.V, vv)
        ins = nc.vector.tensor_mul(H[:, sl], H[:, sl], k.ffn_sq[:, 0:512])
        return (k.V, k.V.inc(ins))
    emit_mm_phase(k, d["w2T"], 32, D, Xb, T, wbig, psP, evac_w2, tok0=tok0,
                  n_tcol=Th // 512)


def emit_l1_ffn_h(k, Hin, Hout, d_w, scale_col, wbig, psP, Th):
    nc = k.nc

    def evac(ob, t, ps_, tv):
        k.wait("S", k.T, tv)
        k.wait("S", k.V, k.V.val)
        ins = nc.scalar.activation(Hout[:, ob * Th + t * 512:ob * Th + (t + 1) * 512],
                                   ps_[:, :], AF.Gelu, scale=scale_col[:, ob:ob + 1])
        return (k.S, k.S.inc(ins))
    emit_mm_phase(k, d_w, 32, DH, Hin, Th, wbig, psP, evac)


def emit_l1_ffn_w3(k, Xn, H, d, wbig, psP, T, tok0, Th):
    nc = k.nc
    s3 = k.ffn_scales["s3"]

    def evac(ob, t, ps_, tv):
        sl = slice(ob * T + tok0 + t * 512, ob * T + tok0 + (t + 1) * 512)
        k.wait("V", k.T, tv)
        k.wait("V", k.S, k.S.val)
        k.wait("V", k.V, k.V.val)
        k.wait("V", k.Y, k.Y.val)
        ins = nc.vector.scalar_tensor_tensor(Xn[:, sl], ps_[:, :],
                                             s3[:, ob:ob + 1], Xn[:, sl],
                                             ALU.mult, ALU.add)
        return (k.V, k.V.inc(ins))
    emit_mm_phase(k, d["w3T"], NC8, DH, H, Th, wbig, psP, evac)


def declare_common(nc):
    d = {}
    d["scratch1"] = nc.dram_tensor("scratch1", [32, 8 * 2048], F32,
                                   kind="Internal").ap()
    return d


def build_launchA(T):
    nc = bass.Bass("TRN2", target_bir_lowering=False, debug=False)
    es = ExitStack()
    d = declare_common(nc)
    X_ = nc.dram_tensor
    d["x_in"] = X_("x_in", [D, T], F32, kind="ExternalInput").ap()
    d["w_in"] = X_("w_in", [D, 1072], F32, kind="ExternalInput").ap()
    d["bz"] = X_("bz", [128, 8], F32, kind="ExternalInput").ap()
    d["bbcd"] = X_("bbcd", [16, 3], F32, kind="ExternalInput").ap()
    d["alog"] = X_("alog", [16, 1], F32, kind="ExternalInput").ap()
    d["w_out"] = X_("w_out", [D, D], F32, kind="ExternalInput").ap()
    d["bout"] = X_("bout", [128, 8], F32, kind="ExternalInput").ap()
    d["rwT"] = X_("rwT", [D, 8], F32, kind="ExternalInput").ap()
    d["cpool"] = X_("cpool", [128, 8], F32, kind="ExternalInput").ap()
    d["x2_out"] = X_("x2_out", [D, T], F32, kind="ExternalOutput").ap()
    d["route_out"] = X_("route_out", [24, T], F32, kind="ExternalOutput").ap()
    d["counts_out"] = X_("counts_out", [8, 1], F32, kind="ExternalOutput").ap()
    k = Kb(nc, es)
    X = es.enter_context(nc.sbuf_tensor("X", [128, 8 * T], F32))
    Xb = es.enter_context(nc.sbuf_tensor("Xb", [128, 8 * T], BF16))
    ones128, ones1, idf = emit_consts(k, d)
    for c in range(NC8):
        k.dma("Y", X[:, c * T:(c + 1) * T],
                                  d["x_in"][c * 128:(c + 1) * 128, :])
    k.wait("V", k.Y, k.Y.val)
    for c in range(NC8):
        k.V.inc(nc.vector.tensor_copy(Xb[:, c * T:(c + 1) * T],
                                      X[:, c * T:(c + 1) * T]))
    emit_l0_attn(k, X, Xb, d, ones128, ones1, T)
    emit_ln(k, "0", X, Xb, ones128, ones1, T)
    route_sb = {
        "selE": es.enter_context(nc.sbuf_tensor("selE", [8, T], F32)),
        "wdE": es.enter_context(nc.sbuf_tensor("wdE", [8, T], F32)),
        "csum": es.enter_context(nc.sbuf_tensor("csum", [8, T], F32)),
        "counts": es.enter_context(nc.sbuf_tensor("counts", [8, 1], F32)),
    }
    emit_routing(k, X, Xb, d, ones128, ones1, idf, T, route_sb)
    for c in range(NC8):
        k.dma("Y", d["x2_out"][c * 128:(c + 1) * 128, :],
                                  X[:, c * T:(c + 1) * T])
    k.dma("Y", d["route_out"][0:8, :],
                              route_sb["selE"][:, :])
    k.dma("Y", d["route_out"][8:16, :],
                              route_sb["wdE"][:, :])
    k.dma("Y", d["route_out"][16:24, :],
                              route_sb["csum"][:, :])
    k.dma("Y", d["counts_out"][:, :],
                              route_sb["counts"][:, :])
    k.wait("Y", k.Y, k.Y.val)
    es.close()
    return nc


def build_launchB(T):
    nc = bass.Bass("TRN2", target_bir_lowering=False, debug=False)
    es = ExitStack()
    d = declare_common(nc)
    X_ = nc.dram_tensor
    d["x2_in"] = X_("x2_in", [D, T], F32, kind="ExternalInput").ap()
    d["route_in"] = X_("route_in", [24, T], F32, kind="ExternalInput").ap()
    d["counts_all"] = X_("counts_all", [8, 8], F32, kind="ExternalInput").ap()
    d["prevmask"] = X_("prevmask", [8, 1], F32, kind="ExternalInput").ap()
    d["rg"] = [X_(f"rg{e}", [IR // 128, 128, D], F32, kind="ExternalInput").ap()
               for e in range(NEXP)]
    d["ru"] = [X_(f"ru{e}", [IR // 128, 128, D], F32, kind="ExternalInput").ap()
               for e in range(NEXP)]
    d["sg"] = X_("sg", [ISH // 128, 128, D], F32, kind="ExternalInput").ap()
    d["su"] = X_("su", [ISH // 128, 128, D], F32, kind="ExternalInput").ap()
    d["sd_h"] = [X_(f"sd{h}", [NC8, 128, IR], F32, kind="ExternalInput").ap()
                 for h in range(2)]
    d["rdk"] = [X_(f"rdk{e}", [32, 128, 1024], F32, kind="ExternalInput").ap()
                for e in range(NEXP)]
    d["x2tm"] = X_("x2tm", [T, D], BF16, kind="Internal").ap()
    d["routedtm"] = X_("routedtm", [T, D], F32, kind="Internal").ap()
    d["paird"] = X_("paird", [NEXP * 1024, 2], F32, kind="Internal").ap()
    d["wqT_t"] = X_("wqT_t", [NC8, 128, D], F32, kind="ExternalInput").ap()
    d["wkvT"] = X_("wkvT", [D, 512], F32, kind="ExternalInput").ap()
    d["wp64"] = X_("wp64", [NC8, 64, 16 * 128], F32, kind="ExternalInput").ap()
    d["cos64"] = X_("cos64", [64, T], F32, kind="ExternalInput").ap()
    d["sin64"] = X_("sin64", [64, T], F32, kind="ExternalInput").ap()
    d["rot64"] = X_("rot64", [64, 64], F32, kind="ExternalInput").ap()
    d["yf"] = X_("yf", [D, T], BF16, kind="Internal").ap()
    d["w1T"] = X_("w1T", [32, 128, D], F32, kind="ExternalInput").ap()
    d["w2T"] = X_("w2T", [32, 128, D], F32, kind="ExternalInput").ap()
    d["h1T"] = X_("h1T", [32, 128, DH], F32, kind="ExternalInput").ap()
    d["h2T"] = X_("h2T", [32, 128, DH], F32, kind="ExternalInput").ap()
    d["w3T"] = X_("w3T", [8, 128, DH], F32, kind="ExternalInput").ap()
    d["s1"] = X_("s1", [128, 32], F32, kind="ExternalInput").ap()
    d["s2"] = X_("s2", [128, 32], F32, kind="ExternalInput").ap()
    d["h1s"] = X_("h1s", [128, 32], F32, kind="ExternalInput").ap()
    d["h2s"] = X_("h2s", [128, 32], F32, kind="ExternalInput").ap()
    d["s3"] = X_("s3", [128, 8], F32, kind="ExternalInput").ap()
    d["cpool"] = X_("cpool", [128, 8], F32, kind="ExternalInput").ap()
    d["xpark"] = X_("xpark", [D, T], F32, kind="Internal").ap()
    d["out"] = X_("out", [D, T], F32, kind="ExternalOutput").ap()

    k = Kb(nc, es)
    Xb = es.enter_context(nc.sbuf_tensor("Xb", [128, 8 * T], BF16))
    ones128, ones1, idf = emit_consts(k, d)
    k.ffn_scales = {}
    for name in ("s1", "s2", "h1s", "h2s", "s3"):
        t_ = es.enter_context(nc.sbuf_tensor(f"sc_{name}",
                                             [128, 8 if name == "s3" else 32],
                                             F32))
        k.dma("Y", t_[:, :], d[name])
        k.ffn_scales[name] = t_

    with ExitStack() as sx:
        X = sx.enter_context(nc.sbuf_tensor("X", [128, 8 * T], F32))
        for c in range(NC8):
            k.dma("Y", X[:, c * T:(c + 1) * T],
                                      d["x2_in"][c * 128:(c + 1) * 128, :])
        with ExitStack() as sw:
            weff_sb = sw.enter_context(nc.sbuf_tensor("weff", [8, T], F32))
            with ExitStack() as sr:
                route_sb = {
                    "selE": sr.enter_context(nc.sbuf_tensor("selE", [8, T], F32)),
                    "wdE": sr.enter_context(nc.sbuf_tensor("wdE", [8, T], F32)),
                    "csum": sr.enter_context(nc.sbuf_tensor("csum", [8, T], F32)),
                    "weff": weff_sb,
                }
                for i, key in enumerate(("selE", "wdE", "csum")):
                    k.dma("Y", 
                        route_sb[key][:, :],
                        d["route_in"][i * 8:(i + 1) * 8, :])
                k.wait("V", k.Y, k.Y.val)
                for c in range(NC8):
                    k.V.inc(nc.vector.tensor_copy(Xb[:, c * T:(c + 1) * T],
                                                  X[:, c * T:(c + 1) * T]))
                emit_weff(k, d, route_sb, T)
            with ExitStack() as sm:
                H = sm.enter_context(nc.sbuf_tensor("Hh", [128, 16 * T], BF16))
                emit_moe(k, X, Xb, H, d, {"weff": weff_sb}, ones128, ones1, T)
        emit_ln(k, "1", X, Xb, ones128, ones1, T)
        emit_l1_attn(k, X, Xb, d, ones128, ones1, T)
        emit_ln(k, "2", X, Xb, ones128, ones1, T)
        k.barrier()
        for c in range(NC8):
            k.dma("Y", d["xpark"][c * 128:(c + 1) * 128, :],
                                      X[:, c * T:(c + 1) * T])
        k.wait("Y", k.Y, k.Y.val)
        k.barrier()
    with ExitStack() as sf:
        Th = T // 2
        wbig = [sf.enter_context(nc.sbuf_tensor(f"f_wb{i}", [128, DH], BF16))
                for i in range(2)]
        k.ffn_sq = sf.enter_context(nc.sbuf_tensor("f_sq", [128, 512], F32))
        for halft in range(2):
            tok0 = halft * Th
            with ExitStack() as sh_:
                H = sh_.enter_context(nc.sbuf_tensor(f"Hf{halft}",
                                                     [128, 32 * Th], BF16))
                with ExitStack() as s3_:
                    H2 = s3_.enter_context(nc.sbuf_tensor(f"Hf2{halft}",
                                                          [128, 32 * Th], BF16))
                    with ExitStack() as ps:
                        psP = [ps.enter_context(
                            nc.psum_tensor(f"fA{halft}{i}", [128, 512], F32))
                            for i in range(2)]
                        emit_l1_ffn_w12(k, Xb, H, d, wbig, psP, T, tok0, Th)
                        k.barrier()
                        emit_l1_ffn_h(k, H, H2, d["h1T"], k.ffn_scales["h1s"],
                                      wbig, psP, Th)
                        k.barrier()
                        emit_l1_ffn_h(k, H2, H, d["h2T"], k.ffn_scales["h2s"],
                                      wbig, psP, Th)
                        k.barrier()
                Xnh = sh_.enter_context(nc.sbuf_tensor(f"Xnh{halft}",
                                                       [128, 8 * Th], F32))
                for c in range(NC8):
                    k.dma("Y", Xnh[:, c * Th:(c + 1) * Th],
                          d["xpark"][c * 128:(c + 1) * 128, tok0:tok0 + Th])
                k.wait("V", k.Y, k.Y.val)
                with ExitStack() as ps:
                    psP = [ps.enter_context(
                        nc.psum_tensor(f"fC{halft}{i}", [128, 512], F32))
                        for i in range(2)]
                    emit_l1_ffn_w3(k, Xnh, H, d, wbig, psP, Th, 0, Th)
                    k.barrier()
                for c in range(NC8):
                    k.dma("Y", d["xpark"][c * 128:(c + 1) * 128, tok0:tok0 + Th],
                          Xnh[:, c * Th:(c + 1) * Th])
                k.wait("Y", k.Y, k.Y.val)
                k.barrier()
        Xn = sf.enter_context(nc.sbuf_tensor("Xn", [128, 8 * T], F32))
        for c in range(NC8):
            k.dma("Y", Xn[:, c * T:(c + 1) * T],
                  d["xpark"][c * 128:(c + 1) * 128, :])
        k.wait("V", k.Y, k.Y.val)
        k.barrier()
        emit_ln(k, "3", Xn, Xb, ones128, ones1, T, out_bf=False)
        for c in range(NC8):
            k.dma("Y", d["out"][c * 128:(c + 1) * 128, :],
                  Xn[:, c * T:(c + 1) * T])
        k.wait("Y", k.Y, k.Y.val)
    es.close()
    return nc


def build_fused(T):
    nc = bass.Bass("TRN2", target_bir_lowering=False, debug=False)
    es = ExitStack()
    ntile = T // 128
    d = declare_common(nc)
    X_ = nc.dram_tensor
    d["x_in"] = X_("x_in", [D, T], F32, kind="ExternalInput").ap()
    d["x_all"] = X_("x_all", [8 * D, T], F32, kind="ExternalInput").ap()
    d["w_in"] = X_("w_in", [D, 1072], F32, kind="ExternalInput").ap()
    d["bz"] = X_("bz", [128, 8], F32, kind="ExternalInput").ap()
    d["bbcd"] = X_("bbcd", [16, 3], F32, kind="ExternalInput").ap()
    d["alog"] = X_("alog", [16, 1], F32, kind="ExternalInput").ap()
    d["w_out"] = X_("w_out", [D, D], F32, kind="ExternalInput").ap()
    d["wo_t"] = X_("wo_t", [NC8, 128, D], F32, kind="ExternalInput").ap()
    d["bout"] = X_("bout", [128, 8], F32, kind="ExternalInput").ap()
    d["rwT"] = X_("rwT", [D, 8], F32, kind="ExternalInput").ap()
    d["im8"] = X_("im8", [8, 8], F32, kind="ExternalInput").ap()
    d["im128"] = X_("im128", [128, 8], F32, kind="ExternalInput").ap()
    d["prevmask"] = X_("prevmask", [8, 1], F32, kind="ExternalInput").ap()
    d["cpool"] = X_("cpool", [128, 8], F32, kind="ExternalInput").ap()
    d["rg"] = [X_(f"rg{e}", [IR // 128, 128, D], F32, kind="ExternalInput").ap()
               for e in range(NEXP)]
    d["ru"] = [X_(f"ru{e}", [IR // 128, 128, D], F32, kind="ExternalInput").ap()
               for e in range(NEXP)]
    d["sg"] = X_("sg", [ISH // 128, 128, D], F32, kind="ExternalInput").ap()
    d["su"] = X_("su", [ISH // 128, 128, D], F32, kind="ExternalInput").ap()
    d["sd_h"] = [X_(f"sd{h}", [NC8, 128, IR], F32, kind="ExternalInput").ap()
                 for h in range(2)]
    d["rdk"] = [X_(f"rdk{e}", [32, 128, 1024], F32, kind="ExternalInput").ap()
                for e in range(NEXP)]
    d["x2tm"] = X_("x2tm", [T, D], BF16, kind="Internal").ap()
    d["routedtm"] = X_("routedtm", [T, D], F32, kind="Internal").ap()
    d["paird"] = X_("paird", [NEXP * 1024, 2], F32, kind="Internal").ap()
    d["wqT_t"] = X_("wqT_t", [NC8, 128, D], F32, kind="ExternalInput").ap()
    d["wkvT"] = X_("wkvT", [D, 512], F32, kind="ExternalInput").ap()
    d["wp64"] = X_("wp64", [NC8, 64, 16 * 128], F32, kind="ExternalInput").ap()
    d["cos64"] = X_("cos64", [64, T], F32, kind="ExternalInput").ap()
    d["sin64"] = X_("sin64", [64, T], F32, kind="ExternalInput").ap()
    d["rot64"] = X_("rot64", [64, 64], F32, kind="ExternalInput").ap()
    d["yf"] = X_("yf", [D, T], BF16, kind="Internal").ap()
    d["w1T"] = X_("w1T", [32, 128, D], F32, kind="ExternalInput").ap()
    d["w2T"] = X_("w2T", [32, 128, D], F32, kind="ExternalInput").ap()
    d["h1T"] = X_("h1T", [32, 128, DH], F32, kind="ExternalInput").ap()
    d["h2T"] = X_("h2T", [32, 128, DH], F32, kind="ExternalInput").ap()
    d["w3T"] = X_("w3T", [8, 128, DH], F32, kind="ExternalInput").ap()
    d["s1"] = X_("s1", [128, 32], F32, kind="ExternalInput").ap()
    d["s2"] = X_("s2", [128, 32], F32, kind="ExternalInput").ap()
    d["h1s"] = X_("h1s", [128, 32], F32, kind="ExternalInput").ap()
    d["h2s"] = X_("h2s", [128, 32], F32, kind="ExternalInput").ap()
    d["s3"] = X_("s3", [128, 8], F32, kind="ExternalInput").ap()
    d["xpark"] = X_("xpark", [D, T], F32, kind="Internal").ap()
    d["out"] = X_("out", [D, T], F32, kind="ExternalOutput").ap()

    k = Kb(nc, es)
    ones128, ones1, idf = emit_consts(k, d)
    k.ffn_scales = {}
    for name in ("s1", "s2", "h1s", "h2s", "s3"):
        t_ = es.enter_context(nc.sbuf_tensor(f"sc_{name}",
                                             [128, 8 if name == "s3" else 32],
                                             F32))
        k.dma("Y", t_[:, :], d[name])
        k.ffn_scales[name] = t_
    sart = ExitStack()
    art = {
        "selE": sart.enter_context(nc.sbuf_tensor("art_se", [8, T], F32,
                                                  side="right")),
        "msk1": sart.enter_context(nc.sbuf_tensor("art_k1", [128, 8 * ntile],
                                                  F32, side="right")),
        "msk2": sart.enter_context(nc.sbuf_tensor("art_k2", [128, 8 * ntile],
                                                  F32, side="right")),
        "dp": sart.enter_context(nc.sbuf_tensor("art_dp", [128, ntile], F32,
                                                side="right")),
        "off": sart.enter_context(nc.sbuf_tensor("art_off", [8, 1], F32,
                                                 side="right")),
    }
    rstd = sart.enter_context(nc.sbuf_tensor("rstd0", [1, T], F32,
                                             side="right"))
    rankI = es.enter_context(nc.sbuf_tensor("rankI", [128, 8 * (T // 128)],
                                            I32))
    weffT = es.enter_context(nc.sbuf_tensor("weffT", [128, 8 * (T // 128)],
                                            F32))

    emit_phase0(k, d, ones128, ones1, idf, T, art)

    Xb = es.enter_context(nc.sbuf_tensor("Xb", [128, 8 * T], BF16))
    with ExitStack() as sx:
        X = sx.enter_context(nc.sbuf_tensor("X", [128, 8 * T], F32))
        for c in range(NC8):
            k.dma("Y", X[:, c * T:(c + 1) * T],
                  d["x_in"][c * 128:(c + 1) * 128, :])
        k.wait("V", k.Y, k.Y.val)
        for c in range(NC8):
            k.V.inc(nc.vector.tensor_copy(Xb[:, c * T:(c + 1) * T],
                                          X[:, c * T:(c + 1) * T]))
        emit_l0_attn(k, X, Xb, d, ones128, ones1, T)
        emit_ln(k, "0", X, Xb, ones128, ones1, T, rstd_out=rstd)
        with ExitStack() as sw:
            weff_sb = sw.enter_context(nc.sbuf_tensor("weff", [8, T], F32))
            emit_routing2(k, X, Xb, d, ones128, ones1, idf, T, art, rstd,
                          weff_sb, rankI=rankI, weffT=weffT)
            sart.close()
        k.barrier()
        for c in range(NC8):
            k.dma("Y", d["xpark"][c * 128:(c + 1) * 128, :],
                  X[:, c * T:(c + 1) * T])
        k.wait("Y", k.Y, k.Y.val)
        k.barrier()
    emit_moe_routed(k, Xb, d, ones128, ones1, idf, rankI, weffT, T)
    with ExitStack() as sx:
        X = sx.enter_context(nc.sbuf_tensor("X2", [128, 8 * T], F32))
        for c in range(NC8):
            k.dma("Y", X[:, c * T:(c + 1) * T],
                  d["xpark"][c * 128:(c + 1) * 128, :])
        k.wait("V", k.Y, k.Y.val)
        k.barrier()
        with ExitStack() as sm:
            H = sm.enter_context(nc.sbuf_tensor("Hh", [128, 16 * T], BF16))
            emit_moe_shared(k, X, Xb, H, d, ones128, ones1, idf, T)
        emit_ln(k, "1", X, Xb, ones128, ones1, T)
        emit_l1_attn(k, X, Xb, d, ones128, ones1, T)
        emit_ln(k, "2", X, Xb, ones128, ones1, T)
        k.barrier()
        for c in range(NC8):
            k.dma("Y", d["xpark"][c * 128:(c + 1) * 128, :],
                  X[:, c * T:(c + 1) * T])
        k.wait("Y", k.Y, k.Y.val)
        k.barrier()
    with ExitStack() as sf:
        Th = T // 2
        wbig = [sf.enter_context(nc.sbuf_tensor(f"f_wb{i}", [128, DH], BF16))
                for i in range(2)]
        k.ffn_sq = sf.enter_context(nc.sbuf_tensor("f_sq", [128, 512], F32))
        for halft in range(2):
            tok0 = halft * Th
            with ExitStack() as sh_:
                H = sh_.enter_context(nc.sbuf_tensor(f"Hf{halft}",
                                                     [128, 32 * Th], BF16))
                with ExitStack() as s3_:
                    H2 = s3_.enter_context(nc.sbuf_tensor(f"Hf2{halft}",
                                                          [128, 32 * Th], BF16))
                    with ExitStack() as ps:
                        psP = [ps.enter_context(
                            nc.psum_tensor(f"fA{halft}{i}", [128, 512], F32))
                            for i in range(2)]
                        emit_l1_ffn_w12(k, Xb, H, d, wbig, psP, T, tok0, Th)
                        k.barrier()
                        emit_l1_ffn_h(k, H, H2, d["h1T"], k.ffn_scales["h1s"],
                                      wbig, psP, Th)
                        k.barrier()
                        emit_l1_ffn_h(k, H2, H, d["h2T"], k.ffn_scales["h2s"],
                                      wbig, psP, Th)
                        k.barrier()
                Xnh = sh_.enter_context(nc.sbuf_tensor(f"Xnh{halft}",
                                                       [128, 8 * Th], F32))
                for c in range(NC8):
                    k.dma("Y", Xnh[:, c * Th:(c + 1) * Th],
                          d["xpark"][c * 128:(c + 1) * 128, tok0:tok0 + Th])
                k.wait("V", k.Y, k.Y.val)
                with ExitStack() as ps:
                    psP = [ps.enter_context(
                        nc.psum_tensor(f"fC{halft}{i}", [128, 512], F32))
                        for i in range(2)]
                    emit_l1_ffn_w3(k, Xnh, H, d, wbig, psP, Th, 0, Th)
                    k.barrier()
                for c in range(NC8):
                    k.dma("Y", d["xpark"][c * 128:(c + 1) * 128, tok0:tok0 + Th],
                          Xnh[:, c * Th:(c + 1) * Th])
                k.wait("Y", k.Y, k.Y.val)
                k.barrier()
        Xn = sf.enter_context(nc.sbuf_tensor("Xn", [128, 8 * T], F32))
        for c in range(NC8):
            k.dma("Y", Xn[:, c * T:(c + 1) * T],
                  d["xpark"][c * 128:(c + 1) * 128, :])
        k.wait("V", k.Y, k.Y.val)
        k.barrier()
        emit_ln(k, "3", Xn, Xb, ones128, ones1, T, out_bf=False)
        for c in range(NC8):
            k.dma("Y", d["out"][c * 128:(c + 1) * 128, :],
                  Xn[:, c * T:(c + 1) * T])
        k.wait("Y", k.Y, k.Y.val)
    es.close()
    return nc


def _tile_oc(wKN, K, N):
    a = np.ascontiguousarray(wKN, np.float32).reshape(K // 128, 128, N // 128, 128)
    return np.ascontiguousarray(a.transpose(2, 1, 0, 3).reshape(N // 128, 128, K))


def _rs(v, n):
    return np.ascontiguousarray(np.asarray(v, np.float32).reshape(n, 128).T)


def _prep(inp):
    T = TFULL
    f = np.float32
    p = {}
    x = np.asarray(inp["L0_attn_x"], f)
    p["xs"] = [np.ascontiguousarray(x[b].T) for b in range(8)]
    w_in = np.asarray(inp["L0_attn_W_in"], f)
    p["w_in"] = np.ascontiguousarray(w_in[:, 1024:])
    b_in = np.asarray(inp["L0_attn_b_in"], f)
    p["bz"] = _rs(b_in[1024:2048], 8)
    p["bbcd"] = np.ascontiguousarray(
        np.stack([b_in[2048:2064], b_in[2064:2080], b_in[2080:2096]], axis=1))
    p["alog"] = np.asarray(inp["L0_attn_A_log"], f).reshape(16, 1).copy()
    p["w_out"] = np.ascontiguousarray(np.asarray(inp["L0_attn_W_out"], f))
    p["wo_t"] = _tile_oc(p["w_out"], D, D)
    p["bout"] = _rs(np.asarray(inp["L0_attn_b_out"], f), 8)
    p["rwT"] = np.ascontiguousarray(np.asarray(inp["L0_ffn_router_w"], f).T)
    rg = np.asarray(inp["L0_ffn_rg"], f)
    ru = np.asarray(inp["L0_ffn_ru"], f)
    rd = np.asarray(inp["L0_ffn_rd"], f)
    p["rg"] = [_tile_oc(rg[e], D, IR) for e in range(NEXP)]
    p["ru"] = [_tile_oc(ru[e], D, IR) for e in range(NEXP)]
    p["rd"] = [_tile_oc(rd[e], IR, D) for e in range(NEXP)]
    p["rdk"] = [np.ascontiguousarray(rd[e].reshape(32, 128, 1024))
                for e in range(NEXP)]
    p["sg"] = _tile_oc(np.asarray(inp["L0_ffn_sg"], f), D, ISH)
    p["su"] = _tile_oc(np.asarray(inp["L0_ffn_su"], f), D, ISH)
    sd = np.asarray(inp["L0_ffn_sd"], f)
    p["sd_h"] = [_tile_oc(sd[h * IR:(h + 1) * IR, :], IR, D) for h in range(2)]
    p["wqT_t"] = _tile_oc(np.asarray(inp["L1_attn_Wq"], f).T.copy(), D, D)
    p["wkvT"] = np.ascontiguousarray(np.asarray(inp["L1_attn_Wkv"], f).T)
    wpt = np.asarray(inp["L1_attn_Wproj"], f).T.copy()
    p["wp64"] = np.ascontiguousarray(
        wpt.reshape(16, 64, 8, 128).transpose(2, 1, 0, 3).reshape(8, 64, 2048))
    p["w1T"] = _tile_oc(np.asarray(inp["L1_ffn_w1_w"], f).T.copy(), D, DH)
    p["w2T"] = _tile_oc(np.asarray(inp["L1_ffn_w2_w"], f).T.copy(), D, DH)
    p["h1T"] = _tile_oc(np.asarray(inp["L1_ffn_h1_w"], f).T.copy(), DH, DH)
    p["h2T"] = _tile_oc(np.asarray(inp["L1_ffn_h2_w"], f).T.copy(), DH, DH)
    p["w3T"] = _tile_oc(np.asarray(inp["L1_ffn_w3_w"], f).T.copy(), DH, D)
    p["s1"] = _rs(inp["L1_ffn_w1_s"], 32)
    p["s2"] = _rs(inp["L1_ffn_w2_s"], 32)
    p["h1s"] = _rs(inp["L1_ffn_h1_s"], 32)
    p["h2s"] = _rs(inp["L1_ffn_h2_s"], 32)
    p["s3"] = _rs(inp["L1_ffn_w3_s"], 8)
    theta = 1.0 / (10000.0 ** (np.arange(0, HDIM, 2, dtype=np.float64) / HDIM))
    pos = np.arange(T, dtype=np.float64)
    fr = np.outer(pos, theta)
    cos = np.concatenate([np.cos(fr), np.cos(fr)], axis=1).astype(f)
    sin = np.concatenate([np.sin(fr), np.sin(fr)], axis=1).astype(f)
    p["cos64"] = np.ascontiguousarray(cos.T)
    p["sin64"] = np.ascontiguousarray(sin.T)
    R = np.zeros((HDIM, HDIM), f)
    for i in range(32):
        R[i, 32 + i] = -1.0
        R[32 + i, i] = 1.0
    p["rot64"] = np.ascontiguousarray(R.T)
    cp = np.zeros((128, 8), f)
    cp[:, 0] = 1e-5
    cp[:, 1] = -0.5
    cp[:, 2] = F32EPS
    cp[:, 3] = 1.0 / D
    cp[:, 4] = -1.0
    cp[:, 5] = 1.0
    cp[:, 6] = 0.125
    cp[:, 7] = 0.5
    p["cpool"] = cp
    p["prevmask"] = [np.ascontiguousarray(
        (np.arange(8) < c).astype(f).reshape(8, 1)) for c in range(8)]
    p["x_all"] = np.ascontiguousarray(np.concatenate(p["xs"], axis=0))
    im8 = []
    im128 = []
    for c in range(8):
        a = np.zeros((8, 8), f)
        a[:, c] = 1.0
        im8.append(a)
        b = np.zeros((128, 8), f)
        b[:, c] = 1.0
        im128.append(b)
    p["im8"] = im8
    p["im128"] = im128
    return p


def kernel(**inputs):
    p = _prep(inputs)
    T = TFULL
    nc = build_fused(T)
    in_maps = []
    for c in range(8):
        m = {"x_in": p["xs"][c], "x_all": p["x_all"], "w_in": p["w_in"],
             "bz": p["bz"], "bbcd": p["bbcd"], "alog": p["alog"],
             "w_out": p["w_out"], "wo_t": p["wo_t"], "bout": p["bout"],
             "rwT": p["rwT"],
             "im8": p["im8"][c], "im128": p["im128"][c],
             "prevmask": p["prevmask"][c], "cpool": p["cpool"],
             "sg": p["sg"], "su": p["su"], "wqT_t": p["wqT_t"],
             "wkvT": p["wkvT"], "wp64": p["wp64"], "cos64": p["cos64"],
             "sin64": p["sin64"], "rot64": p["rot64"],
             "w1T": p["w1T"], "w2T": p["w2T"], "h1T": p["h1T"],
             "h2T": p["h2T"], "w3T": p["w3T"], "s1": p["s1"], "s2": p["s2"],
             "h1s": p["h1s"], "h2s": p["h2s"], "s3": p["s3"]}
        for e in range(NEXP):
            m[f"rg{e}"] = p["rg"][e]
            m[f"ru{e}"] = p["ru"][e]
            m[f"rdk{e}"] = p["rdk"][e]
        for h in range(2):
            m[f"sd{h}"] = p["sd_h"][h]
        in_maps.append(m)
    res = run_bass_kernel_spmd(nc, in_maps, core_ids=list(range(8)))
    out = np.stack([np.ascontiguousarray(res.results[c]["out"].T)
                    for c in range(8)], axis=0)
    return out


# version 09
# speedup vs baseline: 1.0138x; 1.0138x over previous
from contextlib import ExitStack

import numpy as np
import concourse.bass as bass
import concourse.mybir as mybir
from concourse.bass_utils import run_bass_kernel_spmd

F32 = mybir.dt.float32
BF16 = mybir.dt.bfloat16
I32 = mybir.dt.int32
AF = mybir.ActivationFunctionType
ALU = mybir.AluOpType
AX = mybir.AxisListType

D = 1024
NC8 = 8
TFULL = 2048
NEXP = 8
CAP = 2560.0
NHEAD = 16
NKV = 4
HDIM = 64
IR = 4096
ISH = 8192
DH = 4096
F32EPS = float(np.finfo(np.float32).eps)


class Ctr:
    def __init__(self, nc, name, es):
        self.sem = es.enter_context(nc.semaphore(name))
        self.val = 0
        self.name = name

    def inc(self, instr, n=1):
        instr.then_inc(self.sem, n)
        self.val += n
        return self.val


class Kb:
    def __init__(self, nc, es):
        self.nc = nc
        self.es = es
        self.T = Ctr(nc, "cT", es)
        self.V = Ctr(nc, "cV", es)
        self.S = Ctr(nc, "cS", es)
        self.P = Ctr(nc, "cP", es)
        self.Y = Ctr(nc, "dY", es)
        self.G = Ctr(nc, "dG", es)
        self.GW = [Ctr(nc, "dGW0", es), Ctr(nc, "dGW1", es)]
        self.GX = [Ctr(nc, "dGX0", es), Ctr(nc, "dGX1", es)]
        self.YW = [Ctr(nc, "dYW0", es), Ctr(nc, "dYW1", es)]
        self.seen = {}
        self.ctrs = (self.T, self.V, self.S, self.P, self.Y, self.G,
                     self.GW[0], self.GW[1], self.GX[0], self.GX[1],
                     self.YW[0], self.YW[1])

    def eng(self, which):
        return {"T": self.nc.tensor, "V": self.nc.vector, "S": self.nc.scalar,
                "P": self.nc.gpsimd, "Y": self.nc.sync, "G": self.nc.gpsimd}[which]

    def wait(self, ek, ctr, val=None):
        if val is None:
            val = ctr.val
        if val <= 0:
            return
        key = (ek, ctr.name)
        if self.seen.get(key, -1) >= val:
            return
        self.eng(ek).wait_ge(ctr.sem, val)
        self.seen[key] = val

    def cp(self, i, p=128):
        return self.cpool[0:p, i:i + 1]

    def dma(self, qkey, out, in_, ctr=None, **kw):
        ctr = ctr or (self.Y if qkey == "Y" else self.G)
        self.wait(qkey, ctr, ctr.val)
        ins = self.eng(qkey).dma_start(out=out, in_=in_, **kw)
        return ctr.inc(ins, 16)

    def barrier(self, engines="TVSPYG"):
        for e in engines:
            for c in self.ctrs:
                self.wait(e, c)


def emit_consts(k, d):
    nc, es = k.nc, k.es
    cpool = es.enter_context(nc.sbuf_tensor("cpool_sb", [128, 8], F32))
    k.dma("Y", cpool[:, :], d["cpool"])
    k.cpool = cpool
    k.wait("S", k.Y, k.Y.val)
    k.wait("V", k.Y, k.Y.val)
    ones128 = es.enter_context(nc.sbuf_tensor("ones128", [128, 1], F32))
    ones128b = es.enter_context(nc.sbuf_tensor("ones128b", [128, 1], BF16))
    ones1 = es.enter_context(nc.sbuf_tensor("ones1", [1, 128], F32))
    idf = es.enter_context(nc.sbuf_tensor("idf", [128, 128], F32))
    k.ones128b = ones128b
    k.V.inc(nc.vector.memset(ones128b[:, :], 1.0))
    k.V.inc(nc.vector.memset(ones128[:, :], 1.0))
    k.V.inc(nc.vector.memset(ones1[:, :], 1.0))
    k.P.inc(nc.gpsimd.memset(idf[:, :], 1.0))
    k.wait("P", k.P, k.P.val)
    ins = nc.gpsimd.affine_select(idf[:, :], idf[:, :], pattern=[[-1, 128]],
                                  compare_op=ALU.is_equal, fill=0.0, base=0,
                                  channel_multiplier=1)
    pv = k.P.inc(ins)
    idb = es.enter_context(nc.sbuf_tensor("idb", [128, 128], BF16))
    k.wait("V", k.P, pv)
    k.V.inc(nc.vector.tensor_copy(idb[:, :], idf[:, :]))
    k.idb = idb
    return ones128, ones1, idf


def emit_ln(k, tag, X, Xb, ones128, ones1, T, eps=1e-5, out_bf=True,
            rstd_out=None):
    nc, es = k.nc, k.es
    ncol = T // 512
    k.barrier()
    with ExitStack() as ps:
        psAr = ps.enter_context(nc.psum_tensor(f"lnps{tag}", [128, 4096], F32))
        SQ = ps.enter_context(nc.sbuf_tensor(f"lnsq{tag}", [128, T], F32))
        stm = ps.enter_context(nc.sbuf_tensor(f"lnm{tag}", [1, T], F32))
        str_ = ps.enter_context(nc.sbuf_tensor(f"lnr{tag}", [1, T], F32))
        xv = k.V.val
        for c in range(NC8):
            k.wait("S", k.V, xv)
            k.wait("S", k.S, k.S.val)
            k.wait("S", k.T, k.T.val)
            ins = nc.scalar.activation(SQ[:, 0:T], X[:, c * T:(c + 1) * T], AF.Square)
            sv = k.S.inc(ins)
            k.wait("T", k.S, sv)
            k.wait("T", k.V, xv)
            for t in range(ncol):
                sl = slice(t * 512, (t + 1) * 512)
                nc.tensor.matmul(psAr[0:1, sl], ones128[:, 0:1],
                                 X[:, c * T + t * 512:c * T + (t + 1) * 512],
                                 start=(c == 0), stop=(c == NC8 - 1),
                                 skip_group_check=True)
                ins2 = nc.tensor.matmul(
                    psAr[0:1, 2048 + t * 512:2048 + (t + 1) * 512],
                    ones128[:, 0:1], SQ[:, sl],
                    start=(c == 0), stop=(c == NC8 - 1), skip_group_check=True)
            k.T.inc(ins2)
        tv = k.T.val
        k.wait("V", k.T, tv)
        ins = nc.vector.tensor_scalar_mul(stm[0:1, :], psAr[0:1, 0:T], 1.0 / D)
        mv = k.V.inc(ins)
        k.wait("S", k.V, mv)
        ins = nc.scalar.activation(str_[0:1, :], stm[0:1, :], AF.Square)
        sv = k.S.inc(ins)
        k.wait("V", k.S, sv)
        ins = nc.vector.scalar_tensor_tensor(str_[0:1, :], psAr[0:1, 2048:2048 + T],
                                             1.0 / D, str_[0:1, :], ALU.mult,
                                             ALU.subtract)
        vv = k.V.inc(ins)
        k.wait("S", k.V, vv)
        ins = nc.scalar.activation(str_[0:1, :], str_[0:1, :], AF.Ln,
                                   bias=k.cp(0, 1))
        sv = k.S.inc(ins)
        k.wait("S", k.S, sv)
        ins = nc.scalar.activation(str_[0:1, :], str_[0:1, :], AF.Exp,
                                   scale=k.cp(1, 1))
        sv = k.S.inc(ins)
        if rstd_out is not None:
            k.wait("V", k.S, sv)
            k.V.inc(nc.vector.tensor_copy(rstd_out[0:1, :], str_[0:1, :]))
        k.wait("T", k.V, mv)
        k.wait("T", k.S, sv)
        for t in range(ncol):
            sl = slice(t * 512, (t + 1) * 512)
            nc.tensor.matmul(psAr[:, sl], ones1[0:1, :], stm[0:1, sl],
                             start=True, stop=True)
            ins = nc.tensor.matmul(psAr[:, 2048 + t * 512:2048 + (t + 1) * 512],
                                   ones1[0:1, :], str_[0:1, sl], start=True,
                                   stop=True)
        tv = k.T.inc(ins)
        k.wait("V", k.T, tv)
        for c in range(NC8):
            ins = nc.vector.tensor_sub(X[:, c * T:(c + 1) * T],
                                       X[:, c * T:(c + 1) * T], psAr[:, 0:T])
            vv = k.V.inc(ins)
            k.wait("V", k.V, vv)
            ins = nc.vector.tensor_mul(X[:, c * T:(c + 1) * T],
                                       X[:, c * T:(c + 1) * T],
                                       psAr[:, 2048:2048 + T])
            vv = k.V.inc(ins)
            if out_bf:
                k.wait("S", k.V, vv)
                ins = nc.scalar.copy(Xb[:, c * T:(c + 1) * T],
                                     X[:, c * T:(c + 1) * T])
                k.S.inc(ins)
        k.barrier()


def emit_mixer_core(k, Xsrc, G, w, T):
    nc = k.nc
    ncol = T // 512
    wfull, bz, bbcd, alog = w["wfull"], w["bz"], w["bbcd"], w["alog"]
    sB, sC, sDel, sTmp = w["sB"], w["sC"], w["sDel"], w["sTmp"]
    yb, ones16 = w["yb"], w["ones16"]
    psZ, psZ2, psBCD, psy = w["psZ"], w["psZ2"], w["psBCD"], w["psy"]
    ones1 = w["ones1"]

    gelu_last = 0
    for zc in range(NC8):
        for t in range(ncol):
            ps_ = psZ if (zc * ncol + t) % 2 == 0 else psZ2
            for kc in range(8):
                ins = nc.tensor.matmul(
                    ps_[:, :],
                    wfull[:, kc * 1072 + zc * 128:kc * 1072 + (zc + 1) * 128],
                    Xsrc[:, kc * T + t * 512:kc * T + (t + 1) * 512],
                    start=(kc == 0), stop=(kc == 7))
            tv = k.T.inc(ins)
            k.wait("S", k.T, tv)
            ins = nc.scalar.activation(
                G[:, zc * T + t * 512:zc * T + (t + 1) * 512], ps_[:, :],
                AF.Gelu, bias=bz[:, zc:zc + 1])
            gelu_last = k.S.inc(ins)
            k.wait("T", k.S, gelu_last)
    for t in range(ncol):
        for j, c0 in enumerate((1024, 1040, 1056)):
            for kc in range(8):
                ins = nc.tensor.matmul(
                    psBCD[:, j * 512:j * 512 + 512],
                    wfull[:, kc * 1072 + c0:kc * 1072 + c0 + 16],
                    Xsrc[:, kc * T + t * 512:kc * T + (t + 1) * 512],
                    start=(kc == 0), stop=(kc == 7))
        tv = k.T.inc(ins)
        k.wait("V", k.T, tv)
        k.wait("V", k.V, k.V.val)
        for j, dst in enumerate((sB, sC, sDel)):
            ins = nc.vector.tensor_scalar_add(dst[:, t * 512:(t + 1) * 512],
                                              psBCD[:, j * 512:j * 512 + 512],
                                              bbcd[:, j:j + 1])
            k.V.inc(ins)
        k.wait("T", k.V, k.V.val)

    k.wait("S", k.V, k.V.val)
    sv = k.S.inc(nc.scalar.activation(sTmp[:, :], sDel[:, :], AF.Abs))
    k.wait("S", k.S, sv)
    sv = k.S.inc(nc.scalar.activation(sTmp[:, :], sTmp[:, :], AF.Exp,
                                      scale=k.cp(4, 16)))
    k.wait("S", k.S, sv)
    sv = k.S.inc(nc.scalar.activation(sTmp[:, :], sTmp[:, :], AF.Ln,
                                      bias=k.cp(5, 16)))
    k.wait("V", k.S, sv)
    vv = k.V.inc(nc.vector.tensor_scalar_max(sDel[:, :], sDel[:, :], 0.0))
    k.wait("V", k.V, vv)
    vv = k.V.inc(nc.vector.tensor_add(sDel[:, :], sDel[:, :], sTmp[:, :]))
    k.wait("V", k.V, vv)
    vv = k.V.inc(nc.vector.tensor_scalar_mul(sDel[:, :], sDel[:, :],
                                             alog[:, 0:1]))
    k.wait("S", k.V, vv)
    sv = k.S.inc(nc.scalar.activation(sDel[:, :], sDel[:, :], AF.Exp,
                                      scale=k.cp(4, 16)))
    k.wait("V", k.S, sv)
    vv = k.V.inc(nc.vector.tensor_tensor_scan(sTmp[:, :], sDel[:, :], sB[:, :],
                                              0.0, ALU.mult, ALU.add))
    k.wait("V", k.V, vv)
    vv = k.V.inc(nc.vector.tensor_mul(sTmp[:, :], sTmp[:, :], sC[:, :]))
    k.wait("T", k.V, vv)
    k.wait("T", k.S, k.S.val)
    for t in range(ncol):
        tv = k.T.inc(nc.tensor.matmul(psy[0:1, :], ones16[:, 0:1],
                                      sTmp[:, t * 512:(t + 1) * 512],
                                      start=True, stop=True))
        k.wait("V", k.T, tv)
        vv = k.V.inc(nc.vector.tensor_copy(sTmp[0:1, t * 512:(t + 1) * 512],
                                           psy[0:1, :]))
        k.wait("T", k.V, vv)
    for t in range(ncol):
        ps_ = psZ if t % 2 == 0 else psZ2
        tv = k.T.inc(nc.tensor.matmul(ps_[:, :], ones1[0:1, :],
                                      sTmp[0:1, t * 512:(t + 1) * 512],
                                      start=True, stop=True))
        k.wait("V", k.T, tv)
        vv = k.V.inc(nc.vector.tensor_copy(yb[:, t * 512:(t + 1) * 512],
                                           ps_[:, :]))
        k.wait("T", k.V, vv)
    k.wait("V", k.S, gelu_last)
    for c in range(NC8):
        k.wait("V", k.V, k.V.val)
        k.V.inc(nc.vector.tensor_mul(G[:, c * T:(c + 1) * T],
                                     G[:, c * T:(c + 1) * T], yb[:, 0:T]))


def alloc_mixer_scratch(k, ls, tag, T, ones1):
    nc = k.nc
    w = {}
    w["wfull"] = ls.enter_context(nc.sbuf_tensor(f"mx_wf{tag}", [128, 8 * 1072],
                                                 BF16))
    w["bz"] = ls.enter_context(nc.sbuf_tensor(f"mx_bz{tag}", [128, 8], F32))
    w["bbcd"] = ls.enter_context(nc.sbuf_tensor(f"mx_bb{tag}", [16, 3], F32))
    w["alog"] = ls.enter_context(nc.sbuf_tensor(f"mx_al{tag}", [16, 1], F32))
    w["sB"] = ls.enter_context(nc.sbuf_tensor(f"mx_B{tag}", [16, T], F32))
    w["sC"] = ls.enter_context(nc.sbuf_tensor(f"mx_C{tag}", [16, T], F32))
    w["sDel"] = ls.enter_context(nc.sbuf_tensor(f"mx_D{tag}", [16, T], F32))
    w["sTmp"] = ls.enter_context(nc.sbuf_tensor(f"mx_t{tag}", [16, T], F32))
    w["yb"] = ls.enter_context(nc.sbuf_tensor(f"mx_y{tag}", [128, T], BF16))
    w["ones16"] = ls.enter_context(nc.sbuf_tensor(f"mx_o16{tag}", [16, 1], F32))
    w["psZ"] = ls.enter_context(nc.psum_tensor(f"mx_pZ{tag}", [128, 512], F32))
    w["psZ2"] = ls.enter_context(nc.psum_tensor(f"mx_pZ2{tag}", [128, 512], F32))
    w["psBCD"] = ls.enter_context(nc.psum_tensor(f"mx_pB{tag}", [16, 3 * 512],
                                                 F32))
    w["psy"] = ls.enter_context(nc.psum_tensor(f"mx_py{tag}", [1, 512], F32))
    w["ones1"] = ones1
    return w


def load_mixer_weights(k, w, d):
    nc = k.nc
    for name, key in (("bz", "bz"), ("bbcd", "bbcd"), ("alog", "alog")):
        k.dma("Y", w[key][:, :], d[name])
    for kc in range(8):
        k.dma("G", w["wfull"][:, kc * 1072:(kc + 1) * 1072],
              d["w_in"][kc * 128:(kc + 1) * 128, :])
    ldy, ldg = k.Y.val, k.G.val
    k.wait("V", k.Y, ldy)
    k.V.inc(nc.vector.memset(w["ones16"][:, :], 1.0))
    k.wait("S", k.Y, ldy)
    sv = k.S.inc(nc.scalar.activation(w["alog"][:, :], w["alog"][:, :], AF.Exp))
    k.wait("V", k.S, sv)
    k.wait("T", k.G, ldg)
    k.wait("T", k.Y, ldy)
    k.wait("T", k.V, k.V.val)


def emit_l0_attn(k, X, Xb, d, ones128, ones1, T):
    nc, es = k.nc, k.es
    ncol = T // 512
    k.barrier()
    with ExitStack() as ls:
        w = alloc_mixer_scratch(k, ls, "a", T, ones1)
        wob = [ls.enter_context(nc.sbuf_tensor(f"l0wo{i}", [128, 1024], BF16))
               for i in range(2)]
        G = ls.enter_context(nc.sbuf_tensor("l0G", [128, 8 * T], BF16))
        bout = ls.enter_context(nc.sbuf_tensor("l0bout", [128, 8], F32))
        k.dma("Y", bout[:, :], d["bout"])
        load_mixer_weights(k, w, d)
        emit_mixer_core(k, Xb, G, w, T)
        uv = k.V.val
        k.wait("T", k.V, uv)
        k.wait("V", k.Y, k.Y.val)
        psZ, psZ2 = w["psZ"], w["psZ2"]
        for oc in range(NC8):
            b = oc % 2
            k.wait("G", k.T, k.T.val)
            k.dma("G", wob[b][:, :], d["wo_t"][oc])
            k.wait("T", k.G, k.G.val)
            for t in range(ncol):
                ps_ = psZ if (oc * ncol + t) % 2 == 0 else psZ2
                for kc in range(8):
                    ins = nc.tensor.matmul(
                        ps_[:, :], wob[b][:, kc * 128:(kc + 1) * 128],
                        G[:, kc * T + t * 512:kc * T + (t + 1) * 512],
                        start=(kc == 0), stop=(kc == 7))
                tv = k.T.inc(ins)
                k.wait("V", k.T, tv)
                sl = slice(oc * T + t * 512, oc * T + (t + 1) * 512)
                ins = nc.vector.scalar_tensor_tensor(X[:, sl], ps_[:, :],
                                                     bout[:, oc:oc + 1], X[:, sl],
                                                     ALU.add, ALU.add)
                vv = k.V.inc(ins)
                k.wait("T", k.V, vv)
        k.barrier()



def emit_phase0(k, d, ones128, ones1, idf, T, art):
    nc, es = k.nc, k.es
    ncol = T // 512
    ntile = T // 128
    k.barrier()
    with ExitStack() as ls:
        Xf = ls.enter_context(nc.sbuf_tensor("p_x", [128, 8 * T], BF16))
        G = ls.enter_context(nc.sbuf_tensor("p_g", [128, 8 * T], BF16))
        rwtb = ls.enter_context(nc.sbuf_tensor("p_rw", [128, 8 * 8], BF16))
        LX = ls.enter_context(nc.sbuf_tensor("p_lx", [128, 8 * 9], BF16))
        LU = ls.enter_context(nc.sbuf_tensor("p_lu", [128, 8 * 9], BF16))
        boutb = ls.enter_context(nc.sbuf_tensor("p_bo", [128, 8], BF16))
        cfull = ls.enter_context(nc.sbuf_tensor("p_cf", [128, 9], F32))
        swefull = ls.enter_context(nc.sbuf_tensor("p_sw", [128, 9], F32))
        c91 = ls.enter_context(nc.sbuf_tensor("p_c91", [9, 1], F32))
        c19 = ls.enter_context(nc.sbuf_tensor("p_c19", [1, 9], F32))
        PR = ls.enter_context(nc.sbuf_tensor("p_pr", [16, T], F32))
        ptile = ls.enter_context(nc.sbuf_tensor("p_pt", [128, 9], F32))
        pf = ls.enter_context(nc.sbuf_tensor("p_pf", [128, 8], F32))
        pf2 = ls.enter_context(nc.sbuf_tensor("p_pf2", [128, 8], F32))
        scr8 = ls.enter_context(nc.sbuf_tensor("p_s8", [128, 8], F32))
        mm1 = ls.enter_context(nc.sbuf_tensor("p_m1", [128, 1], F32))
        mm2 = ls.enter_context(nc.sbuf_tensor("p_m2", [128, 1], F32))
        mneg = ls.enter_context(nc.sbuf_tensor("p_mn", [128, 1], F32))
        selT = ls.enter_context(nc.sbuf_tensor("p_st", [128, 8 * ntile], F32))
        MSK1b = ls.enter_context(nc.sbuf_tensor("p_k1", [128, 8 * ntile], F32))
        MSK2b = ls.enter_context(nc.sbuf_tensor("p_k2", [128, 8 * ntile], F32))
        dpb = ls.enter_context(nc.sbuf_tensor("p_dp", [128, ntile], F32))
        selEb = ls.enter_context(nc.sbuf_tensor("p_se", [8, T], F32))
        CAc = ls.enter_context(nc.sbuf_tensor("p_ca", [8, 8], F32))
        CAsb = ls.enter_context(nc.sbuf_tensor("p_cas", [8, 8], F32))
        im8 = ls.enter_context(nc.sbuf_tensor("p_im8", [8, 8], F32))
        im128 = ls.enter_context(nc.sbuf_tensor("p_imc", [128, 8], F32))
        pmsb = ls.enter_context(nc.sbuf_tensor("p_pm", [8, 1], F32))

        for kc in range(8):
            k.dma("G", rwtb[:, kc * 8:(kc + 1) * 8],
                  d["rwT"][kc * 128:(kc + 1) * 128, :])
        k.dma("Y", im8[:, :], d["im8"])
        k.dma("Y", im128[:, :], d["im128"])
        k.dma("Y", pmsb[:, :], d["prevmask"])
        ldy, ldg = k.Y.val, k.G.val
        k.wait("V", k.Y, ldy)
        k.wait("V", k.G, ldg)
        k.V.inc(nc.vector.memset(art["selE"][:, :], 0.0))
        k.V.inc(nc.vector.memset(art["msk1"][:, :], 0.0))
        k.V.inc(nc.vector.memset(art["msk2"][:, :], 0.0))
        k.V.inc(nc.vector.memset(art["dp"][:, :], 0.0))
        for kc in range(8):
            k.wait("V", k.V, k.V.val)
            k.V.inc(nc.vector.tensor_copy(LX[:, kc * 9:kc * 9 + 8],
                                          rwtb[:, kc * 8:(kc + 1) * 8]))
            k.V.inc(nc.vector.tensor_scalar_mul(LX[:, kc * 9 + 8:kc * 9 + 9],
                                                ones128[:, 0:1], 1.0 / D))
        lxv = k.V.val

        with ExitStack() as pp:
            wout = pp.enter_context(nc.sbuf_tensor("p_wo", [128, 8 * 1024], BF16))
            WoT = pp.enter_context(nc.sbuf_tensor("p_wot", [128, 8 * 1024], BF16))
            bout = pp.enter_context(nc.sbuf_tensor("p_bof", [128, 8], F32))
            psTb = pp.enter_context(nc.psum_tensor("p_ptb", [128, 128], BF16))
            psLU = pp.enter_context(nc.psum_tensor("p_plu", [128, 9], F32))
            psc = pp.enter_context(nc.psum_tensor("p_pc", [9, 1], F32))
            psT9 = pp.enter_context(nc.psum_tensor("p_pt9", [128, 9], F32))
            for kc in range(8):
                k.dma("G", wout[:, kc * 1024:(kc + 1) * 1024],
                      d["w_out"][kc * 128:(kc + 1) * 128, :])
            k.dma("Y", bout[:, :], d["bout"])
            k.wait("V", k.Y, k.Y.val)
            vv = k.V.inc(nc.vector.tensor_copy(boutb[:, :], bout[:, :]))
            ldg = k.G.val
            k.wait("T", k.G, ldg)
            k.wait("T", k.P, k.P.val)
            k.wait("T", k.V, k.V.val)
            for jc in range(8):
                for dc in range(8):
                    k.wait("T", k.S, k.S.val)
                    tv = k.T.inc(nc.tensor.transpose(
                        psTb[:, :],
                        wout[:, dc * 1024 + jc * 128:dc * 1024 + (jc + 1) * 128],
                        k.idb[:, :]))
                    k.wait("S", k.T, tv)
                    k.S.inc(nc.scalar.copy(
                        WoT[:, jc * 1024 + dc * 128:jc * 1024 + (dc + 1) * 128],
                        psTb[:, :]))
            k.wait("T", k.S, k.S.val)
            k.wait("T", k.V, lxv)
            for dc in range(8):
                for jc in range(8):
                    ins = nc.tensor.matmul(psLU[:, :],
                                           WoT[:, jc * 1024 + dc * 128:
                                               jc * 1024 + (dc + 1) * 128],
                                           LX[:, jc * 9:(jc + 1) * 9],
                                           start=(jc == 0), stop=(jc == 7))
                tv = k.T.inc(ins)
                k.wait("S", k.T, tv)
                sv = k.S.inc(nc.scalar.copy(LU[:, dc * 9:(dc + 1) * 9],
                                            psLU[:, :]))
                k.wait("T", k.S, sv)
            for name in ("c", "s"):
                for dc in range(8):
                    rhs_ap = boutb[:, dc:dc + 1] if name == "c" \
                        else k.ones128b[:, 0:1]
                    ins = nc.tensor.matmul(psc[:, :], LX[:, dc * 9:(dc + 1) * 9],
                                           rhs_ap, start=(dc == 0),
                                           stop=(dc == 7))
                tv = k.T.inc(ins)
                k.wait("V", k.T, tv)
                vv = k.V.inc(nc.vector.tensor_copy(c91[:, :], psc[:, :]))
                k.wait("T", k.V, vv)
                tv = k.T.inc(nc.tensor.transpose(psT9[0:1, 0:9], c91[:, :],
                                                 idf[0:9, 0:9]))
                k.wait("V", k.T, tv)
                vv = k.V.inc(nc.vector.tensor_copy(c19[:, :], psT9[0:1, 0:9]))
                k.wait("T", k.V, vv)
                tv = k.T.inc(nc.tensor.matmul(psT9[:, :], ones1[0:1, :],
                                              c19[0:1, :], start=True,
                                              stop=True))
                k.wait("V", k.T, tv)
                dst = cfull if name == "c" else swefull
                vv = k.V.inc(nc.vector.tensor_copy(dst[:, :], psT9[:, :]))
                k.wait("T", k.V, vv)

        k.barrier()
        w = alloc_mixer_scratch(k, ls, "p", T, ones1)
        load_mixer_weights(k, w, d)
        with ExitStack() as pp:
            ps9 = pp.enter_context(nc.psum_tensor("p_p9", [9, 512], F32))
            psT = pp.enter_context(nc.psum_tensor("p_pT", [128, 128], F32))

            for b8 in range(NEXP):
                k.barrier()
                for c in range(NC8):
                    k.dma("G", Xf[:, c * T:(c + 1) * T],
                          d["x_all"][b8 * D + c * 128:b8 * D + (c + 1) * 128, :])
                k.wait("T", k.G, k.G.val)
                emit_mixer_core(k, Xf, G, w, T)
                uv = k.V.val
                k.wait("T", k.V, uv)
                for t in range(ncol):
                    ts = slice(t * 512, (t + 1) * 512)
                    k.wait("T", k.V, k.V.val)
                    for kc in range(8):
                        nc.tensor.matmul(ps9[:, :], LX[:, kc * 9:(kc + 1) * 9],
                                         Xf[:, kc * T + t * 512:
                                            kc * T + (t + 1) * 512],
                                         start=(kc == 0), stop=False)
                    for kc in range(8):
                        ins = nc.tensor.matmul(ps9[:, :],
                                               LU[:, kc * 9:(kc + 1) * 9],
                                               G[:, kc * T + t * 512:
                                                  kc * T + (t + 1) * 512],
                                               start=False, stop=(kc == 7))
                    tv = k.T.inc(ins)
                    k.wait("V", k.T, tv)
                    vv = k.V.inc(nc.vector.tensor_copy(PR[0:9, ts], ps9[:, :]))
                    k.wait("T", k.V, vv)
                for tt in range(ntile):
                    k.wait("T", k.V, k.V.val)
                    tv = k.T.inc(nc.tensor.transpose(
                        psT[:, 0:9], PR[0:9, tt * 128:(tt + 1) * 128],
                        idf[0:9, 0:9]))
                    k.wait("V", k.T, tv)
                    vv = k.V.inc(nc.vector.tensor_copy(ptile[:, :], psT[:, 0:9]))
                    k.wait("V", k.V, vv)
                    k.wait("T", k.V, vv)
                    vv = k.V.inc(nc.vector.tensor_scalar(
                        mneg[:, :], ptile[:, 8:9], cfull[:, 8:9], -1.0,
                        op0=ALU.add, op1=ALU.mult))
                    k.wait("V", k.V, vv)
                    vv = k.V.inc(nc.vector.tensor_add(scr8[:, :], ptile[:, 0:8],
                                                      cfull[:, 0:8]))
                    k.wait("V", k.V, vv)
                    vv = k.V.inc(nc.vector.scalar_tensor_tensor(
                        pf[:, :], swefull[:, 0:8], mneg[:, 0:1], scr8[:, :],
                        ALU.mult, ALU.add))
                    k.wait("V", k.V, vv)
                    vv = k.V.inc(nc.vector.tensor_reduce(mm1[:, :], pf[:, :],
                                                         axis=AX.X, op=ALU.max))
                    k.wait("V", k.V, vv)
                    vv = k.V.inc(nc.vector.tensor_scalar(
                        MSK1b[:, tt * 8:(tt + 1) * 8], pf[:, :], mm1[:, 0:1],
                        None, op0=ALU.is_equal))
                    k.wait("V", k.V, vv)
                    vv = k.V.inc(nc.vector.scalar_tensor_tensor(
                        pf2[:, :], MSK1b[:, tt * 8:(tt + 1) * 8], -1e30,
                        pf[:, :], ALU.mult, ALU.add))
                    k.wait("V", k.V, vv)
                    vv = k.V.inc(nc.vector.tensor_reduce(mm2[:, :], pf2[:, :],
                                                         axis=AX.X, op=ALU.max))
                    k.wait("V", k.V, vv)
                    vv = k.V.inc(nc.vector.tensor_scalar(
                        MSK2b[:, tt * 8:(tt + 1) * 8], pf2[:, :], mm2[:, 0:1],
                        None, op0=ALU.is_equal))
                    k.wait("V", k.V, vv)
                    vv = k.V.inc(nc.vector.tensor_sub(dpb[:, tt:tt + 1],
                                                      mm1[:, :], mm2[:, :]))
                    k.wait("V", k.V, vv)
                    vv = k.V.inc(nc.vector.tensor_add(
                        selT[:, tt * 8:(tt + 1) * 8],
                        MSK1b[:, tt * 8:(tt + 1) * 8],
                        MSK2b[:, tt * 8:(tt + 1) * 8]))
                k.wait("T", k.V, k.V.val)
                for tt in range(ntile):
                    tv = k.T.inc(nc.tensor.transpose(
                        psT[0:8, :], selT[:, tt * 8:(tt + 1) * 8], idf[:, :]))
                    k.wait("V", k.T, tv)
                    vv = k.V.inc(nc.vector.tensor_copy(
                        selEb[:, tt * 128:(tt + 1) * 128], psT[0:8, :]))
                    k.wait("T", k.V, vv)
                k.wait("V", k.V, k.V.val)
                vv = k.V.inc(nc.vector.tensor_reduce(CAc[:, b8:b8 + 1],
                                                     selEb[:, :], axis=AX.X,
                                                     op=ALU.add))
                k.wait("V", k.V, vv)
                vv = k.V.inc(nc.vector.scalar_tensor_tensor(
                    art["selE"][:, :], selEb[:, :], im8[:, b8:b8 + 1],
                    art["selE"][:, :], ALU.mult, ALU.add))
                k.wait("V", k.V, vv)
                vv = k.V.inc(nc.vector.scalar_tensor_tensor(
                    art["msk1"][:, :], MSK1b[:, :], im128[:, b8:b8 + 1],
                    art["msk1"][:, :], ALU.mult, ALU.add))
                k.wait("V", k.V, vv)
                vv = k.V.inc(nc.vector.scalar_tensor_tensor(
                    art["msk2"][:, :], MSK2b[:, :], im128[:, b8:b8 + 1],
                    art["msk2"][:, :], ALU.mult, ALU.add))
                k.wait("V", k.V, vv)
                vv = k.V.inc(nc.vector.scalar_tensor_tensor(
                    art["dp"][:, :], dpb[:, :], im128[:, b8:b8 + 1],
                    art["dp"][:, :], ALU.mult, ALU.add))
                k.wait("V", k.V, vv)

            k.wait("T", k.V, k.V.val)
            tv = k.T.inc(nc.tensor.transpose(psT[0:8, 0:8], CAc[:, :],
                                             idf[0:8, 0:8]))
            k.wait("V", k.T, tv)
            vv = k.V.inc(nc.vector.tensor_copy(CAsb[:, :], psT[0:8, 0:8]))
            k.wait("T", k.V, vv)
            tv = k.T.inc(nc.tensor.matmul(psT[0:8, 0:1], CAsb[:, :],
                                          pmsb[:, :], start=True, stop=True))
            k.wait("V", k.T, tv)
            k.V.inc(nc.vector.tensor_copy(art["off"][:, :], psT[0:8, 0:1]))
    k.barrier()


def emit_routing2(k, X, Xb, d, ones128, ones1, idf, T, art, rstd, weff,
                  rankI=None, weffT=None):
    nc, es = k.nc, k.es
    ncol = T // 512
    ntile = T // 128
    d_sc = d["scratch1"]
    k.barrier()
    with ExitStack() as ls:
        SQ = ls.enter_context(nc.sbuf_tensor("r2_sq", [128, T], F32))
        rms = ls.enter_context(nc.sbuf_tensor("r2_rms", [1, T], F32))
        rr_tm = ls.enter_context(nc.sbuf_tensor("r2_rmt", [128, ntile], F32))
        rw1 = ls.enter_context(nc.sbuf_tensor("r2_rw", [128, ntile], F32))
        wdT = ls.enter_context(nc.sbuf_tensor("r2_wd", [128, 8 * ntile], F32))
        wdE = ls.enter_context(nc.sbuf_tensor("r2_wde", [8, T], F32))
        csum = ls.enter_context(nc.sbuf_tensor("r2_cs", [8, T], F32))
        ones8 = ls.enter_context(nc.sbuf_tensor("r2_o8", [8, T], F32))
        psS = ls.enter_context(nc.psum_tensor("r2_psS", [1, 2048], F32))
        psT = ls.enter_context(nc.psum_tensor("r2_psT", [8, 128], F32))
        psT2 = ls.enter_context(nc.psum_tensor("r2_psT2", [128, 8], F32))
        k.V.inc(nc.vector.memset(ones8[:, :], 1.0))

        xv = k.V.val
        for c in range(NC8):
            k.wait("S", k.V, xv)
            k.wait("S", k.S, k.S.val)
            k.wait("S", k.T, k.T.val)
            sv = k.S.inc(nc.scalar.activation(SQ[:, 0:T],
                                              X[:, c * T:(c + 1) * T],
                                              AF.Square))
            k.wait("T", k.S, sv)
            for t in range(ncol):
                ins2 = nc.tensor.matmul(psS[0:1, t * 512:(t + 1) * 512],
                                        ones128[:, 0:1],
                                        SQ[:, t * 512:(t + 1) * 512],
                                        start=(c == 0), stop=(c == NC8 - 1),
                                        skip_group_check=True)
            k.T.inc(ins2)
        tv = k.T.val
        k.wait("S", k.T, tv)
        sv = k.S.inc(nc.scalar.activation(rms[0:1, :], psS[0:1, 0:T], AF.Ln,
                                          scale=k.cp(3, 1), bias=k.cp(2, 1)))
        k.wait("S", k.S, sv)
        sv = k.S.inc(nc.scalar.activation(rms[0:1, :], rms[0:1, :], AF.Exp,
                                          scale=k.cp(1, 1)))
        k.wait("V", k.S, sv)
        vv = k.V.inc(nc.vector.tensor_mul(rms[0:1, :], rms[0:1, :],
                                          rstd[0:1, :]))
        k.wait("G", k.V, vv)
        k.dma("G", d_sc[0:1, 0:T], rms[0:1, :])
        k.dma("G", rr_tm[:, 0:ntile],
              d_sc[0:1, 0:T].rearrange("o (a b) -> b (o a)", b=128),
              allow_slow_non_contiguous=True)
        rrv = k.G.val
        k.wait("V", k.G, rrv)
        vv = k.V.inc(nc.vector.tensor_mul(rw1[:, :], art["dp"][:, :],
                                          rr_tm[:, :]))
        k.wait("S", k.V, vv)
        sv = k.S.inc(nc.scalar.activation(rw1[:, :], rw1[:, :], AF.Exp,
                                          scale=k.cp(4)))
        k.wait("V", k.S, sv)
        vv = k.V.inc(nc.vector.tensor_scalar_add(rw1[:, :], rw1[:, :], 1.0))
        k.wait("V", k.V, vv)
        vv = k.V.inc(nc.vector.reciprocal(rw1[:, :], rw1[:, :]))
        k.wait("V", k.V, vv)
        for tt in range(ntile):
            s8 = slice(tt * 8, (tt + 1) * 8)
            vv = k.V.inc(nc.vector.tensor_sub(wdT[:, s8], art["msk1"][:, s8],
                                              art["msk2"][:, s8]))
            k.wait("V", k.V, vv)
            vv = k.V.inc(nc.vector.tensor_scalar_mul(wdT[:, s8], wdT[:, s8],
                                                     rw1[:, tt:tt + 1]))
            k.wait("V", k.V, vv)
            vv = k.V.inc(nc.vector.tensor_add(wdT[:, s8], wdT[:, s8],
                                              art["msk2"][:, s8]))
            k.wait("V", k.V, vv)
        k.wait("T", k.V, k.V.val)
        for tt in range(ntile):
            tv = k.T.inc(nc.tensor.transpose(psT[:, :],
                                             wdT[:, tt * 8:(tt + 1) * 8],
                                             idf[:, :]))
            k.wait("V", k.T, tv)
            vv = k.V.inc(nc.vector.tensor_copy(wdE[:, tt * 128:(tt + 1) * 128],
                                               psT[:, :]))
            k.wait("T", k.V, vv)
        k.wait("V", k.V, k.V.val)
        vv = k.V.inc(nc.vector.tensor_tensor_scan(csum[:, :], ones8[:, :],
                                                  art["selE"][:, :], 0.0,
                                                  ALU.mult, ALU.add))
        if rankI is not None:
            ntl = T // 128
            rkE = ls.enter_context(nc.sbuf_tensor("r2_rk", [8, T], F32))
            rkT = ls.enter_context(nc.sbuf_tensor("r2_rkt", [128, 8 * ntl], F32))
            k.wait("V", k.V, k.V.val)
            vv = k.V.inc(nc.vector.tensor_sub(rkE[:, :], csum[:, :],
                                              art["selE"][:, :]))
            k.wait("V", k.V, vv)
            vv = k.V.inc(nc.vector.tensor_scalar_add(rkE[:, :], rkE[:, :],
                                                     1e6))
            k.wait("V", k.V, vv)
            vv = k.V.inc(nc.vector.scalar_tensor_tensor(
                rkE[:, :], art["selE"][:, :], -1e6, rkE[:, :], ALU.mult,
                ALU.add))
            k.wait("T", k.V, vv)
            for tt in range(ntl):
                tv = k.T.inc(nc.tensor.transpose(
                    psT2[:, :], rkE[:, tt * 128:(tt + 1) * 128], idf[0:8, 0:8]))
                k.wait("V", k.T, tv)
                vv = k.V.inc(nc.vector.tensor_copy(
                    rkT[:, tt * 8:(tt + 1) * 8], psT2[:, :]))
                k.wait("T", k.V, vv)
            k.wait("V", k.V, k.V.val)
            vv = k.V.inc(nc.vector.tensor_copy(rankI[:, :], rkT[:, :]))
        k.wait("V", k.V, k.V.val)
        vv = k.V.inc(nc.vector.tensor_sub(weff[:, :], csum[:, :],
                                          art["selE"][:, :]))
        k.wait("V", k.V, vv)
        vv = k.V.inc(nc.vector.tensor_scalar_add(weff[:, :], weff[:, :],
                                                 art["off"][:, 0:1]))
        k.wait("V", k.V, vv)
        vv = k.V.inc(nc.vector.tensor_scalar(weff[:, :], weff[:, :], CAP,
                                             None, op0=ALU.is_lt))
        k.wait("V", k.V, vv)
        vv = k.V.inc(nc.vector.tensor_mul(weff[:, :], weff[:, :],
                                          art["selE"][:, :]))
        k.wait("V", k.V, vv)
        vv = k.V.inc(nc.vector.tensor_mul(weff[:, :], weff[:, :], wdE[:, :]))
        if weffT is not None:
            k.wait("T", k.V, vv)
            for tt in range(ntile):
                tv = k.T.inc(nc.tensor.transpose(
                    psT2[:, :], weff[:, tt * 128:(tt + 1) * 128],
                    idf[0:8, 0:8]))
                k.wait("V", k.T, tv)
                vv = k.V.inc(nc.vector.tensor_copy(
                    weffT[:, tt * 8:(tt + 1) * 8], psT2[:, :]))
                k.wait("T", k.V, vv)
    k.barrier()


def emit_routing(k, X, Xb, d, ones128, ones1, idf, T, route_sb):
    nc, es = k.nc, k.es
    ncol = T // 512
    ntile = T // 128
    k.barrier()
    selE, wdE, csum, counts = (route_sb["selE"], route_sb["wdE"],
                               route_sb["csum"], route_sb["counts"])
    d_sc = d["scratch1"]
    with ExitStack() as ls:
        rwt = ls.enter_context(nc.sbuf_tensor("rt_w", [128, 8 * 8], F32))
        SQ = ls.enter_context(nc.sbuf_tensor("rt_sq", [128, T], F32))
        rms = ls.enter_context(nc.sbuf_tensor("rt_rms", [1, T], F32))
        rms_tm = ls.enter_context(nc.sbuf_tensor("rt_rmt", [128, ntile], F32))
        lgT = ls.enter_context(nc.sbuf_tensor("rt_lg", [128, 8 * ntile], F32))
        m1 = ls.enter_context(nc.sbuf_tensor("rt_m1", [128, ntile], F32))
        m2 = ls.enter_context(nc.sbuf_tensor("rt_m2", [128, ntile], F32))
        msk1 = ls.enter_context(nc.sbuf_tensor("rt_k1", [128, 8 * ntile], F32))
        msk2 = ls.enter_context(nc.sbuf_tensor("rt_k2", [128, 8 * ntile], F32))
        rw1 = ls.enter_context(nc.sbuf_tensor("rt_rw1", [128, ntile], F32))
        selT = ls.enter_context(nc.sbuf_tensor("rt_selT", [128, 8 * ntile], F32))
        wdT = ls.enter_context(nc.sbuf_tensor("rt_wdT", [128, 8 * ntile], F32))
        ones8 = ls.enter_context(nc.sbuf_tensor("rt_o8", [8, T], F32))
        psS = ls.enter_context(nc.psum_tensor("rt_psS", [1, 2048], F32))
        pslg = ls.enter_context(nc.psum_tensor("pslg", [128, 8], F32))
        pslg2 = ls.enter_context(nc.psum_tensor("pslg2", [128, 8], F32))
        pstr = ls.enter_context(nc.psum_tensor("pstr", [8, 128], F32))
        pstr2 = ls.enter_context(nc.psum_tensor("pstr2", [8, 128], F32))

        for kc in range(8):
            k.dma("G", rwt[:, kc * 8:(kc + 1) * 8],
                                        d["rwT"][kc * 128:(kc + 1) * 128, :])
        ldg = k.G.val
        k.V.inc(nc.vector.memset(ones8[:, :], 1.0))

        xv = k.V.val
        for c in range(NC8):
            k.wait("S", k.V, xv)
            k.wait("S", k.S, k.S.val)
            k.wait("S", k.T, k.T.val)
            ins = nc.scalar.activation(SQ[:, 0:T], X[:, c * T:(c + 1) * T],
                                       AF.Square)
            sv = k.S.inc(ins)
            k.wait("T", k.S, sv)
            for t in range(ncol):
                ins2 = nc.tensor.matmul(psS[0:1, t * 512:(t + 1) * 512],
                                        ones128[:, 0:1],
                                        SQ[:, t * 512:(t + 1) * 512],
                                        start=(c == 0), stop=(c == NC8 - 1),
                                        skip_group_check=True)
            k.T.inc(ins2)
        tv = k.T.val
        k.wait("S", k.T, tv)
        ins = nc.scalar.activation(rms[0:1, :], psS[0:1, 0:T], AF.Ln,
                                   scale=k.cp(3, 1), bias=k.cp(2, 1))
        sv = k.S.inc(ins)
        k.wait("S", k.S, sv)
        sv = k.S.inc(nc.scalar.activation(rms[0:1, :], rms[0:1, :], AF.Exp,
                                          scale=k.cp(1, 1)))
        k.wait("G", k.S, sv)
        k.dma("G", d_sc[0:1, 0:T], rms[0:1, :])
        k.wait("G", k.G, k.G.val)
        k.dma("G", 
            rms_tm[:, 0:ntile],
            d_sc[0:1, 0:T].rearrange("o (a b) -> b (o a)", b=128),
            allow_slow_non_contiguous=True)
        rmsv = k.G.val

        k.wait("T", k.G, ldg)
        k.wait("T", k.Y, k.Y.val)
        for tt in range(ntile):
            ps_ = pslg if tt % 2 == 0 else pslg2
            for kc in range(8):
                ins = nc.tensor.matmul(
                    ps_[:, :], X[:, kc * T + tt * 128:kc * T + (tt + 1) * 128],
                    rwt[:, kc * 8:(kc + 1) * 8], start=(kc == 0), stop=(kc == 7))
            tv = k.T.inc(ins)
            k.wait("V", k.T, tv)
            vv = k.V.inc(nc.vector.tensor_copy(lgT[:, tt * 8:(tt + 1) * 8],
                                               ps_[:, :]))
            k.wait("T", k.V, vv)
        for tt in range(ntile):
            s8 = slice(tt * 8, (tt + 1) * 8)
            s1_ = slice(tt, tt + 1)
            k.wait("V", k.V, k.V.val)
            vv = k.V.inc(nc.vector.tensor_reduce(m1[:, s1_], lgT[:, s8],
                                                 axis=AX.X, op=ALU.max))
            k.wait("V", k.V, vv)
            vv = k.V.inc(nc.vector.tensor_scalar(msk1[:, s8], lgT[:, s8],
                                                 m1[:, s1_], None,
                                                 op0=ALU.is_equal))
            k.wait("V", k.V, vv)
            vv = k.V.inc(nc.vector.scalar_tensor_tensor(msk2[:, s8], msk1[:, s8],
                                                        -1e30, lgT[:, s8],
                                                        ALU.mult, ALU.add))
            k.wait("V", k.V, vv)
            vv = k.V.inc(nc.vector.tensor_reduce(m2[:, s1_], msk2[:, s8],
                                                 axis=AX.X, op=ALU.max))
            k.wait("V", k.V, vv)
            vv = k.V.inc(nc.vector.tensor_scalar(msk2[:, s8], msk2[:, s8],
                                                 m2[:, s1_], None,
                                                 op0=ALU.is_equal))
            k.wait("V", k.V, vv)
            vv = k.V.inc(nc.vector.tensor_sub(m1[:, s1_], m1[:, s1_], m2[:, s1_]))
            k.wait("V", k.G, rmsv)
            k.wait("V", k.V, vv)
            vv = k.V.inc(nc.vector.tensor_mul(m1[:, s1_], m1[:, s1_],
                                              rms_tm[:, s1_]))
            k.wait("S", k.V, vv)
            sv = k.S.inc(nc.scalar.activation(m2[:, s1_], m1[:, s1_], AF.Exp,
                                              scale=k.cp(4)))
            k.wait("V", k.S, sv)
            vv = k.V.inc(nc.vector.tensor_scalar_add(m2[:, s1_], m2[:, s1_], 1.0))
            k.wait("V", k.V, vv)
            vv = k.V.inc(nc.vector.reciprocal(rw1[:, s1_], m2[:, s1_]))
            k.wait("V", k.V, vv)
            k.V.inc(nc.vector.tensor_add(selT[:, s8], msk1[:, s8], msk2[:, s8]))
            vv = k.V.inc(nc.vector.tensor_sub(wdT[:, s8], msk1[:, s8],
                                              msk2[:, s8]))
            k.wait("V", k.V, vv)
            vv = k.V.inc(nc.vector.tensor_scalar_mul(wdT[:, s8], wdT[:, s8],
                                                     rw1[:, s1_]))
            k.wait("V", k.V, vv)
            vv = k.V.inc(nc.vector.tensor_add(wdT[:, s8], wdT[:, s8],
                                              msk2[:, s8]))
        topv = k.V.val
        k.wait("T", k.V, topv)
        for tt in range(ntile):
            tv = k.T.inc(nc.tensor.transpose(pstr[:, :],
                                             selT[:, tt * 8:(tt + 1) * 8],
                                             idf[:, :]))
            k.wait("V", k.T, tv)
            vv = k.V.inc(nc.vector.tensor_copy(selE[:, tt * 128:(tt + 1) * 128],
                                               pstr[:, :]))
            k.wait("T", k.V, vv)
            tv = k.T.inc(nc.tensor.transpose(pstr2[:, :],
                                             wdT[:, tt * 8:(tt + 1) * 8],
                                             idf[:, :]))
            k.wait("V", k.T, tv)
            vv = k.V.inc(nc.vector.tensor_copy(wdE[:, tt * 128:(tt + 1) * 128],
                                               pstr2[:, :]))
            k.wait("T", k.V, vv)
        k.wait("V", k.V, k.V.val)
        vv = k.V.inc(nc.vector.tensor_tensor_scan(csum[:, :], ones8[:, :],
                                                  selE[:, :], 0.0, ALU.mult,
                                                  ALU.add))
        k.wait("V", k.V, vv)
        k.V.inc(nc.vector.tensor_copy(counts[:, 0:1], csum[:, T - 1:T]))
        k.barrier()


def emit_weff(k, d, route_sb, T):
    nc, es = k.nc, k.es
    weff = route_sb["weff"]
    selE, wdE, csum = route_sb["selE"], route_sb["wdE"], route_sb["csum"]
    with ExitStack() as ls:
        ca = ls.enter_context(nc.sbuf_tensor("wf_ca", [8, 8], F32))
        pm = ls.enter_context(nc.sbuf_tensor("wf_pm", [8, 1], F32))
        off = ls.enter_context(nc.sbuf_tensor("wf_off", [8, 1], F32))
        psoff = ls.enter_context(nc.psum_tensor("psoff", [8, 1], F32))
        k.dma("Y", ca[:, :], d["counts_all"])
        k.dma("Y", pm[:, :], d["prevmask"])
        ldy = k.Y.val
        k.wait("T", k.Y, ldy)
        tv = k.T.inc(nc.tensor.matmul(psoff[:, :], ca[:, :], pm[:, :],
                                      start=True, stop=True))
        k.wait("V", k.T, tv)
        vv = k.V.inc(nc.vector.tensor_copy(off[:, 0:1], psoff[:, :]))
        k.wait("V", k.V, vv)
        vv = k.V.inc(nc.vector.tensor_sub(weff[:, :], csum[:, :], selE[:, :]))
        k.wait("V", k.V, vv)
        vv = k.V.inc(nc.vector.tensor_scalar_add(weff[:, :], weff[:, :],
                                                 off[:, 0:1]))
        k.wait("V", k.V, vv)
        vv = k.V.inc(nc.vector.tensor_scalar(weff[:, :], weff[:, :], CAP, None,
                                             op0=ALU.is_lt))
        k.wait("V", k.V, vv)
        vv = k.V.inc(nc.vector.tensor_mul(weff[:, :], weff[:, :], selE[:, :]))
        k.wait("V", k.V, vv)
        k.V.inc(nc.vector.tensor_mul(weff[:, :], weff[:, :], wdE[:, :]))
        k.barrier()


def emit_mm_phase(k, d_wt, n_ob, K, rhs_sb, rhs_T, wb, psP, evac,
                  kc0=0, tok0=0, n_tcol=None):
    nc = k.nc
    nkc = K // 128
    if n_tcol is None:
        n_tcol = rhs_T // 512
    nbuf = len(wb)
    k.wait("T", k.V, k.V.val)
    k.wait("T", k.S, k.S.val)
    war = [(k.T, k.T.val)] * nbuf
    ps_war = [None, None]
    for ob in range(n_ob):
        b = ob % nbuf
        if war[b] is not None:
            ctr, v = war[b]
            k.wait("G", ctr, v)
        ld = k.dma("G", wb[b][:, 0:K], d_wt[ob][:, kc0 * 128:kc0 * 128 + K],
                   ctr=k.GW[b % 2])
        k.wait("T", k.GW[b % 2], ld)
        for t in range(n_tcol):
            pb = (ob * n_tcol + t) % 2
            ps_ = psP[pb]
            if ps_war[pb] is not None:
                ctr, v = ps_war[pb]
                k.wait("T", ctr, v)
            for kc in range(nkc):
                ins = nc.tensor.matmul(
                    ps_[:, :], wb[b][:, kc * 128:(kc + 1) * 128],
                    rhs_sb[:, kc * rhs_T + tok0 + t * 512:
                           kc * rhs_T + tok0 + (t + 1) * 512],
                    start=(kc == 0), stop=(kc == nkc - 1))
            tv = k.T.inc(ins)
            ps_war[pb] = evac(ob, t, ps_, tv)
        war[b] = (k.T, k.T.val)


CLOC = 1024


def emit_moe_routed(k, Xb, d, ones128, ones1, idf, rankI, weffT, T):
    nc, es = k.nc, k.es
    ncol = T // 512
    ntile = T // 128
    k.barrier()
    bregT = nc.gpsimd.to_reg(T - 1)
    bregC = nc.gpsimd.to_reg(CLOC - 1)
    with ExitStack() as ls:
        wgu = [ls.enter_context(nc.sbuf_tensor(f"m_wgu{i}", [128, D], BF16))
               for i in range(2)]
        wdn = [ls.enter_context(nc.sbuf_tensor(f"m_wdn{i}", [128, 2048], BF16))
               for i in range(2)]
        iotaF = ls.enter_context(nc.sbuf_tensor("m_iota", [128, ntile], F32))
        bigF = ls.enter_context(nc.sbuf_tensor("m_big", [128, 8], F32))
        zero1k = ls.enter_context(nc.sbuf_tensor("m_z", [128, 1024], F32))
        pairt = [ls.enter_context(nc.sbuf_tensor(f"m_pt{i}", [128, 2], F32))
                 for i in range(2)]
        dispbF = ls.enter_context(nc.sbuf_tensor("m_dbf", [128, 1], F32))
        xrow = [ls.enter_context(nc.sbuf_tensor(f"m_xr{i}", [128, 1024], BF16))
                for i in range(2)]
        eo = [ls.enter_context(nc.sbuf_tensor(f"m_eo{i}", [128, 1024], F32))
              for i in range(2)]
        psP = [ls.enter_context(nc.psum_tensor(f"m_ps{i}", [128, 512], F32))
               for i in range(2)]
        psTb = ls.enter_context(nc.psum_tensor("m_ptb", [128, 128], BF16))

        k.P.inc(nc.gpsimd.iota(iotaF[:, :], pattern=[[128, ntile]], base=0,
                               channel_multiplier=1,
                               allow_small_or_imprecise_dtypes=True))
        k.P.inc(nc.gpsimd.iota(bigF[:, :], pattern=[[0, 8]], base=1000000,
                               channel_multiplier=0,
                               allow_small_or_imprecise_dtypes=True))
        k.V.inc(nc.vector.memset(zero1k[:, :], 0.0))
        k.barrier()

        for tt in range(ntile):
            b = tt % 2
            k.wait("S", k.YW[b], k.YW[b].val)
            for c in range(NC8):
                k.wait("T", k.S, k.S.val)
                tv = k.T.inc(nc.tensor.transpose(
                    psTb[:, :], Xb[:, c * T + tt * 128:c * T + (tt + 1) * 128],
                    k.idb[:, :]))
                k.wait("S", k.T, tv)
                k.S.inc(nc.scalar.copy(xrow[b][:, c * 128:(c + 1) * 128],
                                       psTb[:, :]))
            k.wait("Y", k.S, k.S.val)
            k.wait("Y", k.V, k.V.val)
            k.dma("Y", d["x2tm"][tt * 128:(tt + 1) * 128, :], xrow[b][:, :],
                  ctr=k.YW[b])
            k.dma("Y", d["routedtm"][tt * 128:(tt + 1) * 128, :], zero1k[:, :])
        k.barrier()

        for ex in range(NEXP):
            with ExitStack() as se:
                dispb = se.enter_context(nc.sbuf_tensor(f"m_db{ex}", [128, 8],
                                                        I32))
                wdsp = se.enter_context(nc.sbuf_tensor(f"m_wd{ex}", [128, 8],
                                                       F32))
                XD = se.enter_context(nc.sbuf_tensor(f"m_xd{ex}",
                                                     [128, 8 * CLOC], BF16))
                k.wait("G", k.P, k.P.val)
                k.wait("G", k.V, k.V.val)
                k.dma("G", d["paird"][ex * CLOC:(ex + 1) * CLOC, 0:1],
                      bigF[:, :], allow_slow_non_contiguous=True)
                k.dma("G", d["paird"][ex * CLOC:(ex + 1) * CLOC, 1:2],
                      zero1k[:, 0:8], allow_slow_non_contiguous=True)
                k.wait("G", k.G, k.G.val)
                for tt in range(ntile):
                    bx = tt % 2
                    k.wait("V", k.GX[0], k.GX[0].val)
                    k.wait("V", k.GX[1], k.GX[1].val)
                    k.V.inc(nc.vector.tensor_copy(pairt[bx][:, 0:1],
                                                  iotaF[:, tt:tt + 1]))
                    vv = k.V.inc(nc.vector.tensor_copy(
                        pairt[bx][:, 1:2],
                        weffT[:, tt * 8 + ex:tt * 8 + ex + 1]))
                    k.wait("G", k.V, vv)
                    ins = nc.gpsimd.indirect_dma_start(
                        out=d["paird"],
                        out_offset=bass.IndirectOffsetOnAxis(
                            ap=rankI[:, tt * 8 + ex:tt * 8 + ex + 1], axis=0),
                        in_=pairt[bx][:, :], in_offset=None,
                        element_offset=ex * CLOC * 2,
                        bounds_check=bregC, oob_is_err=False)
                    k.GX[bx].inc(ins, 16)
                k.wait("Y", k.GX[0], k.GX[0].val)
                k.wait("Y", k.GX[1], k.GX[1].val)
                for st in range(8):
                    k.wait("Y", k.V, k.V.val)
                    k.dma("Y", dispbF[:, 0:1],
                          d["paird"][ex * CLOC + st * 128:
                                     ex * CLOC + (st + 1) * 128, 0:1],
                          allow_slow_non_contiguous=True)
                    k.wait("V", k.Y, k.Y.val)
                    vv = k.V.inc(nc.vector.tensor_copy(dispb[:, st:st + 1],
                                                       dispbF[:, 0:1]))
                    k.dma("Y", wdsp[:, st:st + 1],
                          d["paird"][ex * CLOC + st * 128:
                                     ex * CLOC + (st + 1) * 128, 1:2],
                          allow_slow_non_contiguous=True)
                with ExitStack() as sg_:
                    xg = sg_.enter_context(nc.sbuf_tensor(f"m_xg{ex}",
                                                          [128, 8 * 1024],
                                                          BF16))
                    if ex == 0:
                        k.wait("V", k.G, k.G.val)
                        k.V.inc(nc.vector.memset(xg[:, :], 0.0))
                        k.wait("G", k.V, k.V.val)
                    k.wait("G", k.V, k.V.val)
                    k.wait("G", k.T, k.T.val)
                    gx_evt = [0, 0]
                    for st in range(8):
                        bx = st % 2
                        ins = nc.gpsimd.indirect_dma_start(
                            out=xg[:, st * 1024:(st + 1) * 1024],
                            out_offset=None, in_=d["x2tm"],
                            in_offset=bass.IndirectOffsetOnAxis(
                                ap=dispb[:, st:st + 1], axis=0),
                            bounds_check=bregT, oob_is_err=False)
                        gx_evt[bx] = k.GX[bx].inc(ins, 16)
                    k.wait("T", k.GX[0], gx_evt[0])
                    k.wait("T", k.GX[1], gx_evt[1])
                    for st in range(8):
                        for c in range(NC8):
                            k.wait("T", k.S, k.S.val)
                            tv = k.T.inc(nc.tensor.transpose(
                                psTb[:, :],
                                xg[:, st * 1024 + c * 128:
                                   st * 1024 + (c + 1) * 128],
                                k.idb[:, :]))
                            k.wait("S", k.T, tv)
                            k.S.inc(nc.scalar.copy(
                                XD[:, c * CLOC + st * 128:
                                   c * CLOC + (st + 1) * 128], psTb[:, :]))
                k.barrier()
                with ExitStack() as sh:
                    He = sh.enter_context(nc.sbuf_tensor(f"m_he{ex}",
                                                         [128, 16 * CLOC],
                                                         BF16))
                    rdres = sh.enter_context(nc.sbuf_tensor(f"m_rd{ex}",
                                                            [128, 16 * 1024],
                                                            BF16))
                    for half in range(2):
                        def evac_gate(ob, t, ps_, tv):
                            k.wait("S", k.T, tv)
                            ins = nc.scalar.activation(
                                He[:, ob * CLOC + t * 512:
                                   ob * CLOC + (t + 1) * 512],
                                ps_[:, :], AF.Silu)
                            return (k.S, k.S.inc(ins))

                        def evac_up(ob, t, ps_, tv):
                            sl = slice(ob * CLOC + t * 512,
                                       ob * CLOC + (t + 1) * 512)
                            k.wait("V", k.T, tv)
                            k.wait("V", k.S, k.S.val)
                            k.wait("V", k.V, k.V.val)
                            ins = nc.vector.tensor_mul(He[:, sl], He[:, sl],
                                                       ps_[:, :])
                            return (k.V, k.V.inc(ins))
                        emit_mm_phase(k, d["rg"][ex][half * 16:(half + 1) * 16],
                                      16, D, XD, CLOC, wgu, psP, evac_gate)
                        emit_mm_phase(k, d["ru"][ex][half * 16:(half + 1) * 16],
                                      16, D, XD, CLOC, wgu, psP, evac_up)
                        k.wait("G", k.T, k.T.val)
                        ldrd = k.dma("G",
                                     rdres.rearrange("p (a n) -> p a n", a=16),
                                     d["rdk"][ex][half * 16:(half + 1) * 16]
                                     .rearrange("a p n -> p a n"),
                                     ctr=k.GW[0])
                        k.wait("T", k.GW[0], ldrd)
                        k.wait("T", k.S, k.S.val)
                        k.wait("T", k.V, k.V.val)
                        for st in range(8):
                            b = st % 2
                            k.wait("V", k.GX[b], k.GX[b].val)
                            k.wait("T", k.V, k.V.val)
                            for oc2 in range(2):
                                for kc in range(16):
                                    ins = nc.tensor.matmul(
                                        psP[oc2][:, :],
                                        He[:, kc * CLOC + st * 128:
                                           kc * CLOC + (st + 1) * 128],
                                        rdres[:, kc * 1024 + oc2 * 512:
                                              kc * 1024 + oc2 * 512 + 512],
                                        start=(kc == 0), stop=(kc == 15))
                                tv = k.T.inc(ins)
                                k.wait("V", k.T, tv)
                                vv = k.V.inc(nc.vector.tensor_scalar_mul(
                                    eo[b][:, oc2 * 512:(oc2 + 1) * 512],
                                    psP[oc2][:, :], wdsp[:, st:st + 1]))
                            k.wait("G", k.V, vv)
                            ins = nc.gpsimd.indirect_dma_start(
                                out=d["routedtm"],
                                out_offset=bass.IndirectOffsetOnAxis(
                                    ap=dispb[:, st:st + 1], axis=0),
                                in_=eo[b][:, :], in_offset=None,
                                bounds_check=bregT, oob_is_err=False,
                                compute_op=ALU.add)
                            k.GX[b].inc(ins, 16)
                k.barrier()

    k.barrier()


def emit_moe_shared(k, X, Xb, H, d, ones128, ones1, idf, T):
    nc, es = k.nc, k.es
    ntile = T // 128
    k.barrier()
    with ExitStack() as ls:
        wgu = [ls.enter_context(nc.sbuf_tensor(f"s_wgu{i}", [128, D], BF16))
               for i in range(2)]
        wdn = [ls.enter_context(nc.sbuf_tensor(f"s_wdn{i}", [128, 2048], BF16))
               for i in range(2)]
        psP = [ls.enter_context(nc.psum_tensor(f"s_ps{i}", [128, 512], F32))
               for i in range(2)]
        with ExitStack() as sr:
            rin = [sr.enter_context(nc.sbuf_tensor(f"m_ri{i}", [128, 1024],
                                                   F32)) for i in range(2)]
            psTf = sr.enter_context(nc.psum_tensor("m_ptf", [128, 128], F32))
            for tt in range(ntile):
                b = tt % 2
                k.wait("G", k.T, k.T.val)
                k.dma("G", rin[b][:, :], d["routedtm"][tt * 128:(tt + 1) * 128, :])
                k.wait("T", k.G, k.G.val)
                for c in range(NC8):
                    k.wait("T", k.V, k.V.val)
                    tv = k.T.inc(nc.tensor.transpose(
                        psTf[:, :], rin[b][:, c * 128:(c + 1) * 128],
                        idf[:, :]))
                    k.wait("V", k.T, tv)
                    sl = slice(c * T + tt * 128, c * T + (tt + 1) * 128)
                    vv = k.V.inc(nc.vector.tensor_add(X[:, sl], X[:, sl],
                                                      psTf[:, :]))
        k.barrier()

        def evac_gate_s(ob, t, ps_, tv):
            k.wait("S", k.T, tv)
            ins = nc.scalar.activation(
                H[:, ob * T + t * 512:ob * T + (t + 1) * 512], ps_[:, :],
                AF.Silu)
            return (k.S, k.S.inc(ins))

        def evac_up_s(ob, t, ps_, tv):
            sl = slice(ob * T + t * 512, ob * T + (t + 1) * 512)
            k.wait("V", k.T, tv)
            k.wait("V", k.S, k.S.val)
            k.wait("V", k.V, k.V.val)
            ins = nc.vector.tensor_mul(H[:, sl], H[:, sl], ps_[:, :])
            return (k.V, k.V.inc(ins))

        def evac_down_s(ob, t, ps_, tv):
            sl = slice(ob * T + t * 512, ob * T + (t + 1) * 512)
            k.wait("V", k.T, tv)
            k.wait("V", k.V, k.V.val)
            ins = nc.vector.tensor_add(X[:, sl], X[:, sl], ps_[:, :])
            return (k.V, k.V.inc(ins))

        for q in range(4):
            emit_mm_phase(k, d["sg"][q * 16:(q + 1) * 16], 16, D, Xb, T,
                          wgu, psP, evac_gate_s)
            emit_mm_phase(k, d["su"][q * 16:(q + 1) * 16], 16, D, Xb, T,
                          wgu, psP, evac_up_s)
            emit_mm_phase(k, d["sd_h"][q // 2], NC8, 2048, H, T, wdn, psP,
                          evac_down_s, kc0=(q % 2) * 16)
    k.barrier()



def emit_l1_attn(k, X, Xb, d, ones128, ones1, T):
    nc, es = k.nc, k.es
    ncol = T // 512
    ntile = T // 128
    k.barrier()

    def norm_pipeline(src64, sl, ps1, nq1, psB, SQ64):
        k.wait("S", k.S, k.S.val)
        k.wait("S", k.T, k.T.val)
        k.wait("S", k.V, k.V.val)
        sv = k.S.inc(nc.scalar.activation(SQ64[:, :], src64[:, sl], AF.Square))
        k.wait("T", k.S, sv)
        tv = k.T.inc(nc.tensor.matmul(ps1[0:1, :], ones128[0:64, 0:1],
                                      SQ64[:, :], start=True, stop=True))
        k.wait("V", k.T, tv)
        vv = k.V.inc(nc.vector.tensor_copy(nq1[0:1, :], ps1[0:1, :]))
        k.wait("S", k.V, vv)
        sv = k.S.inc(nc.scalar.activation(nq1[0:1, :], nq1[0:1, :], AF.Ln))
        k.wait("S", k.S, sv)
        sv = k.S.inc(nc.scalar.activation(nq1[0:1, :], nq1[0:1, :], AF.Exp,
                                          scale=k.cp(7, 1)))
        k.wait("V", k.S, sv)
        vv = k.V.inc(nc.vector.tensor_scalar_add(nq1[0:1, :], nq1[0:1, :], 1e-6))
        k.wait("V", k.V, vv)
        vv = k.V.inc(nc.vector.reciprocal(nq1[0:1, :], nq1[0:1, :]))
        k.wait("T", k.V, vv)
        k.wait("T", k.T, k.T.val)
        tv = k.T.inc(nc.tensor.matmul(psB[:, :], ones1[0:1, 0:64], nq1[0:1, :],
                                      start=True, stop=True))
        k.wait("V", k.T, tv)
        vv = k.V.inc(nc.vector.tensor_mul(src64[:, sl], src64[:, sl], psB[:, :]))
        k.wait("T", k.V, vv)

    def rope_pipeline(src64, sl, rot, cosb, sinb, psR, tbf):
        tv = k.T.inc(nc.tensor.matmul(psR[0:64, :], rot[:, :], src64[:, sl],
                                      start=True, stop=True))
        k.wait("V", k.T, tv)
        vv = k.V.inc(nc.vector.tensor_mul(tbf[:, :], src64[:, sl], cosb[:, :]))
        k.wait("V", k.V, vv)
        vv = k.V.inc(nc.vector.tensor_mul(src64[:, sl], psR[0:64, :], sinb[:, :]))
        k.wait("V", k.V, vv)
        vv = k.V.inc(nc.vector.tensor_add(src64[:, sl], src64[:, sl], tbf[:, :]))
        k.wait("T", k.V, vv)

    with ExitStack() as ls:
        wkv = ls.enter_context(nc.sbuf_tensor("a_wkv", [128, 8 * 512], BF16))
        wbuf = [ls.enter_context(nc.sbuf_tensor(f"a_wb{i}", [128, 1024], BF16))
                for i in range(2)]
        K64 = ls.enter_context(nc.sbuf_tensor("a_k", [64, 4 * T], BF16))
        Vt = ls.enter_context(nc.sbuf_tensor("a_v", [128, ntile * 260], BF16))
        Q64 = ls.enter_context(nc.sbuf_tensor("a_q", [64, 16 * 512], BF16))
        st2 = [ls.enter_context(nc.sbuf_tensor(f"a_st{i}", [128, 8 * 512], BF16))
               for i in range(2)]
        SQ64 = ls.enter_context(nc.sbuf_tensor("a_sq", [64, 512], F32))
        nq1 = ls.enter_context(nc.sbuf_tensor("a_nq1", [1, 512], F32))
        rdn = [ls.enter_context(nc.sbuf_tensor(f"a_rd{i}", [1, 512], F32))
               for i in range(2)]
        cosb = ls.enter_context(nc.sbuf_tensor("a_cos", [64, 512], BF16))
        sinb = ls.enter_context(nc.sbuf_tensor("a_sin", [64, 512], BF16))
        rot = ls.enter_context(nc.sbuf_tensor("a_rot", [64, 64], BF16))
        tbf = ls.enter_context(nc.sbuf_tensor("a_tbf", [64, 512], BF16))
        ybf = [ls.enter_context(nc.sbuf_tensor(f"a_ybf{i}", [64, 512], BF16))
               for i in range(2)]

        for kc in range(8):
            k.dma("G", wkv[:, kc * 512:(kc + 1) * 512],
                  d["wkvT"][kc * 128:(kc + 1) * 128, :])
        k.dma("G", rot[:, :], d["rot64"])
        ldg = k.G.val
        k.wait("T", k.G, ldg)
        k.wait("T", k.V, k.V.val)
        k.wait("T", k.S, k.S.val)
        k.wait("V", k.G, ldg)

        with ExitStack() as hs:
            psPd = hs.enter_context(nc.psum_tensor("atPd", [128, 1024], F32))
            psP = [hs.enter_context(nc.psum_tensor(f"atA{i}", [128, 512], F32))
                   for i in range(2)]
            ps1 = hs.enter_context(nc.psum_tensor("at1", [1, 512], F32))
            psB = hs.enter_context(nc.psum_tensor("atB", [64, 512], F32))
            psPV = [hs.enter_context(nc.psum_tensor(f"atPV{i}", [65, 512], F32))
                    for i in range(2)]

            for g in range(NKV):
                for t in range(ncol):
                    pb = (g * ncol + t) % 2
                    k.wait("T", k.S, k.S.val)
                    for kc in range(8):
                        ins = nc.tensor.matmul(
                            psP[pb][0:64, :],
                            wkv[:, kc * 512 + g * 64:kc * 512 + (g + 1) * 64],
                            Xb[:, kc * T + t * 512:kc * T + (t + 1) * 512],
                            start=(kc == 0), stop=(kc == 7))
                    tv = k.T.inc(ins)
                    k.wait("S", k.T, tv)
                    k.S.inc(nc.scalar.copy(
                        K64[:, g * T + t * 512:g * T + (t + 1) * 512],
                        psP[pb][0:64, :]))
            k.V.inc(nc.vector.memset(Vt[:, :], 1.0))
            k.wait("T", k.V, k.V.val)
            for tt in range(ntile):
                pb = tt % 2
                k.wait("T", k.V, k.V.val)
                k.wait("T", k.S, k.S.val)
                for kc in range(8):
                    ins = nc.tensor.matmul(
                        psP[pb][:, 0:256],
                        Xb[:, kc * T + tt * 128:kc * T + (tt + 1) * 128],
                        wkv[:, kc * 512 + 256:kc * 512 + 512],
                        start=(kc == 0), stop=(kc == 7))
                tv = k.T.inc(ins)
                k.wait("V", k.T, tv)
                for g in range(NKV):
                    k.V.inc(nc.vector.tensor_copy(
                        Vt[:, tt * 260 + g * 65:tt * 260 + g * 65 + 64],
                        psP[pb][:, g * 64:(g + 1) * 64]))

            for t in range(ncol):
                k.wait("G", k.V, k.V.val)
                k.wait("G", k.T, k.T.val)
                k.dma("G", cosb[:, :], d["cos64"][:, t * 512:(t + 1) * 512])
                k.dma("G", sinb[:, :], d["sin64"][:, t * 512:(t + 1) * 512])
                k.wait("V", k.G, k.G.val)
                k.wait("T", k.G, k.G.val)
                for g in range(NKV):
                    sl = slice(g * T + t * 512, g * T + (t + 1) * 512)
                    norm_pipeline(K64, sl, ps1, nq1, psB, SQ64)
                    rope_pipeline(K64, sl, rot, cosb, sinb, psP[g % 2], tbf)

            for tq in range(ncol):
                ts = slice(tq * 512, (tq + 1) * 512)
                k.wait("G", k.V, k.V.val)
                k.wait("G", k.T, k.T.val)
                k.dma("G", cosb[:, :], d["cos64"][:, ts])
                k.dma("G", sinb[:, :], d["sin64"][:, ts])
                k.wait("V", k.G, k.G.val)
                k.wait("T", k.G, k.G.val)
                for oc in range(NC8):
                    b = oc % 2
                    k.wait("G", k.T, k.T.val)
                    k.dma("G", wbuf[b][:, :], d["wqT_t"][oc], ctr=k.GW[b])
                    k.wait("T", k.GW[b], k.GW[b].val)
                    for par in range(2):
                        h = 2 * oc + par
                        pb = h % 2
                        k.wait("T", k.S, k.S.val)
                        for kc in range(8):
                            ins = nc.tensor.matmul(
                                psP[pb][0:64, :],
                                wbuf[b][:, kc * 128 + par * 64:
                                        kc * 128 + par * 64 + 64],
                                Xb[:, kc * T + tq * 512:kc * T + (tq + 1) * 512],
                                start=(kc == 0), stop=(kc == 7))
                        tv = k.T.inc(ins)
                        k.wait("S", k.T, tv)
                        k.S.inc(nc.scalar.copy(Q64[:, h * 512:(h + 1) * 512],
                                               psP[pb][0:64, :]))
                for h in range(NHEAD):
                    sl = slice(h * 512, (h + 1) * 512)
                    norm_pipeline(Q64, sl, ps1, nq1, psB, SQ64)
                    rope_pipeline(Q64, sl, rot, cosb, sinb, psP[h % 2], tbf)

                exp_evt = [0, 0]
                pv_evt = [0, 0]
                msk_evt = [0, 0]
                evac_evt = [0, 0]
                for h in range(NHEAD):
                    g = h // 4
                    hb = h % 2
                    stb = st2[hb]
                    subs = [(j, tq * 512 - 512 + j * 128) for j in range(8)
                            if tq * 512 - 512 + j * 128 >= 0]
                    for i0 in range(0, len(subs), 2):
                        pair = subs[i0:i0 + 2]
                        for jj, (j, tkb) in enumerate(pair):
                            ins = nc.tensor.matmul(
                                psPd[:, jj * 512:(jj + 1) * 512],
                                K64[:, g * T + tkb:g * T + tkb + 128],
                                Q64[:, h * 512:(h + 1) * 512],
                                start=True, stop=True)
                        tv = k.T.inc(ins)
                        k.wait("S", k.T, tv)
                        k.wait("S", k.T, pv_evt[hb])
                        w = 512 * len(pair)
                        j0 = pair[0][0]
                        ins = nc.scalar.activation(
                            stb[:, j0 * 512:j0 * 512 + w],
                            psPd[:, 0:w], AF.Exp, scale=k.cp(6))
                        sv = k.S.inc(ins)
                        exp_evt[hb] = sv
                        k.wait("T", k.S, sv)
                        k.wait("P", k.S, sv)
                        k.wait("P", k.T, pv_evt[hb])
                        for (j, tkb) in pair:
                            if j < 4:
                                ins = nc.gpsimd.affine_select(
                                    stb[:, j * 512:(j + 1) * 512],
                                    stb[:, j * 512:(j + 1) * 512],
                                    pattern=[[-1, 512]], compare_op=ALU.is_ge,
                                    fill=0.0, base=511 - tq * 512 + tkb,
                                    channel_multiplier=1)
                            else:
                                ins = nc.gpsimd.affine_select(
                                    stb[:, j * 512:(j + 1) * 512],
                                    stb[:, j * 512:(j + 1) * 512],
                                    pattern=[[1, 512]], compare_op=ALU.is_ge,
                                    fill=0.0, base=tq * 512 - tkb,
                                    channel_multiplier=-1)
                            msk_evt[hb] = k.P.inc(ins)
                    k.wait("T", k.P, msk_evt[hb])
                    k.wait("T", k.V, evac_evt[hb])
                    for i, (j, tkb) in enumerate(subs):
                        tt = tkb // 128
                        ins = nc.tensor.matmul(
                            psPV[hb][:, :],
                            Vt[:, tt * 260 + g * 65:tt * 260 + (g + 1) * 65],
                            stb[:, j * 512:(j + 1) * 512], start=(i == 0),
                            stop=(i == len(subs) - 1))
                    tv = k.T.inc(ins)
                    pv_evt[hb] = tv
                    k.wait("V", k.T, tv)
                    vv = k.V.inc(nc.vector.reciprocal(rdn[hb][0:1, :],
                                                      psPV[hb][64:65, :]))
                    k.wait("T", k.V, vv)
                    bv = k.T.inc(nc.tensor.matmul(psB[:, :], ones1[0:1, 0:64],
                                                  rdn[hb][0:1, :], start=True,
                                                  stop=True))
                    k.wait("S", k.T, tv)
                    k.wait("S", k.GX[hb], k.GX[hb].val)
                    sv = k.S.inc(nc.scalar.copy(ybf[hb][:, :],
                                                psPV[hb][0:64, :]))
                    k.wait("V", k.S, sv)
                    k.wait("V", k.T, bv)
                    k.wait("V", k.GX[hb], k.GX[hb].val)
                    vv = k.V.inc(nc.vector.tensor_mul(ybf[hb][:, :],
                                                      ybf[hb][:, :],
                                                      psB[0:64, :]))
                    evac_evt[hb] = vv
                    k.wait("G", k.V, vv)
                    k.dma("G", d["yf"][h * 64:(h + 1) * 64, ts], ybf[hb][:, :],
                          ctr=k.GX[hb])
        k.barrier()

    with ExitStack() as os_:
        ybin = [os_.enter_context(nc.sbuf_tensor(f"a_yb{i}", [64, 512], BF16))
                for i in range(2)]
        wp8 = os_.enter_context(nc.sbuf_tensor("a_wp8", [64, 8 * 2048], BF16))
        psO = os_.enter_context(nc.psum_tensor("a_psO", [128, 4096], F32))
        for oc in range(NC8):
            k.dma("G", wp8[:, oc * 2048:(oc + 1) * 2048], d["wp64"][oc])
        k.wait("T", k.G, k.G.val)
        ld_evt = [0, 0]
        for t in range(ncol):
            for h in range(NHEAD):
                b = h % 2
                k.wait("G", k.T, k.T.val)
                k.dma("G", ybin[b][:, :],
                      d["yf"][h * 64:(h + 1) * 64, t * 512:(t + 1) * 512],
                      ctr=k.GX[b])
                ld_evt[b] = k.GX[b].val
                k.wait("T", k.GX[b], ld_evt[b])
                if h == 0:
                    k.wait("T", k.V, k.V.val)
                for oc in range(NC8):
                    ins = nc.tensor.matmul(
                        psO[:, oc * 512:(oc + 1) * 512],
                        wp8[:, oc * 2048 + h * 128:oc * 2048 + (h + 1) * 128],
                        ybin[b][:, :],
                        start=(h == 0), stop=(h == NHEAD - 1))
                k.T.inc(ins)
            tv = k.T.val
            k.wait("V", k.T, tv)
            for oc in range(NC8):
                sl = slice(oc * T + t * 512, oc * T + (t + 1) * 512)
                k.V.inc(nc.vector.tensor_add(X[:, sl], X[:, sl],
                                             psO[:, oc * 512:(oc + 1) * 512]))
    k.barrier()



def emit_l1_ffn_w12(k, Xb, H, d, wbig, psP, T, tok0, Th):
    nc = k.nc
    s1 = k.ffn_scales["s1"]
    s2 = k.ffn_scales["s2"]

    def evac_w1(ob, t, ps_, tv):
        k.wait("S", k.T, tv)
        ins = nc.scalar.activation(H[:, ob * Th + t * 512:ob * Th + (t + 1) * 512],
                                   ps_[:, :], AF.Silu, scale=s1[:, ob:ob + 1])
        return (k.S, k.S.inc(ins))
    emit_mm_phase(k, d["w1T"], 32, D, Xb, T, wbig, psP, evac_w1, tok0=tok0,
                  n_tcol=Th // 512)

    def evac_w2(ob, t, ps_, tv):
        sl = slice(ob * Th + t * 512, ob * Th + (t + 1) * 512)
        k.wait("V", k.T, tv)
        k.wait("V", k.S, k.S.val)
        k.wait("V", k.V, k.V.val)
        ins = nc.vector.tensor_scalar_mul(k.ffn_sq[:, 0:512], ps_[:, :],
                                          s2[:, ob:ob + 1])
        vv = k.V.inc(ins)
        k.wait("V", k.V, vv)
        ins = nc.vector.tensor_mul(H[:, sl], H[:, sl], k.ffn_sq[:, 0:512])
        return (k.V, k.V.inc(ins))
    emit_mm_phase(k, d["w2T"], 32, D, Xb, T, wbig, psP, evac_w2, tok0=tok0,
                  n_tcol=Th // 512)


def emit_l1_ffn_h(k, Hin, Hout, d_w, scale_col, wbig, psP, Th):
    nc = k.nc

    def evac(ob, t, ps_, tv):
        k.wait("S", k.T, tv)
        k.wait("S", k.V, k.V.val)
        ins = nc.scalar.activation(Hout[:, ob * Th + t * 512:ob * Th + (t + 1) * 512],
                                   ps_[:, :], AF.Gelu, scale=scale_col[:, ob:ob + 1])
        return (k.S, k.S.inc(ins))
    emit_mm_phase(k, d_w, 32, DH, Hin, Th, wbig, psP, evac)


def emit_l1_ffn_w3(k, Xn, H, d, wbig, psP, T, tok0, Th):
    nc = k.nc
    s3 = k.ffn_scales["s3"]

    def evac(ob, t, ps_, tv):
        sl = slice(ob * T + tok0 + t * 512, ob * T + tok0 + (t + 1) * 512)
        k.wait("V", k.T, tv)
        k.wait("V", k.S, k.S.val)
        k.wait("V", k.V, k.V.val)
        k.wait("V", k.Y, k.Y.val)
        ins = nc.vector.scalar_tensor_tensor(Xn[:, sl], ps_[:, :],
                                             s3[:, ob:ob + 1], Xn[:, sl],
                                             ALU.mult, ALU.add)
        return (k.V, k.V.inc(ins))
    emit_mm_phase(k, d["w3T"], NC8, DH, H, Th, wbig, psP, evac)


def declare_common(nc):
    d = {}
    d["scratch1"] = nc.dram_tensor("scratch1", [32, 8 * 2048], F32,
                                   kind="Internal").ap()
    return d


def build_launchA(T):
    nc = bass.Bass("TRN2", target_bir_lowering=False, debug=False)
    es = ExitStack()
    d = declare_common(nc)
    X_ = nc.dram_tensor
    d["x_in"] = X_("x_in", [D, T], F32, kind="ExternalInput").ap()
    d["w_in"] = X_("w_in", [D, 1072], F32, kind="ExternalInput").ap()
    d["bz"] = X_("bz", [128, 8], F32, kind="ExternalInput").ap()
    d["bbcd"] = X_("bbcd", [16, 3], F32, kind="ExternalInput").ap()
    d["alog"] = X_("alog", [16, 1], F32, kind="ExternalInput").ap()
    d["w_out"] = X_("w_out", [D, D], F32, kind="ExternalInput").ap()
    d["bout"] = X_("bout", [128, 8], F32, kind="ExternalInput").ap()
    d["rwT"] = X_("rwT", [D, 8], F32, kind="ExternalInput").ap()
    d["cpool"] = X_("cpool", [128, 8], F32, kind="ExternalInput").ap()
    d["x2_out"] = X_("x2_out", [D, T], F32, kind="ExternalOutput").ap()
    d["route_out"] = X_("route_out", [24, T], F32, kind="ExternalOutput").ap()
    d["counts_out"] = X_("counts_out", [8, 1], F32, kind="ExternalOutput").ap()
    k = Kb(nc, es)
    X = es.enter_context(nc.sbuf_tensor("X", [128, 8 * T], F32))
    Xb = es.enter_context(nc.sbuf_tensor("Xb", [128, 8 * T], BF16))
    ones128, ones1, idf = emit_consts(k, d)
    for c in range(NC8):
        k.dma("Y", X[:, c * T:(c + 1) * T],
                                  d["x_in"][c * 128:(c + 1) * 128, :])
    k.wait("V", k.Y, k.Y.val)
    for c in range(NC8):
        k.V.inc(nc.vector.tensor_copy(Xb[:, c * T:(c + 1) * T],
                                      X[:, c * T:(c + 1) * T]))
    emit_l0_attn(k, X, Xb, d, ones128, ones1, T)
    emit_ln(k, "0", X, Xb, ones128, ones1, T)
    route_sb = {
        "selE": es.enter_context(nc.sbuf_tensor("selE", [8, T], F32)),
        "wdE": es.enter_context(nc.sbuf_tensor("wdE", [8, T], F32)),
        "csum": es.enter_context(nc.sbuf_tensor("csum", [8, T], F32)),
        "counts": es.enter_context(nc.sbuf_tensor("counts", [8, 1], F32)),
    }
    emit_routing(k, X, Xb, d, ones128, ones1, idf, T, route_sb)
    for c in range(NC8):
        k.dma("Y", d["x2_out"][c * 128:(c + 1) * 128, :],
                                  X[:, c * T:(c + 1) * T])
    k.dma("Y", d["route_out"][0:8, :],
                              route_sb["selE"][:, :])
    k.dma("Y", d["route_out"][8:16, :],
                              route_sb["wdE"][:, :])
    k.dma("Y", d["route_out"][16:24, :],
                              route_sb["csum"][:, :])
    k.dma("Y", d["counts_out"][:, :],
                              route_sb["counts"][:, :])
    k.wait("Y", k.Y, k.Y.val)
    es.close()
    return nc


def build_launchB(T):
    nc = bass.Bass("TRN2", target_bir_lowering=False, debug=False)
    es = ExitStack()
    d = declare_common(nc)
    X_ = nc.dram_tensor
    d["x2_in"] = X_("x2_in", [D, T], F32, kind="ExternalInput").ap()
    d["route_in"] = X_("route_in", [24, T], F32, kind="ExternalInput").ap()
    d["counts_all"] = X_("counts_all", [8, 8], F32, kind="ExternalInput").ap()
    d["prevmask"] = X_("prevmask", [8, 1], F32, kind="ExternalInput").ap()
    d["rg"] = [X_(f"rg{e}", [IR // 128, 128, D], F32, kind="ExternalInput").ap()
               for e in range(NEXP)]
    d["ru"] = [X_(f"ru{e}", [IR // 128, 128, D], F32, kind="ExternalInput").ap()
               for e in range(NEXP)]
    d["sg"] = X_("sg", [ISH // 128, 128, D], F32, kind="ExternalInput").ap()
    d["su"] = X_("su", [ISH // 128, 128, D], F32, kind="ExternalInput").ap()
    d["sd_h"] = [X_(f"sd{h}", [NC8, 128, IR], F32, kind="ExternalInput").ap()
                 for h in range(2)]
    d["rdk"] = [X_(f"rdk{e}", [32, 128, 1024], F32, kind="ExternalInput").ap()
                for e in range(NEXP)]
    d["x2tm"] = X_("x2tm", [T, D], BF16, kind="Internal").ap()
    d["routedtm"] = X_("routedtm", [T, D], F32, kind="Internal").ap()
    d["paird"] = X_("paird", [NEXP * 1024, 2], F32, kind="Internal").ap()
    d["wqT_t"] = X_("wqT_t", [NC8, 128, D], F32, kind="ExternalInput").ap()
    d["wkvT"] = X_("wkvT", [D, 512], F32, kind="ExternalInput").ap()
    d["wp64"] = X_("wp64", [NC8, 64, 16 * 128], F32, kind="ExternalInput").ap()
    d["cos64"] = X_("cos64", [64, T], F32, kind="ExternalInput").ap()
    d["sin64"] = X_("sin64", [64, T], F32, kind="ExternalInput").ap()
    d["rot64"] = X_("rot64", [64, 64], F32, kind="ExternalInput").ap()
    d["yf"] = X_("yf", [D, T], BF16, kind="Internal").ap()
    d["w1T"] = X_("w1T", [32, 128, D], F32, kind="ExternalInput").ap()
    d["w2T"] = X_("w2T", [32, 128, D], F32, kind="ExternalInput").ap()
    d["h1T"] = X_("h1T", [32, 128, DH], F32, kind="ExternalInput").ap()
    d["h2T"] = X_("h2T", [32, 128, DH], F32, kind="ExternalInput").ap()
    d["w3T"] = X_("w3T", [8, 128, DH], F32, kind="ExternalInput").ap()
    d["s1"] = X_("s1", [128, 32], F32, kind="ExternalInput").ap()
    d["s2"] = X_("s2", [128, 32], F32, kind="ExternalInput").ap()
    d["h1s"] = X_("h1s", [128, 32], F32, kind="ExternalInput").ap()
    d["h2s"] = X_("h2s", [128, 32], F32, kind="ExternalInput").ap()
    d["s3"] = X_("s3", [128, 8], F32, kind="ExternalInput").ap()
    d["cpool"] = X_("cpool", [128, 8], F32, kind="ExternalInput").ap()
    d["xpark"] = X_("xpark", [D, T], F32, kind="Internal").ap()
    d["out"] = X_("out", [D, T], F32, kind="ExternalOutput").ap()

    k = Kb(nc, es)
    Xb = es.enter_context(nc.sbuf_tensor("Xb", [128, 8 * T], BF16))
    ones128, ones1, idf = emit_consts(k, d)
    k.ffn_scales = {}
    for name in ("s1", "s2", "h1s", "h2s", "s3"):
        t_ = es.enter_context(nc.sbuf_tensor(f"sc_{name}",
                                             [128, 8 if name == "s3" else 32],
                                             F32))
        k.dma("Y", t_[:, :], d[name])
        k.ffn_scales[name] = t_

    with ExitStack() as sx:
        X = sx.enter_context(nc.sbuf_tensor("X", [128, 8 * T], F32))
        for c in range(NC8):
            k.dma("Y", X[:, c * T:(c + 1) * T],
                                      d["x2_in"][c * 128:(c + 1) * 128, :])
        with ExitStack() as sw:
            weff_sb = sw.enter_context(nc.sbuf_tensor("weff", [8, T], F32))
            with ExitStack() as sr:
                route_sb = {
                    "selE": sr.enter_context(nc.sbuf_tensor("selE", [8, T], F32)),
                    "wdE": sr.enter_context(nc.sbuf_tensor("wdE", [8, T], F32)),
                    "csum": sr.enter_context(nc.sbuf_tensor("csum", [8, T], F32)),
                    "weff": weff_sb,
                }
                for i, key in enumerate(("selE", "wdE", "csum")):
                    k.dma("Y", 
                        route_sb[key][:, :],
                        d["route_in"][i * 8:(i + 1) * 8, :])
                k.wait("V", k.Y, k.Y.val)
                for c in range(NC8):
                    k.V.inc(nc.vector.tensor_copy(Xb[:, c * T:(c + 1) * T],
                                                  X[:, c * T:(c + 1) * T]))
                emit_weff(k, d, route_sb, T)
            with ExitStack() as sm:
                H = sm.enter_context(nc.sbuf_tensor("Hh", [128, 16 * T], BF16))
                emit_moe(k, X, Xb, H, d, {"weff": weff_sb}, ones128, ones1, T)
        emit_ln(k, "1", X, Xb, ones128, ones1, T)
        emit_l1_attn(k, X, Xb, d, ones128, ones1, T)
        emit_ln(k, "2", X, Xb, ones128, ones1, T)
        k.barrier()
        for c in range(NC8):
            k.dma("Y", d["xpark"][c * 128:(c + 1) * 128, :],
                                      X[:, c * T:(c + 1) * T])
        k.wait("Y", k.Y, k.Y.val)
        k.barrier()
    with ExitStack() as sf:
        Th = T // 2
        wbig = [sf.enter_context(nc.sbuf_tensor(f"f_wb{i}", [128, DH], BF16))
                for i in range(2)]
        k.ffn_sq = sf.enter_context(nc.sbuf_tensor("f_sq", [128, 512], F32))
        for halft in range(2):
            tok0 = halft * Th
            with ExitStack() as sh_:
                H = sh_.enter_context(nc.sbuf_tensor(f"Hf{halft}",
                                                     [128, 32 * Th], BF16))
                with ExitStack() as s3_:
                    H2 = s3_.enter_context(nc.sbuf_tensor(f"Hf2{halft}",
                                                          [128, 32 * Th], BF16))
                    with ExitStack() as ps:
                        psP = [ps.enter_context(
                            nc.psum_tensor(f"fA{halft}{i}", [128, 512], F32))
                            for i in range(2)]
                        emit_l1_ffn_w12(k, Xb, H, d, wbig, psP, T, tok0, Th)
                        k.barrier()
                        emit_l1_ffn_h(k, H, H2, d["h1T"], k.ffn_scales["h1s"],
                                      wbig, psP, Th)
                        k.barrier()
                        emit_l1_ffn_h(k, H2, H, d["h2T"], k.ffn_scales["h2s"],
                                      wbig, psP, Th)
                        k.barrier()
                Xnh = sh_.enter_context(nc.sbuf_tensor(f"Xnh{halft}",
                                                       [128, 8 * Th], F32))
                for c in range(NC8):
                    k.dma("Y", Xnh[:, c * Th:(c + 1) * Th],
                          d["xpark"][c * 128:(c + 1) * 128, tok0:tok0 + Th])
                k.wait("V", k.Y, k.Y.val)
                with ExitStack() as ps:
                    psP = [ps.enter_context(
                        nc.psum_tensor(f"fC{halft}{i}", [128, 512], F32))
                        for i in range(2)]
                    emit_l1_ffn_w3(k, Xnh, H, d, wbig, psP, Th, 0, Th)
                    k.barrier()
                for c in range(NC8):
                    k.dma("Y", d["xpark"][c * 128:(c + 1) * 128, tok0:tok0 + Th],
                          Xnh[:, c * Th:(c + 1) * Th])
                k.wait("Y", k.Y, k.Y.val)
                k.barrier()
        Xn = sf.enter_context(nc.sbuf_tensor("Xn", [128, 8 * T], F32))
        for c in range(NC8):
            k.dma("Y", Xn[:, c * T:(c + 1) * T],
                  d["xpark"][c * 128:(c + 1) * 128, :])
        k.wait("V", k.Y, k.Y.val)
        k.barrier()
        emit_ln(k, "3", Xn, Xb, ones128, ones1, T, out_bf=False)
        for c in range(NC8):
            k.dma("Y", d["out"][c * 128:(c + 1) * 128, :],
                  Xn[:, c * T:(c + 1) * T])
        k.wait("Y", k.Y, k.Y.val)
    es.close()
    return nc


def build_fused(T):
    nc = bass.Bass("TRN2", target_bir_lowering=False, debug=False)
    es = ExitStack()
    ntile = T // 128
    d = declare_common(nc)
    X_ = nc.dram_tensor
    d["x_in"] = X_("x_in", [D, T], F32, kind="ExternalInput").ap()
    d["x_all"] = X_("x_all", [8 * D, T], F32, kind="ExternalInput").ap()
    d["w_in"] = X_("w_in", [D, 1072], F32, kind="ExternalInput").ap()
    d["bz"] = X_("bz", [128, 8], F32, kind="ExternalInput").ap()
    d["bbcd"] = X_("bbcd", [16, 3], F32, kind="ExternalInput").ap()
    d["alog"] = X_("alog", [16, 1], F32, kind="ExternalInput").ap()
    d["w_out"] = X_("w_out", [D, D], F32, kind="ExternalInput").ap()
    d["wo_t"] = X_("wo_t", [NC8, 128, D], F32, kind="ExternalInput").ap()
    d["bout"] = X_("bout", [128, 8], F32, kind="ExternalInput").ap()
    d["rwT"] = X_("rwT", [D, 8], F32, kind="ExternalInput").ap()
    d["im8"] = X_("im8", [8, 8], F32, kind="ExternalInput").ap()
    d["im128"] = X_("im128", [128, 8], F32, kind="ExternalInput").ap()
    d["prevmask"] = X_("prevmask", [8, 1], F32, kind="ExternalInput").ap()
    d["cpool"] = X_("cpool", [128, 8], F32, kind="ExternalInput").ap()
    d["rg"] = [X_(f"rg{e}", [IR // 128, 128, D], F32, kind="ExternalInput").ap()
               for e in range(NEXP)]
    d["ru"] = [X_(f"ru{e}", [IR // 128, 128, D], F32, kind="ExternalInput").ap()
               for e in range(NEXP)]
    d["sg"] = X_("sg", [ISH // 128, 128, D], F32, kind="ExternalInput").ap()
    d["su"] = X_("su", [ISH // 128, 128, D], F32, kind="ExternalInput").ap()
    d["sd_h"] = [X_(f"sd{h}", [NC8, 128, IR], F32, kind="ExternalInput").ap()
                 for h in range(2)]
    d["rdk"] = [X_(f"rdk{e}", [32, 128, 1024], F32, kind="ExternalInput").ap()
                for e in range(NEXP)]
    d["x2tm"] = X_("x2tm", [T, D], BF16, kind="Internal").ap()
    d["routedtm"] = X_("routedtm", [T, D], F32, kind="Internal").ap()
    d["paird"] = X_("paird", [NEXP * 1024, 2], F32, kind="Internal").ap()
    d["wqT_t"] = X_("wqT_t", [NC8, 128, D], F32, kind="ExternalInput").ap()
    d["wkvT"] = X_("wkvT", [D, 512], F32, kind="ExternalInput").ap()
    d["wp64"] = X_("wp64", [NC8, 64, 16 * 128], F32, kind="ExternalInput").ap()
    d["cos64"] = X_("cos64", [64, T], F32, kind="ExternalInput").ap()
    d["sin64"] = X_("sin64", [64, T], F32, kind="ExternalInput").ap()
    d["rot64"] = X_("rot64", [64, 64], F32, kind="ExternalInput").ap()
    d["yf"] = X_("yf", [D, T], BF16, kind="Internal").ap()
    d["w1T"] = X_("w1T", [32, 128, D], F32, kind="ExternalInput").ap()
    d["w2T"] = X_("w2T", [32, 128, D], F32, kind="ExternalInput").ap()
    d["h1T"] = X_("h1T", [32, 128, DH], F32, kind="ExternalInput").ap()
    d["h2T"] = X_("h2T", [32, 128, DH], F32, kind="ExternalInput").ap()
    d["w3T"] = X_("w3T", [8, 128, DH], F32, kind="ExternalInput").ap()
    d["s1"] = X_("s1", [128, 32], F32, kind="ExternalInput").ap()
    d["s2"] = X_("s2", [128, 32], F32, kind="ExternalInput").ap()
    d["h1s"] = X_("h1s", [128, 32], F32, kind="ExternalInput").ap()
    d["h2s"] = X_("h2s", [128, 32], F32, kind="ExternalInput").ap()
    d["s3"] = X_("s3", [128, 8], F32, kind="ExternalInput").ap()
    d["xpark"] = X_("xpark", [D, T], F32, kind="Internal").ap()
    d["out"] = X_("out", [D, T], F32, kind="ExternalOutput").ap()

    k = Kb(nc, es)
    ones128, ones1, idf = emit_consts(k, d)
    k.ffn_scales = {}
    for name in ("s1", "s2", "h1s", "h2s", "s3"):
        t_ = es.enter_context(nc.sbuf_tensor(f"sc_{name}",
                                             [128, 8 if name == "s3" else 32],
                                             F32))
        k.dma("Y", t_[:, :], d[name])
        k.ffn_scales[name] = t_
    sart = ExitStack()
    art = {
        "selE": sart.enter_context(nc.sbuf_tensor("art_se", [8, T], F32,
                                                  side="right")),
        "msk1": sart.enter_context(nc.sbuf_tensor("art_k1", [128, 8 * ntile],
                                                  F32, side="right")),
        "msk2": sart.enter_context(nc.sbuf_tensor("art_k2", [128, 8 * ntile],
                                                  F32, side="right")),
        "dp": sart.enter_context(nc.sbuf_tensor("art_dp", [128, ntile], F32,
                                                side="right")),
        "off": sart.enter_context(nc.sbuf_tensor("art_off", [8, 1], F32,
                                                 side="right")),
    }
    rstd = sart.enter_context(nc.sbuf_tensor("rstd0", [1, T], F32,
                                             side="right"))
    rankI = es.enter_context(nc.sbuf_tensor("rankI", [128, 8 * (T // 128)],
                                            I32))
    weffT = es.enter_context(nc.sbuf_tensor("weffT", [128, 8 * (T // 128)],
                                            F32))

    emit_phase0(k, d, ones128, ones1, idf, T, art)

    Xb = es.enter_context(nc.sbuf_tensor("Xb", [128, 8 * T], BF16))
    with ExitStack() as sx:
        X = sx.enter_context(nc.sbuf_tensor("X", [128, 8 * T], F32))
        for c in range(NC8):
            k.dma("Y", X[:, c * T:(c + 1) * T],
                  d["x_in"][c * 128:(c + 1) * 128, :])
        k.wait("V", k.Y, k.Y.val)
        for c in range(NC8):
            k.V.inc(nc.vector.tensor_copy(Xb[:, c * T:(c + 1) * T],
                                          X[:, c * T:(c + 1) * T]))
        emit_l0_attn(k, X, Xb, d, ones128, ones1, T)
        emit_ln(k, "0", X, Xb, ones128, ones1, T, rstd_out=rstd)
        with ExitStack() as sw:
            weff_sb = sw.enter_context(nc.sbuf_tensor("weff", [8, T], F32))
            emit_routing2(k, X, Xb, d, ones128, ones1, idf, T, art, rstd,
                          weff_sb, rankI=rankI, weffT=weffT)
            sart.close()
        k.barrier()
        for c in range(NC8):
            k.dma("Y", d["xpark"][c * 128:(c + 1) * 128, :],
                  X[:, c * T:(c + 1) * T])
        k.wait("Y", k.Y, k.Y.val)
        k.barrier()
    emit_moe_routed(k, Xb, d, ones128, ones1, idf, rankI, weffT, T)
    with ExitStack() as sx:
        X = sx.enter_context(nc.sbuf_tensor("X2", [128, 8 * T], F32))
        for c in range(NC8):
            k.dma("Y", X[:, c * T:(c + 1) * T],
                  d["xpark"][c * 128:(c + 1) * 128, :])
        k.wait("V", k.Y, k.Y.val)
        k.barrier()
        with ExitStack() as sm:
            H = sm.enter_context(nc.sbuf_tensor("Hh", [128, 16 * T], BF16))
            emit_moe_shared(k, X, Xb, H, d, ones128, ones1, idf, T)
        emit_ln(k, "1", X, Xb, ones128, ones1, T)
        emit_l1_attn(k, X, Xb, d, ones128, ones1, T)
        emit_ln(k, "2", X, Xb, ones128, ones1, T)
        k.barrier()
        for c in range(NC8):
            k.dma("Y", d["xpark"][c * 128:(c + 1) * 128, :],
                  X[:, c * T:(c + 1) * T])
        k.wait("Y", k.Y, k.Y.val)
        k.barrier()
    with ExitStack() as sf:
        Th = T // 2
        wbig = [sf.enter_context(nc.sbuf_tensor(f"f_wb{i}", [128, DH], BF16))
                for i in range(2)]
        k.ffn_sq = sf.enter_context(nc.sbuf_tensor("f_sq", [128, 512], F32))
        for halft in range(2):
            tok0 = halft * Th
            with ExitStack() as sh_:
                H = sh_.enter_context(nc.sbuf_tensor(f"Hf{halft}",
                                                     [128, 32 * Th], BF16))
                with ExitStack() as s3_:
                    H2 = s3_.enter_context(nc.sbuf_tensor(f"Hf2{halft}",
                                                          [128, 32 * Th], BF16))
                    with ExitStack() as ps:
                        psP = [ps.enter_context(
                            nc.psum_tensor(f"fA{halft}{i}", [128, 512], F32))
                            for i in range(2)]
                        emit_l1_ffn_w12(k, Xb, H, d, wbig, psP, T, tok0, Th)
                        k.barrier()
                        emit_l1_ffn_h(k, H, H2, d["h1T"], k.ffn_scales["h1s"],
                                      wbig, psP, Th)
                        k.barrier()
                        emit_l1_ffn_h(k, H2, H, d["h2T"], k.ffn_scales["h2s"],
                                      wbig, psP, Th)
                        k.barrier()
                Xnh = sh_.enter_context(nc.sbuf_tensor(f"Xnh{halft}",
                                                       [128, 8 * Th], F32))
                for c in range(NC8):
                    k.dma("Y", Xnh[:, c * Th:(c + 1) * Th],
                          d["xpark"][c * 128:(c + 1) * 128, tok0:tok0 + Th])
                k.wait("V", k.Y, k.Y.val)
                with ExitStack() as ps:
                    psP = [ps.enter_context(
                        nc.psum_tensor(f"fC{halft}{i}", [128, 512], F32))
                        for i in range(2)]
                    emit_l1_ffn_w3(k, Xnh, H, d, wbig, psP, Th, 0, Th)
                    k.barrier()
                for c in range(NC8):
                    k.dma("Y", d["xpark"][c * 128:(c + 1) * 128, tok0:tok0 + Th],
                          Xnh[:, c * Th:(c + 1) * Th])
                k.wait("Y", k.Y, k.Y.val)
                k.barrier()
        Xn = sf.enter_context(nc.sbuf_tensor("Xn", [128, 8 * T], F32))
        for c in range(NC8):
            k.dma("Y", Xn[:, c * T:(c + 1) * T],
                  d["xpark"][c * 128:(c + 1) * 128, :])
        k.wait("V", k.Y, k.Y.val)
        k.barrier()
        emit_ln(k, "3", Xn, Xb, ones128, ones1, T, out_bf=False)
        for c in range(NC8):
            k.dma("Y", d["out"][c * 128:(c + 1) * 128, :],
                  Xn[:, c * T:(c + 1) * T])
        k.wait("Y", k.Y, k.Y.val)
    es.close()
    return nc


def _tile_oc(wKN, K, N):
    a = np.ascontiguousarray(wKN, np.float32).reshape(K // 128, 128, N // 128, 128)
    return np.ascontiguousarray(a.transpose(2, 1, 0, 3).reshape(N // 128, 128, K))


def _rs(v, n):
    return np.ascontiguousarray(np.asarray(v, np.float32).reshape(n, 128).T)


def _prep(inp):
    T = TFULL
    f = np.float32
    p = {}
    x = np.asarray(inp["L0_attn_x"], f)
    p["xs"] = [np.ascontiguousarray(x[b].T) for b in range(8)]
    w_in = np.asarray(inp["L0_attn_W_in"], f)
    p["w_in"] = np.ascontiguousarray(w_in[:, 1024:])
    b_in = np.asarray(inp["L0_attn_b_in"], f)
    p["bz"] = _rs(b_in[1024:2048], 8)
    p["bbcd"] = np.ascontiguousarray(
        np.stack([b_in[2048:2064], b_in[2064:2080], b_in[2080:2096]], axis=1))
    p["alog"] = np.asarray(inp["L0_attn_A_log"], f).reshape(16, 1).copy()
    p["w_out"] = np.ascontiguousarray(np.asarray(inp["L0_attn_W_out"], f))
    p["wo_t"] = _tile_oc(p["w_out"], D, D)
    p["bout"] = _rs(np.asarray(inp["L0_attn_b_out"], f), 8)
    p["rwT"] = np.ascontiguousarray(np.asarray(inp["L0_ffn_router_w"], f).T)
    rg = np.asarray(inp["L0_ffn_rg"], f)
    ru = np.asarray(inp["L0_ffn_ru"], f)
    rd = np.asarray(inp["L0_ffn_rd"], f)
    p["rg"] = [_tile_oc(rg[e], D, IR) for e in range(NEXP)]
    p["ru"] = [_tile_oc(ru[e], D, IR) for e in range(NEXP)]
    p["rd"] = [_tile_oc(rd[e], IR, D) for e in range(NEXP)]
    p["rdk"] = [np.ascontiguousarray(rd[e].reshape(32, 128, 1024))
                for e in range(NEXP)]
    p["sg"] = _tile_oc(np.asarray(inp["L0_ffn_sg"], f), D, ISH)
    p["su"] = _tile_oc(np.asarray(inp["L0_ffn_su"], f), D, ISH)
    sd = np.asarray(inp["L0_ffn_sd"], f)
    p["sd_h"] = [_tile_oc(sd[h * IR:(h + 1) * IR, :], IR, D) for h in range(2)]
    p["wqT_t"] = _tile_oc(np.asarray(inp["L1_attn_Wq"], f).T.copy(), D, D)
    p["wkvT"] = np.ascontiguousarray(np.asarray(inp["L1_attn_Wkv"], f).T)
    wpt = np.asarray(inp["L1_attn_Wproj"], f).T.copy()
    p["wp64"] = np.ascontiguousarray(
        wpt.reshape(16, 64, 8, 128).transpose(2, 1, 0, 3).reshape(8, 64, 2048))
    p["w1T"] = _tile_oc(np.asarray(inp["L1_ffn_w1_w"], f).T.copy(), D, DH)
    p["w2T"] = _tile_oc(np.asarray(inp["L1_ffn_w2_w"], f).T.copy(), D, DH)
    p["h1T"] = _tile_oc(np.asarray(inp["L1_ffn_h1_w"], f).T.copy(), DH, DH)
    p["h2T"] = _tile_oc(np.asarray(inp["L1_ffn_h2_w"], f).T.copy(), DH, DH)
    p["w3T"] = _tile_oc(np.asarray(inp["L1_ffn_w3_w"], f).T.copy(), DH, D)
    p["s1"] = _rs(inp["L1_ffn_w1_s"], 32)
    p["s2"] = _rs(inp["L1_ffn_w2_s"], 32)
    p["h1s"] = _rs(inp["L1_ffn_h1_s"], 32)
    p["h2s"] = _rs(inp["L1_ffn_h2_s"], 32)
    p["s3"] = _rs(inp["L1_ffn_w3_s"], 8)
    theta = 1.0 / (10000.0 ** (np.arange(0, HDIM, 2, dtype=np.float64) / HDIM))
    pos = np.arange(T, dtype=np.float64)
    fr = np.outer(pos, theta)
    cos = np.concatenate([np.cos(fr), np.cos(fr)], axis=1).astype(f)
    sin = np.concatenate([np.sin(fr), np.sin(fr)], axis=1).astype(f)
    p["cos64"] = np.ascontiguousarray(cos.T)
    p["sin64"] = np.ascontiguousarray(sin.T)
    R = np.zeros((HDIM, HDIM), f)
    for i in range(32):
        R[i, 32 + i] = -1.0
        R[32 + i, i] = 1.0
    p["rot64"] = np.ascontiguousarray(R.T)
    cp = np.zeros((128, 8), f)
    cp[:, 0] = 1e-5
    cp[:, 1] = -0.5
    cp[:, 2] = F32EPS
    cp[:, 3] = 1.0 / D
    cp[:, 4] = -1.0
    cp[:, 5] = 1.0
    cp[:, 6] = 0.125
    cp[:, 7] = 0.5
    p["cpool"] = cp
    p["prevmask"] = [np.ascontiguousarray(
        (np.arange(8) < c).astype(f).reshape(8, 1)) for c in range(8)]
    p["x_all"] = np.ascontiguousarray(np.concatenate(p["xs"], axis=0))
    im8 = []
    im128 = []
    for c in range(8):
        a = np.zeros((8, 8), f)
        a[:, c] = 1.0
        im8.append(a)
        b = np.zeros((128, 8), f)
        b[:, c] = 1.0
        im128.append(b)
    p["im8"] = im8
    p["im128"] = im128
    return p


def kernel(**inputs):
    p = _prep(inputs)
    T = TFULL
    nc = build_fused(T)
    in_maps = []
    for c in range(8):
        m = {"x_in": p["xs"][c], "x_all": p["x_all"], "w_in": p["w_in"],
             "bz": p["bz"], "bbcd": p["bbcd"], "alog": p["alog"],
             "w_out": p["w_out"], "wo_t": p["wo_t"], "bout": p["bout"],
             "rwT": p["rwT"],
             "im8": p["im8"][c], "im128": p["im128"][c],
             "prevmask": p["prevmask"][c], "cpool": p["cpool"],
             "sg": p["sg"], "su": p["su"], "wqT_t": p["wqT_t"],
             "wkvT": p["wkvT"], "wp64": p["wp64"], "cos64": p["cos64"],
             "sin64": p["sin64"], "rot64": p["rot64"],
             "w1T": p["w1T"], "w2T": p["w2T"], "h1T": p["h1T"],
             "h2T": p["h2T"], "w3T": p["w3T"], "s1": p["s1"], "s2": p["s2"],
             "h1s": p["h1s"], "h2s": p["h2s"], "s3": p["s3"]}
        for e in range(NEXP):
            m[f"rg{e}"] = p["rg"][e]
            m[f"ru{e}"] = p["ru"][e]
            m[f"rdk{e}"] = p["rdk"][e]
        for h in range(2):
            m[f"sd{h}"] = p["sd_h"][h]
        in_maps.append(m)
    res = run_bass_kernel_spmd(nc, in_maps, core_ids=list(range(8)))
    out = np.stack([np.ascontiguousarray(res.results[c]["out"].T)
                    for c in range(8)], axis=0)
    return out


# version 10
# speedup vs baseline: 1.0289x; 1.0149x over previous
from contextlib import ExitStack

import numpy as np
import concourse.bass as bass
import concourse.mybir as mybir
from concourse.bass_utils import run_bass_kernel_spmd

F32 = mybir.dt.float32
BF16 = mybir.dt.bfloat16
I32 = mybir.dt.int32
AF = mybir.ActivationFunctionType
ALU = mybir.AluOpType
AX = mybir.AxisListType

D = 1024
NC8 = 8
TFULL = 2048
NEXP = 8
CAP = 2560.0
NHEAD = 16
NKV = 4
HDIM = 64
IR = 4096
ISH = 8192
DH = 4096
F32EPS = float(np.finfo(np.float32).eps)


class Ctr:
    def __init__(self, nc, name, es):
        self.sem = es.enter_context(nc.semaphore(name))
        self.val = 0
        self.name = name

    def inc(self, instr, n=1):
        instr.then_inc(self.sem, n)
        self.val += n
        return self.val


class Kb:
    def __init__(self, nc, es):
        self.nc = nc
        self.es = es
        self.T = Ctr(nc, "cT", es)
        self.V = Ctr(nc, "cV", es)
        self.S = Ctr(nc, "cS", es)
        self.P = Ctr(nc, "cP", es)
        self.Y = Ctr(nc, "dY", es)
        self.G = Ctr(nc, "dG", es)
        self.GW = [Ctr(nc, "dGW0", es), Ctr(nc, "dGW1", es)]
        self.GX = [Ctr(nc, "dGX0", es), Ctr(nc, "dGX1", es)]
        self.YW = [Ctr(nc, "dYW0", es), Ctr(nc, "dYW1", es)]
        self.seen = {}
        self.ctrs = (self.T, self.V, self.S, self.P, self.Y, self.G,
                     self.GW[0], self.GW[1], self.GX[0], self.GX[1],
                     self.YW[0], self.YW[1])

    def eng(self, which):
        return {"T": self.nc.tensor, "V": self.nc.vector, "S": self.nc.scalar,
                "P": self.nc.gpsimd, "Y": self.nc.sync, "G": self.nc.gpsimd}[which]

    def wait(self, ek, ctr, val=None):
        if val is None:
            val = ctr.val
        if val <= 0:
            return
        key = (ek, ctr.name)
        if self.seen.get(key, -1) >= val:
            return
        self.eng(ek).wait_ge(ctr.sem, val)
        self.seen[key] = val

    def cp(self, i, p=128):
        return self.cpool[0:p, i:i + 1]

    def dma(self, qkey, out, in_, ctr=None, **kw):
        ctr = ctr or (self.Y if qkey == "Y" else self.G)
        self.wait(qkey, ctr, ctr.val)
        ins = self.eng(qkey).dma_start(out=out, in_=in_, **kw)
        return ctr.inc(ins, 16)

    def barrier(self, engines="TVSPYG"):
        for e in engines:
            for c in self.ctrs:
                self.wait(e, c)


def emit_consts(k, d):
    nc, es = k.nc, k.es
    cpool = es.enter_context(nc.sbuf_tensor("cpool_sb", [128, 8], F32))
    k.dma("Y", cpool[:, :], d["cpool"])
    k.cpool = cpool
    k.wait("S", k.Y, k.Y.val)
    k.wait("V", k.Y, k.Y.val)
    ones128 = es.enter_context(nc.sbuf_tensor("ones128", [128, 1], F32))
    ones128b = es.enter_context(nc.sbuf_tensor("ones128b", [128, 1], BF16))
    ones1 = es.enter_context(nc.sbuf_tensor("ones1", [1, 128], F32))
    idf = es.enter_context(nc.sbuf_tensor("idf", [128, 128], F32))
    k.ones128b = ones128b
    k.V.inc(nc.vector.memset(ones128b[:, :], 1.0))
    k.V.inc(nc.vector.memset(ones128[:, :], 1.0))
    k.V.inc(nc.vector.memset(ones1[:, :], 1.0))
    k.P.inc(nc.gpsimd.memset(idf[:, :], 1.0))
    k.wait("P", k.P, k.P.val)
    ins = nc.gpsimd.affine_select(idf[:, :], idf[:, :], pattern=[[-1, 128]],
                                  compare_op=ALU.is_equal, fill=0.0, base=0,
                                  channel_multiplier=1)
    pv = k.P.inc(ins)
    idb = es.enter_context(nc.sbuf_tensor("idb", [128, 128], BF16))
    k.wait("V", k.P, pv)
    k.V.inc(nc.vector.tensor_copy(idb[:, :], idf[:, :]))
    k.idb = idb
    return ones128, ones1, idf


def emit_ln(k, tag, X, Xb, ones128, ones1, T, eps=1e-5, out_bf=True,
            rstd_out=None):
    nc, es = k.nc, k.es
    ncol = T // 512
    k.barrier()
    with ExitStack() as ps:
        psAr = ps.enter_context(nc.psum_tensor(f"lnps{tag}", [128, 4096], F32))
        SQ = ps.enter_context(nc.sbuf_tensor(f"lnsq{tag}", [128, T], F32))
        stm = ps.enter_context(nc.sbuf_tensor(f"lnm{tag}", [1, T], F32))
        str_ = ps.enter_context(nc.sbuf_tensor(f"lnr{tag}", [1, T], F32))
        xv = k.V.val
        for c in range(NC8):
            k.wait("S", k.V, xv)
            k.wait("S", k.S, k.S.val)
            k.wait("S", k.T, k.T.val)
            ins = nc.scalar.activation(SQ[:, 0:T], X[:, c * T:(c + 1) * T], AF.Square)
            sv = k.S.inc(ins)
            k.wait("T", k.S, sv)
            k.wait("T", k.V, xv)
            for t in range(ncol):
                sl = slice(t * 512, (t + 1) * 512)
                nc.tensor.matmul(psAr[0:1, sl], ones128[:, 0:1],
                                 X[:, c * T + t * 512:c * T + (t + 1) * 512],
                                 start=(c == 0), stop=(c == NC8 - 1),
                                 skip_group_check=True)
                ins2 = nc.tensor.matmul(
                    psAr[0:1, 2048 + t * 512:2048 + (t + 1) * 512],
                    ones128[:, 0:1], SQ[:, sl],
                    start=(c == 0), stop=(c == NC8 - 1), skip_group_check=True)
            k.T.inc(ins2)
        tv = k.T.val
        k.wait("V", k.T, tv)
        ins = nc.vector.tensor_scalar_mul(stm[0:1, :], psAr[0:1, 0:T], 1.0 / D)
        mv = k.V.inc(ins)
        k.wait("S", k.V, mv)
        ins = nc.scalar.activation(str_[0:1, :], stm[0:1, :], AF.Square)
        sv = k.S.inc(ins)
        k.wait("V", k.S, sv)
        ins = nc.vector.scalar_tensor_tensor(str_[0:1, :], psAr[0:1, 2048:2048 + T],
                                             1.0 / D, str_[0:1, :], ALU.mult,
                                             ALU.subtract)
        vv = k.V.inc(ins)
        k.wait("S", k.V, vv)
        ins = nc.scalar.activation(str_[0:1, :], str_[0:1, :], AF.Ln,
                                   bias=k.cp(0, 1))
        sv = k.S.inc(ins)
        k.wait("S", k.S, sv)
        ins = nc.scalar.activation(str_[0:1, :], str_[0:1, :], AF.Exp,
                                   scale=k.cp(1, 1))
        sv = k.S.inc(ins)
        if rstd_out is not None:
            k.wait("V", k.S, sv)
            k.V.inc(nc.vector.tensor_copy(rstd_out[0:1, :], str_[0:1, :]))
        k.wait("T", k.V, mv)
        k.wait("T", k.S, sv)
        for t in range(ncol):
            sl = slice(t * 512, (t + 1) * 512)
            nc.tensor.matmul(psAr[:, sl], ones1[0:1, :], stm[0:1, sl],
                             start=True, stop=True)
            ins = nc.tensor.matmul(psAr[:, 2048 + t * 512:2048 + (t + 1) * 512],
                                   ones1[0:1, :], str_[0:1, sl], start=True,
                                   stop=True)
        tv = k.T.inc(ins)
        k.wait("V", k.T, tv)
        for c in range(NC8):
            ins = nc.vector.tensor_sub(X[:, c * T:(c + 1) * T],
                                       X[:, c * T:(c + 1) * T], psAr[:, 0:T])
            vv = k.V.inc(ins)
            k.wait("V", k.V, vv)
            ins = nc.vector.tensor_mul(X[:, c * T:(c + 1) * T],
                                       X[:, c * T:(c + 1) * T],
                                       psAr[:, 2048:2048 + T])
            vv = k.V.inc(ins)
            if out_bf:
                k.wait("S", k.V, vv)
                ins = nc.scalar.copy(Xb[:, c * T:(c + 1) * T],
                                     X[:, c * T:(c + 1) * T])
                k.S.inc(ins)
        k.barrier()


def emit_mixer_core(k, Xsrc, G, w, T):
    nc = k.nc
    ncol = T // 512
    wfull, bz, bbcd, alog = w["wfull"], w["bz"], w["bbcd"], w["alog"]
    sB, sC, sDel, sTmp = w["sB"], w["sC"], w["sDel"], w["sTmp"]
    yb, ones16 = w["yb"], w["ones16"]
    psZ, psZ2, psBCD, psy = w["psZ"], w["psZ2"], w["psBCD"], w["psy"]
    ones1 = w["ones1"]

    gelu_last = 0
    for zc in range(NC8):
        for t in range(ncol):
            ps_ = psZ if (zc * ncol + t) % 2 == 0 else psZ2
            for kc in range(8):
                ins = nc.tensor.matmul(
                    ps_[:, :],
                    wfull[:, kc * 1072 + zc * 128:kc * 1072 + (zc + 1) * 128],
                    Xsrc[:, kc * T + t * 512:kc * T + (t + 1) * 512],
                    start=(kc == 0), stop=(kc == 7))
            tv = k.T.inc(ins)
            k.wait("S", k.T, tv)
            ins = nc.scalar.activation(
                G[:, zc * T + t * 512:zc * T + (t + 1) * 512], ps_[:, :],
                AF.Gelu, bias=bz[:, zc:zc + 1])
            gelu_last = k.S.inc(ins)
            k.wait("T", k.S, gelu_last)
    for t in range(ncol):
        for j, c0 in enumerate((1024, 1040, 1056)):
            for kc in range(8):
                ins = nc.tensor.matmul(
                    psBCD[:, j * 512:j * 512 + 512],
                    wfull[:, kc * 1072 + c0:kc * 1072 + c0 + 16],
                    Xsrc[:, kc * T + t * 512:kc * T + (t + 1) * 512],
                    start=(kc == 0), stop=(kc == 7))
        tv = k.T.inc(ins)
        k.wait("V", k.T, tv)
        k.wait("V", k.V, k.V.val)
        for j, dst in enumerate((sB, sC, sDel)):
            ins = nc.vector.tensor_scalar_add(dst[:, t * 512:(t + 1) * 512],
                                              psBCD[:, j * 512:j * 512 + 512],
                                              bbcd[:, j:j + 1])
            k.V.inc(ins)
        k.wait("T", k.V, k.V.val)

    k.wait("S", k.V, k.V.val)
    sv = k.S.inc(nc.scalar.activation(sTmp[:, :], sDel[:, :], AF.Abs))
    k.wait("S", k.S, sv)
    sv = k.S.inc(nc.scalar.activation(sTmp[:, :], sTmp[:, :], AF.Exp,
                                      scale=k.cp(4, 16)))
    k.wait("S", k.S, sv)
    sv = k.S.inc(nc.scalar.activation(sTmp[:, :], sTmp[:, :], AF.Ln,
                                      bias=k.cp(5, 16)))
    k.wait("V", k.S, sv)
    vv = k.V.inc(nc.vector.tensor_scalar_max(sDel[:, :], sDel[:, :], 0.0))
    k.wait("V", k.V, vv)
    vv = k.V.inc(nc.vector.tensor_add(sDel[:, :], sDel[:, :], sTmp[:, :]))
    k.wait("V", k.V, vv)
    vv = k.V.inc(nc.vector.tensor_scalar_mul(sDel[:, :], sDel[:, :],
                                             alog[:, 0:1]))
    k.wait("S", k.V, vv)
    sv = k.S.inc(nc.scalar.activation(sDel[:, :], sDel[:, :], AF.Exp,
                                      scale=k.cp(4, 16)))
    k.wait("V", k.S, sv)
    vv = k.V.inc(nc.vector.tensor_tensor_scan(sTmp[:, :], sDel[:, :], sB[:, :],
                                              0.0, ALU.mult, ALU.add))
    k.wait("V", k.V, vv)
    vv = k.V.inc(nc.vector.tensor_mul(sTmp[:, :], sTmp[:, :], sC[:, :]))
    k.wait("T", k.V, vv)
    k.wait("T", k.S, k.S.val)
    for t in range(ncol):
        tv = k.T.inc(nc.tensor.matmul(psy[0:1, :], ones16[:, 0:1],
                                      sTmp[:, t * 512:(t + 1) * 512],
                                      start=True, stop=True))
        k.wait("V", k.T, tv)
        vv = k.V.inc(nc.vector.tensor_copy(sTmp[0:1, t * 512:(t + 1) * 512],
                                           psy[0:1, :]))
        k.wait("T", k.V, vv)
    for t in range(ncol):
        ps_ = psZ if t % 2 == 0 else psZ2
        tv = k.T.inc(nc.tensor.matmul(ps_[:, :], ones1[0:1, :],
                                      sTmp[0:1, t * 512:(t + 1) * 512],
                                      start=True, stop=True))
        k.wait("V", k.T, tv)
        vv = k.V.inc(nc.vector.tensor_copy(yb[:, t * 512:(t + 1) * 512],
                                           ps_[:, :]))
        k.wait("T", k.V, vv)
    k.wait("V", k.S, gelu_last)
    for c in range(NC8):
        k.wait("V", k.V, k.V.val)
        k.V.inc(nc.vector.tensor_mul(G[:, c * T:(c + 1) * T],
                                     G[:, c * T:(c + 1) * T], yb[:, 0:T]))


def alloc_mixer_scratch(k, ls, tag, T, ones1):
    nc = k.nc
    w = {}
    w["wfull"] = ls.enter_context(nc.sbuf_tensor(f"mx_wf{tag}", [128, 8 * 1072],
                                                 BF16))
    w["bz"] = ls.enter_context(nc.sbuf_tensor(f"mx_bz{tag}", [128, 8], F32))
    w["bbcd"] = ls.enter_context(nc.sbuf_tensor(f"mx_bb{tag}", [16, 3], F32))
    w["alog"] = ls.enter_context(nc.sbuf_tensor(f"mx_al{tag}", [16, 1], F32))
    w["sB"] = ls.enter_context(nc.sbuf_tensor(f"mx_B{tag}", [16, T], F32))
    w["sC"] = ls.enter_context(nc.sbuf_tensor(f"mx_C{tag}", [16, T], F32))
    w["sDel"] = ls.enter_context(nc.sbuf_tensor(f"mx_D{tag}", [16, T], F32))
    w["sTmp"] = ls.enter_context(nc.sbuf_tensor(f"mx_t{tag}", [16, T], F32))
    w["yb"] = ls.enter_context(nc.sbuf_tensor(f"mx_y{tag}", [128, T], BF16))
    w["ones16"] = ls.enter_context(nc.sbuf_tensor(f"mx_o16{tag}", [16, 1], F32))
    w["psZ"] = ls.enter_context(nc.psum_tensor(f"mx_pZ{tag}", [128, 512], F32))
    w["psZ2"] = ls.enter_context(nc.psum_tensor(f"mx_pZ2{tag}", [128, 512], F32))
    w["psBCD"] = ls.enter_context(nc.psum_tensor(f"mx_pB{tag}", [16, 3 * 512],
                                                 F32))
    w["psy"] = ls.enter_context(nc.psum_tensor(f"mx_py{tag}", [1, 512], F32))
    w["ones1"] = ones1
    return w


def load_mixer_weights(k, w, d):
    nc = k.nc
    for name, key in (("bz", "bz"), ("bbcd", "bbcd"), ("alog", "alog")):
        k.dma("Y", w[key][:, :], d[name])
    for kc in range(8):
        k.dma("G", w["wfull"][:, kc * 1072:(kc + 1) * 1072],
              d["w_in"][kc * 128:(kc + 1) * 128, :])
    ldy, ldg = k.Y.val, k.G.val
    k.wait("V", k.Y, ldy)
    k.V.inc(nc.vector.memset(w["ones16"][:, :], 1.0))
    k.wait("S", k.Y, ldy)
    sv = k.S.inc(nc.scalar.activation(w["alog"][:, :], w["alog"][:, :], AF.Exp))
    k.wait("V", k.S, sv)
    k.wait("T", k.G, ldg)
    k.wait("T", k.Y, ldy)
    k.wait("T", k.V, k.V.val)


def emit_l0_attn(k, X, Xb, d, ones128, ones1, T):
    nc, es = k.nc, k.es
    ncol = T // 512
    k.barrier()
    with ExitStack() as ls:
        w = alloc_mixer_scratch(k, ls, "a", T, ones1)
        wob = [ls.enter_context(nc.sbuf_tensor(f"l0wo{i}", [128, 1024], BF16))
               for i in range(2)]
        G = ls.enter_context(nc.sbuf_tensor("l0G", [128, 8 * T], BF16))
        bout = ls.enter_context(nc.sbuf_tensor("l0bout", [128, 8], F32))
        k.dma("Y", bout[:, :], d["bout"])
        load_mixer_weights(k, w, d)
        emit_mixer_core(k, Xb, G, w, T)
        uv = k.V.val
        k.wait("T", k.V, uv)
        k.wait("V", k.Y, k.Y.val)
        psZ, psZ2 = w["psZ"], w["psZ2"]
        for oc in range(NC8):
            b = oc % 2
            k.wait("G", k.T, k.T.val)
            k.dma("G", wob[b][:, :], d["wo_t"][oc])
            k.wait("T", k.G, k.G.val)
            for t in range(ncol):
                ps_ = psZ if (oc * ncol + t) % 2 == 0 else psZ2
                for kc in range(8):
                    ins = nc.tensor.matmul(
                        ps_[:, :], wob[b][:, kc * 128:(kc + 1) * 128],
                        G[:, kc * T + t * 512:kc * T + (t + 1) * 512],
                        start=(kc == 0), stop=(kc == 7))
                tv = k.T.inc(ins)
                k.wait("V", k.T, tv)
                sl = slice(oc * T + t * 512, oc * T + (t + 1) * 512)
                ins = nc.vector.scalar_tensor_tensor(X[:, sl], ps_[:, :],
                                                     bout[:, oc:oc + 1], X[:, sl],
                                                     ALU.add, ALU.add)
                vv = k.V.inc(ins)
                k.wait("T", k.V, vv)
        k.barrier()



def emit_phase0(k, d, ones128, ones1, idf, T, art):
    nc, es = k.nc, k.es
    ncol = T // 512
    ntile = T // 128
    k.barrier()
    with ExitStack() as ls:
        Xf = ls.enter_context(nc.sbuf_tensor("p_x", [128, 8 * T], BF16))
        G = ls.enter_context(nc.sbuf_tensor("p_g", [128, 8 * T], BF16))
        rwtb = ls.enter_context(nc.sbuf_tensor("p_rw", [128, 8 * 8], BF16))
        LX = ls.enter_context(nc.sbuf_tensor("p_lx", [128, 8 * 9], BF16))
        LU = ls.enter_context(nc.sbuf_tensor("p_lu", [128, 8 * 9], BF16))
        boutb = ls.enter_context(nc.sbuf_tensor("p_bo", [128, 8], BF16))
        cfull = ls.enter_context(nc.sbuf_tensor("p_cf", [128, 9], F32))
        swefull = ls.enter_context(nc.sbuf_tensor("p_sw", [128, 9], F32))
        c91 = ls.enter_context(nc.sbuf_tensor("p_c91", [9, 1], F32))
        c19 = ls.enter_context(nc.sbuf_tensor("p_c19", [1, 9], F32))
        PR = ls.enter_context(nc.sbuf_tensor("p_pr", [16, T], F32))
        ptile = ls.enter_context(nc.sbuf_tensor("p_pt", [128, 9], F32))
        pf = ls.enter_context(nc.sbuf_tensor("p_pf", [128, 8], F32))
        pf2 = ls.enter_context(nc.sbuf_tensor("p_pf2", [128, 8], F32))
        scr8 = ls.enter_context(nc.sbuf_tensor("p_s8", [128, 8], F32))
        mm1 = ls.enter_context(nc.sbuf_tensor("p_m1", [128, 1], F32))
        mm2 = ls.enter_context(nc.sbuf_tensor("p_m2", [128, 1], F32))
        mneg = ls.enter_context(nc.sbuf_tensor("p_mn", [128, 1], F32))
        selT = ls.enter_context(nc.sbuf_tensor("p_st", [128, 8 * ntile], F32))
        MSK1b = ls.enter_context(nc.sbuf_tensor("p_k1", [128, 8 * ntile], F32))
        MSK2b = ls.enter_context(nc.sbuf_tensor("p_k2", [128, 8 * ntile], F32))
        dpb = ls.enter_context(nc.sbuf_tensor("p_dp", [128, ntile], F32))
        selEb = ls.enter_context(nc.sbuf_tensor("p_se", [8, T], F32))
        CAc = ls.enter_context(nc.sbuf_tensor("p_ca", [8, 8], F32))
        CAsb = ls.enter_context(nc.sbuf_tensor("p_cas", [8, 8], F32))
        im8 = ls.enter_context(nc.sbuf_tensor("p_im8", [8, 8], F32))
        im128 = ls.enter_context(nc.sbuf_tensor("p_imc", [128, 8], F32))
        pmsb = ls.enter_context(nc.sbuf_tensor("p_pm", [8, 1], F32))

        for kc in range(8):
            k.dma("G", rwtb[:, kc * 8:(kc + 1) * 8],
                  d["rwT"][kc * 128:(kc + 1) * 128, :])
        k.dma("Y", im8[:, :], d["im8"])
        k.dma("Y", im128[:, :], d["im128"])
        k.dma("Y", pmsb[:, :], d["prevmask"])
        ldy, ldg = k.Y.val, k.G.val
        k.wait("V", k.Y, ldy)
        k.wait("V", k.G, ldg)
        k.V.inc(nc.vector.memset(art["selE"][:, :], 0.0))
        k.V.inc(nc.vector.memset(art["msk1"][:, :], 0.0))
        k.V.inc(nc.vector.memset(art["msk2"][:, :], 0.0))
        k.V.inc(nc.vector.memset(art["dp"][:, :], 0.0))
        for kc in range(8):
            k.wait("V", k.V, k.V.val)
            k.V.inc(nc.vector.tensor_copy(LX[:, kc * 9:kc * 9 + 8],
                                          rwtb[:, kc * 8:(kc + 1) * 8]))
            k.V.inc(nc.vector.tensor_scalar_mul(LX[:, kc * 9 + 8:kc * 9 + 9],
                                                ones128[:, 0:1], 1.0 / D))
        lxv = k.V.val

        with ExitStack() as pp:
            wout = pp.enter_context(nc.sbuf_tensor("p_wo", [128, 8 * 1024], BF16))
            WoT = pp.enter_context(nc.sbuf_tensor("p_wot", [128, 8 * 1024], BF16))
            bout = pp.enter_context(nc.sbuf_tensor("p_bof", [128, 8], F32))
            psTb = pp.enter_context(nc.psum_tensor("p_ptb", [128, 128], BF16))
            psLU = pp.enter_context(nc.psum_tensor("p_plu", [128, 9], F32))
            psc = pp.enter_context(nc.psum_tensor("p_pc", [9, 1], F32))
            psT9 = pp.enter_context(nc.psum_tensor("p_pt9", [128, 9], F32))
            for kc in range(8):
                k.dma("G", wout[:, kc * 1024:(kc + 1) * 1024],
                      d["w_out"][kc * 128:(kc + 1) * 128, :])
            k.dma("Y", bout[:, :], d["bout"])
            k.wait("V", k.Y, k.Y.val)
            vv = k.V.inc(nc.vector.tensor_copy(boutb[:, :], bout[:, :]))
            ldg = k.G.val
            k.wait("T", k.G, ldg)
            k.wait("T", k.P, k.P.val)
            k.wait("T", k.V, k.V.val)
            for jc in range(8):
                for dc in range(8):
                    k.wait("T", k.S, k.S.val)
                    tv = k.T.inc(nc.tensor.transpose(
                        psTb[:, :],
                        wout[:, dc * 1024 + jc * 128:dc * 1024 + (jc + 1) * 128],
                        k.idb[:, :]))
                    k.wait("S", k.T, tv)
                    k.S.inc(nc.scalar.copy(
                        WoT[:, jc * 1024 + dc * 128:jc * 1024 + (dc + 1) * 128],
                        psTb[:, :]))
            k.wait("T", k.S, k.S.val)
            k.wait("T", k.V, lxv)
            for dc in range(8):
                for jc in range(8):
                    ins = nc.tensor.matmul(psLU[:, :],
                                           WoT[:, jc * 1024 + dc * 128:
                                               jc * 1024 + (dc + 1) * 128],
                                           LX[:, jc * 9:(jc + 1) * 9],
                                           start=(jc == 0), stop=(jc == 7))
                tv = k.T.inc(ins)
                k.wait("S", k.T, tv)
                sv = k.S.inc(nc.scalar.copy(LU[:, dc * 9:(dc + 1) * 9],
                                            psLU[:, :]))
                k.wait("T", k.S, sv)
            for name in ("c", "s"):
                for dc in range(8):
                    rhs_ap = boutb[:, dc:dc + 1] if name == "c" \
                        else k.ones128b[:, 0:1]
                    ins = nc.tensor.matmul(psc[:, :], LX[:, dc * 9:(dc + 1) * 9],
                                           rhs_ap, start=(dc == 0),
                                           stop=(dc == 7))
                tv = k.T.inc(ins)
                k.wait("V", k.T, tv)
                vv = k.V.inc(nc.vector.tensor_copy(c91[:, :], psc[:, :]))
                k.wait("T", k.V, vv)
                tv = k.T.inc(nc.tensor.transpose(psT9[0:1, 0:9], c91[:, :],
                                                 idf[0:9, 0:9]))
                k.wait("V", k.T, tv)
                vv = k.V.inc(nc.vector.tensor_copy(c19[:, :], psT9[0:1, 0:9]))
                k.wait("T", k.V, vv)
                tv = k.T.inc(nc.tensor.matmul(psT9[:, :], ones1[0:1, :],
                                              c19[0:1, :], start=True,
                                              stop=True))
                k.wait("V", k.T, tv)
                dst = cfull if name == "c" else swefull
                vv = k.V.inc(nc.vector.tensor_copy(dst[:, :], psT9[:, :]))
                k.wait("T", k.V, vv)

        k.barrier()
        w = alloc_mixer_scratch(k, ls, "p", T, ones1)
        load_mixer_weights(k, w, d)
        with ExitStack() as pp:
            ps9 = pp.enter_context(nc.psum_tensor("p_p9", [9, 512], F32))
            psT = pp.enter_context(nc.psum_tensor("p_pT", [128, 128], F32))

            for b8 in range(NEXP):
                k.barrier()
                for c in range(NC8):
                    k.dma("G", Xf[:, c * T:(c + 1) * T],
                          d["x_all"][b8 * D + c * 128:b8 * D + (c + 1) * 128, :])
                k.wait("T", k.G, k.G.val)
                emit_mixer_core(k, Xf, G, w, T)
                uv = k.V.val
                k.wait("T", k.V, uv)
                for t in range(ncol):
                    ts = slice(t * 512, (t + 1) * 512)
                    k.wait("T", k.V, k.V.val)
                    for kc in range(8):
                        nc.tensor.matmul(ps9[:, :], LX[:, kc * 9:(kc + 1) * 9],
                                         Xf[:, kc * T + t * 512:
                                            kc * T + (t + 1) * 512],
                                         start=(kc == 0), stop=False)
                    for kc in range(8):
                        ins = nc.tensor.matmul(ps9[:, :],
                                               LU[:, kc * 9:(kc + 1) * 9],
                                               G[:, kc * T + t * 512:
                                                  kc * T + (t + 1) * 512],
                                               start=False, stop=(kc == 7))
                    tv = k.T.inc(ins)
                    k.wait("V", k.T, tv)
                    vv = k.V.inc(nc.vector.tensor_copy(PR[0:9, ts], ps9[:, :]))
                    k.wait("T", k.V, vv)
                for tt in range(ntile):
                    k.wait("T", k.V, k.V.val)
                    tv = k.T.inc(nc.tensor.transpose(
                        psT[:, 0:9], PR[0:9, tt * 128:(tt + 1) * 128],
                        idf[0:9, 0:9]))
                    k.wait("V", k.T, tv)
                    vv = k.V.inc(nc.vector.tensor_copy(ptile[:, :], psT[:, 0:9]))
                    k.wait("V", k.V, vv)
                    k.wait("T", k.V, vv)
                    vv = k.V.inc(nc.vector.tensor_scalar(
                        mneg[:, :], ptile[:, 8:9], cfull[:, 8:9], -1.0,
                        op0=ALU.add, op1=ALU.mult))
                    k.wait("V", k.V, vv)
                    vv = k.V.inc(nc.vector.tensor_add(scr8[:, :], ptile[:, 0:8],
                                                      cfull[:, 0:8]))
                    k.wait("V", k.V, vv)
                    vv = k.V.inc(nc.vector.scalar_tensor_tensor(
                        pf[:, :], swefull[:, 0:8], mneg[:, 0:1], scr8[:, :],
                        ALU.mult, ALU.add))
                    k.wait("V", k.V, vv)
                    vv = k.V.inc(nc.vector.tensor_reduce(mm1[:, :], pf[:, :],
                                                         axis=AX.X, op=ALU.max))
                    k.wait("V", k.V, vv)
                    vv = k.V.inc(nc.vector.tensor_scalar(
                        MSK1b[:, tt * 8:(tt + 1) * 8], pf[:, :], mm1[:, 0:1],
                        None, op0=ALU.is_equal))
                    k.wait("V", k.V, vv)
                    vv = k.V.inc(nc.vector.scalar_tensor_tensor(
                        pf2[:, :], MSK1b[:, tt * 8:(tt + 1) * 8], -1e30,
                        pf[:, :], ALU.mult, ALU.add))
                    k.wait("V", k.V, vv)
                    vv = k.V.inc(nc.vector.tensor_reduce(mm2[:, :], pf2[:, :],
                                                         axis=AX.X, op=ALU.max))
                    k.wait("V", k.V, vv)
                    vv = k.V.inc(nc.vector.tensor_scalar(
                        MSK2b[:, tt * 8:(tt + 1) * 8], pf2[:, :], mm2[:, 0:1],
                        None, op0=ALU.is_equal))
                    k.wait("V", k.V, vv)
                    vv = k.V.inc(nc.vector.tensor_sub(dpb[:, tt:tt + 1],
                                                      mm1[:, :], mm2[:, :]))
                    k.wait("V", k.V, vv)
                    vv = k.V.inc(nc.vector.tensor_add(
                        selT[:, tt * 8:(tt + 1) * 8],
                        MSK1b[:, tt * 8:(tt + 1) * 8],
                        MSK2b[:, tt * 8:(tt + 1) * 8]))
                k.wait("T", k.V, k.V.val)
                for tt in range(ntile):
                    tv = k.T.inc(nc.tensor.transpose(
                        psT[0:8, :], selT[:, tt * 8:(tt + 1) * 8], idf[:, :]))
                    k.wait("V", k.T, tv)
                    vv = k.V.inc(nc.vector.tensor_copy(
                        selEb[:, tt * 128:(tt + 1) * 128], psT[0:8, :]))
                    k.wait("T", k.V, vv)
                k.wait("V", k.V, k.V.val)
                vv = k.V.inc(nc.vector.tensor_reduce(CAc[:, b8:b8 + 1],
                                                     selEb[:, :], axis=AX.X,
                                                     op=ALU.add))
                k.wait("V", k.V, vv)
                vv = k.V.inc(nc.vector.scalar_tensor_tensor(
                    art["selE"][:, :], selEb[:, :], im8[:, b8:b8 + 1],
                    art["selE"][:, :], ALU.mult, ALU.add))
                k.wait("V", k.V, vv)
                vv = k.V.inc(nc.vector.scalar_tensor_tensor(
                    art["msk1"][:, :], MSK1b[:, :], im128[:, b8:b8 + 1],
                    art["msk1"][:, :], ALU.mult, ALU.add))
                k.wait("V", k.V, vv)
                vv = k.V.inc(nc.vector.scalar_tensor_tensor(
                    art["msk2"][:, :], MSK2b[:, :], im128[:, b8:b8 + 1],
                    art["msk2"][:, :], ALU.mult, ALU.add))
                k.wait("V", k.V, vv)
                vv = k.V.inc(nc.vector.scalar_tensor_tensor(
                    art["dp"][:, :], dpb[:, :], im128[:, b8:b8 + 1],
                    art["dp"][:, :], ALU.mult, ALU.add))
                k.wait("V", k.V, vv)

            k.wait("T", k.V, k.V.val)
            tv = k.T.inc(nc.tensor.transpose(psT[0:8, 0:8], CAc[:, :],
                                             idf[0:8, 0:8]))
            k.wait("V", k.T, tv)
            vv = k.V.inc(nc.vector.tensor_copy(CAsb[:, :], psT[0:8, 0:8]))
            k.wait("T", k.V, vv)
            tv = k.T.inc(nc.tensor.matmul(psT[0:8, 0:1], CAsb[:, :],
                                          pmsb[:, :], start=True, stop=True))
            k.wait("V", k.T, tv)
            k.V.inc(nc.vector.tensor_copy(art["off"][:, :], psT[0:8, 0:1]))
    k.barrier()


def emit_routing2(k, X, Xb, d, ones128, ones1, idf, T, art, rstd, weff,
                  rankI=None, weffT=None):
    nc, es = k.nc, k.es
    ncol = T // 512
    ntile = T // 128
    d_sc = d["scratch1"]
    k.barrier()
    with ExitStack() as ls:
        SQ = ls.enter_context(nc.sbuf_tensor("r2_sq", [128, T], F32))
        rms = ls.enter_context(nc.sbuf_tensor("r2_rms", [1, T], F32))
        rr_tm = ls.enter_context(nc.sbuf_tensor("r2_rmt", [128, ntile], F32))
        rw1 = ls.enter_context(nc.sbuf_tensor("r2_rw", [128, ntile], F32))
        wdT = ls.enter_context(nc.sbuf_tensor("r2_wd", [128, 8 * ntile], F32))
        wdE = ls.enter_context(nc.sbuf_tensor("r2_wde", [8, T], F32))
        csum = ls.enter_context(nc.sbuf_tensor("r2_cs", [8, T], F32))
        ones8 = ls.enter_context(nc.sbuf_tensor("r2_o8", [8, T], F32))
        psS = ls.enter_context(nc.psum_tensor("r2_psS", [1, 2048], F32))
        psT = ls.enter_context(nc.psum_tensor("r2_psT", [8, 128], F32))
        psT2 = ls.enter_context(nc.psum_tensor("r2_psT2", [128, 8], F32))
        k.V.inc(nc.vector.memset(ones8[:, :], 1.0))

        xv = k.V.val
        for c in range(NC8):
            k.wait("S", k.V, xv)
            k.wait("S", k.S, k.S.val)
            k.wait("S", k.T, k.T.val)
            sv = k.S.inc(nc.scalar.activation(SQ[:, 0:T],
                                              X[:, c * T:(c + 1) * T],
                                              AF.Square))
            k.wait("T", k.S, sv)
            for t in range(ncol):
                ins2 = nc.tensor.matmul(psS[0:1, t * 512:(t + 1) * 512],
                                        ones128[:, 0:1],
                                        SQ[:, t * 512:(t + 1) * 512],
                                        start=(c == 0), stop=(c == NC8 - 1),
                                        skip_group_check=True)
            k.T.inc(ins2)
        tv = k.T.val
        k.wait("S", k.T, tv)
        sv = k.S.inc(nc.scalar.activation(rms[0:1, :], psS[0:1, 0:T], AF.Ln,
                                          scale=k.cp(3, 1), bias=k.cp(2, 1)))
        k.wait("S", k.S, sv)
        sv = k.S.inc(nc.scalar.activation(rms[0:1, :], rms[0:1, :], AF.Exp,
                                          scale=k.cp(1, 1)))
        k.wait("V", k.S, sv)
        vv = k.V.inc(nc.vector.tensor_mul(rms[0:1, :], rms[0:1, :],
                                          rstd[0:1, :]))
        k.wait("G", k.V, vv)
        k.dma("G", d_sc[0:1, 0:T], rms[0:1, :])
        k.dma("G", rr_tm[:, 0:ntile],
              d_sc[0:1, 0:T].rearrange("o (a b) -> b (o a)", b=128),
              allow_slow_non_contiguous=True)
        rrv = k.G.val
        k.wait("V", k.G, rrv)
        vv = k.V.inc(nc.vector.tensor_mul(rw1[:, :], art["dp"][:, :],
                                          rr_tm[:, :]))
        k.wait("S", k.V, vv)
        sv = k.S.inc(nc.scalar.activation(rw1[:, :], rw1[:, :], AF.Exp,
                                          scale=k.cp(4)))
        k.wait("V", k.S, sv)
        vv = k.V.inc(nc.vector.tensor_scalar_add(rw1[:, :], rw1[:, :], 1.0))
        k.wait("V", k.V, vv)
        vv = k.V.inc(nc.vector.reciprocal(rw1[:, :], rw1[:, :]))
        k.wait("V", k.V, vv)
        for tt in range(ntile):
            s8 = slice(tt * 8, (tt + 1) * 8)
            vv = k.V.inc(nc.vector.tensor_sub(wdT[:, s8], art["msk1"][:, s8],
                                              art["msk2"][:, s8]))
            k.wait("V", k.V, vv)
            vv = k.V.inc(nc.vector.tensor_scalar_mul(wdT[:, s8], wdT[:, s8],
                                                     rw1[:, tt:tt + 1]))
            k.wait("V", k.V, vv)
            vv = k.V.inc(nc.vector.tensor_add(wdT[:, s8], wdT[:, s8],
                                              art["msk2"][:, s8]))
            k.wait("V", k.V, vv)
        k.wait("T", k.V, k.V.val)
        for tt in range(ntile):
            tv = k.T.inc(nc.tensor.transpose(psT[:, :],
                                             wdT[:, tt * 8:(tt + 1) * 8],
                                             idf[:, :]))
            k.wait("V", k.T, tv)
            vv = k.V.inc(nc.vector.tensor_copy(wdE[:, tt * 128:(tt + 1) * 128],
                                               psT[:, :]))
            k.wait("T", k.V, vv)
        k.wait("V", k.V, k.V.val)
        vv = k.V.inc(nc.vector.tensor_tensor_scan(csum[:, :], ones8[:, :],
                                                  art["selE"][:, :], 0.0,
                                                  ALU.mult, ALU.add))
        if rankI is not None:
            ntl = T // 128
            rkE = ls.enter_context(nc.sbuf_tensor("r2_rk", [8, T], F32))
            rkT = ls.enter_context(nc.sbuf_tensor("r2_rkt", [128, 8 * ntl], F32))
            k.wait("V", k.V, k.V.val)
            vv = k.V.inc(nc.vector.tensor_sub(rkE[:, :], csum[:, :],
                                              art["selE"][:, :]))
            k.wait("V", k.V, vv)
            vv = k.V.inc(nc.vector.tensor_scalar_add(rkE[:, :], rkE[:, :],
                                                     1e6))
            k.wait("V", k.V, vv)
            vv = k.V.inc(nc.vector.scalar_tensor_tensor(
                rkE[:, :], art["selE"][:, :], -1e6, rkE[:, :], ALU.mult,
                ALU.add))
            k.wait("T", k.V, vv)
            for tt in range(ntl):
                tv = k.T.inc(nc.tensor.transpose(
                    psT2[:, :], rkE[:, tt * 128:(tt + 1) * 128], idf[0:8, 0:8]))
                k.wait("V", k.T, tv)
                vv = k.V.inc(nc.vector.tensor_copy(
                    rkT[:, tt * 8:(tt + 1) * 8], psT2[:, :]))
                k.wait("T", k.V, vv)
            k.wait("V", k.V, k.V.val)
            vv = k.V.inc(nc.vector.tensor_copy(rankI[:, :], rkT[:, :]))
        k.wait("V", k.V, k.V.val)
        vv = k.V.inc(nc.vector.tensor_sub(weff[:, :], csum[:, :],
                                          art["selE"][:, :]))
        k.wait("V", k.V, vv)
        vv = k.V.inc(nc.vector.tensor_scalar_add(weff[:, :], weff[:, :],
                                                 art["off"][:, 0:1]))
        k.wait("V", k.V, vv)
        vv = k.V.inc(nc.vector.tensor_scalar(weff[:, :], weff[:, :], CAP,
                                             None, op0=ALU.is_lt))
        k.wait("V", k.V, vv)
        vv = k.V.inc(nc.vector.tensor_mul(weff[:, :], weff[:, :],
                                          art["selE"][:, :]))
        k.wait("V", k.V, vv)
        vv = k.V.inc(nc.vector.tensor_mul(weff[:, :], weff[:, :], wdE[:, :]))
        if weffT is not None:
            k.wait("T", k.V, vv)
            for tt in range(ntile):
                tv = k.T.inc(nc.tensor.transpose(
                    psT2[:, :], weff[:, tt * 128:(tt + 1) * 128],
                    idf[0:8, 0:8]))
                k.wait("V", k.T, tv)
                vv = k.V.inc(nc.vector.tensor_copy(
                    weffT[:, tt * 8:(tt + 1) * 8], psT2[:, :]))
                k.wait("T", k.V, vv)
    k.barrier()


def emit_routing(k, X, Xb, d, ones128, ones1, idf, T, route_sb):
    nc, es = k.nc, k.es
    ncol = T // 512
    ntile = T // 128
    k.barrier()
    selE, wdE, csum, counts = (route_sb["selE"], route_sb["wdE"],
                               route_sb["csum"], route_sb["counts"])
    d_sc = d["scratch1"]
    with ExitStack() as ls:
        rwt = ls.enter_context(nc.sbuf_tensor("rt_w", [128, 8 * 8], F32))
        SQ = ls.enter_context(nc.sbuf_tensor("rt_sq", [128, T], F32))
        rms = ls.enter_context(nc.sbuf_tensor("rt_rms", [1, T], F32))
        rms_tm = ls.enter_context(nc.sbuf_tensor("rt_rmt", [128, ntile], F32))
        lgT = ls.enter_context(nc.sbuf_tensor("rt_lg", [128, 8 * ntile], F32))
        m1 = ls.enter_context(nc.sbuf_tensor("rt_m1", [128, ntile], F32))
        m2 = ls.enter_context(nc.sbuf_tensor("rt_m2", [128, ntile], F32))
        msk1 = ls.enter_context(nc.sbuf_tensor("rt_k1", [128, 8 * ntile], F32))
        msk2 = ls.enter_context(nc.sbuf_tensor("rt_k2", [128, 8 * ntile], F32))
        rw1 = ls.enter_context(nc.sbuf_tensor("rt_rw1", [128, ntile], F32))
        selT = ls.enter_context(nc.sbuf_tensor("rt_selT", [128, 8 * ntile], F32))
        wdT = ls.enter_context(nc.sbuf_tensor("rt_wdT", [128, 8 * ntile], F32))
        ones8 = ls.enter_context(nc.sbuf_tensor("rt_o8", [8, T], F32))
        psS = ls.enter_context(nc.psum_tensor("rt_psS", [1, 2048], F32))
        pslg = ls.enter_context(nc.psum_tensor("pslg", [128, 8], F32))
        pslg2 = ls.enter_context(nc.psum_tensor("pslg2", [128, 8], F32))
        pstr = ls.enter_context(nc.psum_tensor("pstr", [8, 128], F32))
        pstr2 = ls.enter_context(nc.psum_tensor("pstr2", [8, 128], F32))

        for kc in range(8):
            k.dma("G", rwt[:, kc * 8:(kc + 1) * 8],
                                        d["rwT"][kc * 128:(kc + 1) * 128, :])
        ldg = k.G.val
        k.V.inc(nc.vector.memset(ones8[:, :], 1.0))

        xv = k.V.val
        for c in range(NC8):
            k.wait("S", k.V, xv)
            k.wait("S", k.S, k.S.val)
            k.wait("S", k.T, k.T.val)
            ins = nc.scalar.activation(SQ[:, 0:T], X[:, c * T:(c + 1) * T],
                                       AF.Square)
            sv = k.S.inc(ins)
            k.wait("T", k.S, sv)
            for t in range(ncol):
                ins2 = nc.tensor.matmul(psS[0:1, t * 512:(t + 1) * 512],
                                        ones128[:, 0:1],
                                        SQ[:, t * 512:(t + 1) * 512],
                                        start=(c == 0), stop=(c == NC8 - 1),
                                        skip_group_check=True)
            k.T.inc(ins2)
        tv = k.T.val
        k.wait("S", k.T, tv)
        ins = nc.scalar.activation(rms[0:1, :], psS[0:1, 0:T], AF.Ln,
                                   scale=k.cp(3, 1), bias=k.cp(2, 1))
        sv = k.S.inc(ins)
        k.wait("S", k.S, sv)
        sv = k.S.inc(nc.scalar.activation(rms[0:1, :], rms[0:1, :], AF.Exp,
                                          scale=k.cp(1, 1)))
        k.wait("G", k.S, sv)
        k.dma("G", d_sc[0:1, 0:T], rms[0:1, :])
        k.wait("G", k.G, k.G.val)
        k.dma("G", 
            rms_tm[:, 0:ntile],
            d_sc[0:1, 0:T].rearrange("o (a b) -> b (o a)", b=128),
            allow_slow_non_contiguous=True)
        rmsv = k.G.val

        k.wait("T", k.G, ldg)
        k.wait("T", k.Y, k.Y.val)
        for tt in range(ntile):
            ps_ = pslg if tt % 2 == 0 else pslg2
            for kc in range(8):
                ins = nc.tensor.matmul(
                    ps_[:, :], X[:, kc * T + tt * 128:kc * T + (tt + 1) * 128],
                    rwt[:, kc * 8:(kc + 1) * 8], start=(kc == 0), stop=(kc == 7))
            tv = k.T.inc(ins)
            k.wait("V", k.T, tv)
            vv = k.V.inc(nc.vector.tensor_copy(lgT[:, tt * 8:(tt + 1) * 8],
                                               ps_[:, :]))
            k.wait("T", k.V, vv)
        for tt in range(ntile):
            s8 = slice(tt * 8, (tt + 1) * 8)
            s1_ = slice(tt, tt + 1)
            k.wait("V", k.V, k.V.val)
            vv = k.V.inc(nc.vector.tensor_reduce(m1[:, s1_], lgT[:, s8],
                                                 axis=AX.X, op=ALU.max))
            k.wait("V", k.V, vv)
            vv = k.V.inc(nc.vector.tensor_scalar(msk1[:, s8], lgT[:, s8],
                                                 m1[:, s1_], None,
                                                 op0=ALU.is_equal))
            k.wait("V", k.V, vv)
            vv = k.V.inc(nc.vector.scalar_tensor_tensor(msk2[:, s8], msk1[:, s8],
                                                        -1e30, lgT[:, s8],
                                                        ALU.mult, ALU.add))
            k.wait("V", k.V, vv)
            vv = k.V.inc(nc.vector.tensor_reduce(m2[:, s1_], msk2[:, s8],
                                                 axis=AX.X, op=ALU.max))
            k.wait("V", k.V, vv)
            vv = k.V.inc(nc.vector.tensor_scalar(msk2[:, s8], msk2[:, s8],
                                                 m2[:, s1_], None,
                                                 op0=ALU.is_equal))
            k.wait("V", k.V, vv)
            vv = k.V.inc(nc.vector.tensor_sub(m1[:, s1_], m1[:, s1_], m2[:, s1_]))
            k.wait("V", k.G, rmsv)
            k.wait("V", k.V, vv)
            vv = k.V.inc(nc.vector.tensor_mul(m1[:, s1_], m1[:, s1_],
                                              rms_tm[:, s1_]))
            k.wait("S", k.V, vv)
            sv = k.S.inc(nc.scalar.activation(m2[:, s1_], m1[:, s1_], AF.Exp,
                                              scale=k.cp(4)))
            k.wait("V", k.S, sv)
            vv = k.V.inc(nc.vector.tensor_scalar_add(m2[:, s1_], m2[:, s1_], 1.0))
            k.wait("V", k.V, vv)
            vv = k.V.inc(nc.vector.reciprocal(rw1[:, s1_], m2[:, s1_]))
            k.wait("V", k.V, vv)
            k.V.inc(nc.vector.tensor_add(selT[:, s8], msk1[:, s8], msk2[:, s8]))
            vv = k.V.inc(nc.vector.tensor_sub(wdT[:, s8], msk1[:, s8],
                                              msk2[:, s8]))
            k.wait("V", k.V, vv)
            vv = k.V.inc(nc.vector.tensor_scalar_mul(wdT[:, s8], wdT[:, s8],
                                                     rw1[:, s1_]))
            k.wait("V", k.V, vv)
            vv = k.V.inc(nc.vector.tensor_add(wdT[:, s8], wdT[:, s8],
                                              msk2[:, s8]))
        topv = k.V.val
        k.wait("T", k.V, topv)
        for tt in range(ntile):
            tv = k.T.inc(nc.tensor.transpose(pstr[:, :],
                                             selT[:, tt * 8:(tt + 1) * 8],
                                             idf[:, :]))
            k.wait("V", k.T, tv)
            vv = k.V.inc(nc.vector.tensor_copy(selE[:, tt * 128:(tt + 1) * 128],
                                               pstr[:, :]))
            k.wait("T", k.V, vv)
            tv = k.T.inc(nc.tensor.transpose(pstr2[:, :],
                                             wdT[:, tt * 8:(tt + 1) * 8],
                                             idf[:, :]))
            k.wait("V", k.T, tv)
            vv = k.V.inc(nc.vector.tensor_copy(wdE[:, tt * 128:(tt + 1) * 128],
                                               pstr2[:, :]))
            k.wait("T", k.V, vv)
        k.wait("V", k.V, k.V.val)
        vv = k.V.inc(nc.vector.tensor_tensor_scan(csum[:, :], ones8[:, :],
                                                  selE[:, :], 0.0, ALU.mult,
                                                  ALU.add))
        k.wait("V", k.V, vv)
        k.V.inc(nc.vector.tensor_copy(counts[:, 0:1], csum[:, T - 1:T]))
        k.barrier()


def emit_weff(k, d, route_sb, T):
    nc, es = k.nc, k.es
    weff = route_sb["weff"]
    selE, wdE, csum = route_sb["selE"], route_sb["wdE"], route_sb["csum"]
    with ExitStack() as ls:
        ca = ls.enter_context(nc.sbuf_tensor("wf_ca", [8, 8], F32))
        pm = ls.enter_context(nc.sbuf_tensor("wf_pm", [8, 1], F32))
        off = ls.enter_context(nc.sbuf_tensor("wf_off", [8, 1], F32))
        psoff = ls.enter_context(nc.psum_tensor("psoff", [8, 1], F32))
        k.dma("Y", ca[:, :], d["counts_all"])
        k.dma("Y", pm[:, :], d["prevmask"])
        ldy = k.Y.val
        k.wait("T", k.Y, ldy)
        tv = k.T.inc(nc.tensor.matmul(psoff[:, :], ca[:, :], pm[:, :],
                                      start=True, stop=True))
        k.wait("V", k.T, tv)
        vv = k.V.inc(nc.vector.tensor_copy(off[:, 0:1], psoff[:, :]))
        k.wait("V", k.V, vv)
        vv = k.V.inc(nc.vector.tensor_sub(weff[:, :], csum[:, :], selE[:, :]))
        k.wait("V", k.V, vv)
        vv = k.V.inc(nc.vector.tensor_scalar_add(weff[:, :], weff[:, :],
                                                 off[:, 0:1]))
        k.wait("V", k.V, vv)
        vv = k.V.inc(nc.vector.tensor_scalar(weff[:, :], weff[:, :], CAP, None,
                                             op0=ALU.is_lt))
        k.wait("V", k.V, vv)
        vv = k.V.inc(nc.vector.tensor_mul(weff[:, :], weff[:, :], selE[:, :]))
        k.wait("V", k.V, vv)
        k.V.inc(nc.vector.tensor_mul(weff[:, :], weff[:, :], wdE[:, :]))
        k.barrier()


def emit_mm_phase(k, d_wt, n_ob, K, rhs_sb, rhs_T, wb, psP, evac,
                  kc0=0, tok0=0, n_tcol=None):
    nc = k.nc
    nkc = K // 128
    if n_tcol is None:
        n_tcol = rhs_T // 512
    nbuf = len(wb)
    k.wait("T", k.V, k.V.val)
    k.wait("T", k.S, k.S.val)
    war = [(k.T, k.T.val)] * nbuf
    ps_war = [None, None]
    for ob in range(n_ob):
        b = ob % nbuf
        if war[b] is not None:
            ctr, v = war[b]
            k.wait("G", ctr, v)
        ld = k.dma("G", wb[b][:, 0:K], d_wt[ob][:, kc0 * 128:kc0 * 128 + K],
                   ctr=k.GW[b % 2])
        k.wait("T", k.GW[b % 2], ld)
        for t in range(n_tcol):
            pb = (ob * n_tcol + t) % 2
            ps_ = psP[pb]
            if ps_war[pb] is not None:
                ctr, v = ps_war[pb]
                k.wait("T", ctr, v)
            for kc in range(nkc):
                ins = nc.tensor.matmul(
                    ps_[:, :], wb[b][:, kc * 128:(kc + 1) * 128],
                    rhs_sb[:, kc * rhs_T + tok0 + t * 512:
                           kc * rhs_T + tok0 + (t + 1) * 512],
                    start=(kc == 0), stop=(kc == nkc - 1))
            tv = k.T.inc(ins)
            ps_war[pb] = evac(ob, t, ps_, tv)
        war[b] = (k.T, k.T.val)


CLOC = 1024


def emit_moe_routed(k, Xb, d, ones128, ones1, idf, rankI, weffT, T):
    nc, es = k.nc, k.es
    ncol = T // 512
    ntile = T // 128
    k.barrier()
    bregT = nc.gpsimd.to_reg(T - 1)
    bregC = nc.gpsimd.to_reg(CLOC - 1)
    with ExitStack() as ls:
        wgu = [ls.enter_context(nc.sbuf_tensor(f"m_wgu{i}", [128, D], BF16))
               for i in range(2)]
        iotaF = ls.enter_context(nc.sbuf_tensor("m_iota", [128, ntile], F32))
        bigF = ls.enter_context(nc.sbuf_tensor("m_big", [128, 8], F32))
        zero1k = ls.enter_context(nc.sbuf_tensor("m_z", [128, 1024], F32))
        pairt = [ls.enter_context(nc.sbuf_tensor(f"m_pt{i}", [128, 2], F32))
                 for i in range(2)]
        dispbF = ls.enter_context(nc.sbuf_tensor("m_dbf", [128, 1], F32))
        xrow = [ls.enter_context(nc.sbuf_tensor(f"m_xr{i}", [128, 1024], BF16))
                for i in range(2)]
        eo = [ls.enter_context(nc.sbuf_tensor(f"m_eo{i}", [128, 1024], F32))
              for i in range(2)]
        dispb = [ls.enter_context(nc.sbuf_tensor(f"m_db{i}", [128, 8], I32))
                 for i in range(2)]
        wdsp = [ls.enter_context(nc.sbuf_tensor(f"m_wd{i}", [128, 8], F32))
                for i in range(2)]
        XD = [ls.enter_context(nc.sbuf_tensor(f"m_xd{i}", [128, 8 * CLOC],
                                              BF16)) for i in range(2)]
        xg = ls.enter_context(nc.sbuf_tensor("m_xg", [128, 8 * 1024], BF16))
        He = ls.enter_context(nc.sbuf_tensor("m_he", [128, 16 * CLOC], BF16))
        rdres = ls.enter_context(nc.sbuf_tensor("m_rd", [128, 16 * 1024], BF16))
        psP = [ls.enter_context(nc.psum_tensor(f"m_ps{i}", [128, 512], F32))
               for i in range(2)]
        psTb = ls.enter_context(nc.psum_tensor("m_ptb", [128, 128], BF16))

        k.P.inc(nc.gpsimd.iota(iotaF[:, :], pattern=[[128, ntile]], base=0,
                               channel_multiplier=1,
                               allow_small_or_imprecise_dtypes=True))
        k.P.inc(nc.gpsimd.iota(bigF[:, :], pattern=[[0, 8]], base=1000000,
                               channel_multiplier=0,
                               allow_small_or_imprecise_dtypes=True))
        k.V.inc(nc.vector.memset(zero1k[:, :], 0.0))
        k.V.inc(nc.vector.memset(xg[:, :], 0.0))
        k.barrier()

        for tt in range(ntile):
            b = tt % 2
            k.wait("S", k.YW[b], k.YW[b].val)
            for c in range(NC8):
                k.wait("T", k.S, k.S.val)
                tv = k.T.inc(nc.tensor.transpose(
                    psTb[:, :], Xb[:, c * T + tt * 128:c * T + (tt + 1) * 128],
                    k.idb[:, :]))
                k.wait("S", k.T, tv)
                k.S.inc(nc.scalar.copy(xrow[b][:, c * 128:(c + 1) * 128],
                                       psTb[:, :]))
            k.wait("Y", k.S, k.S.val)
            k.wait("Y", k.V, k.V.val)
            k.dma("Y", d["x2tm"][tt * 128:(tt + 1) * 128, :], xrow[b][:, :],
                  ctr=k.YW[b])
            k.dma("Y", d["routedtm"][tt * 128:(tt + 1) * 128, :], zero1k[:, :])
        k.barrier()

        def prep_expert(ex, pe):
            k.wait("G", k.P, k.P.val)
            k.wait("G", k.V, k.V.val)
            k.dma("G", d["paird"][ex * CLOC:(ex + 1) * CLOC, 0:1],
                  bigF[:, :], allow_slow_non_contiguous=True)
            k.dma("G", d["paird"][ex * CLOC:(ex + 1) * CLOC, 1:2],
                  zero1k[:, 0:8], allow_slow_non_contiguous=True)
            k.wait("G", k.G, k.G.val)
            for tt in range(ntile):
                bx = tt % 2
                k.wait("V", k.GX[0], k.GX[0].val)
                k.wait("V", k.GX[1], k.GX[1].val)
                k.V.inc(nc.vector.tensor_copy(pairt[bx][:, 0:1],
                                              iotaF[:, tt:tt + 1]))
                vv = k.V.inc(nc.vector.tensor_copy(
                    pairt[bx][:, 1:2],
                    weffT[:, tt * 8 + ex:tt * 8 + ex + 1]))
                k.wait("G", k.V, vv)
                ins = nc.gpsimd.indirect_dma_start(
                    out=d["paird"],
                    out_offset=bass.IndirectOffsetOnAxis(
                        ap=rankI[:, tt * 8 + ex:tt * 8 + ex + 1], axis=0),
                    in_=pairt[bx][:, :], in_offset=None,
                    element_offset=ex * CLOC * 2,
                    bounds_check=bregC, oob_is_err=False)
                k.GX[bx].inc(ins, 16)
            k.wait("Y", k.GX[0], k.GX[0].val)
            k.wait("Y", k.GX[1], k.GX[1].val)
            for st in range(8):
                k.wait("Y", k.V, k.V.val)
                k.dma("Y", dispbF[:, 0:1],
                      d["paird"][ex * CLOC + st * 128:
                                 ex * CLOC + (st + 1) * 128, 0:1],
                      allow_slow_non_contiguous=True)
                k.wait("V", k.Y, k.Y.val)
                vv = k.V.inc(nc.vector.tensor_copy(dispb[pe][:, st:st + 1],
                                                   dispbF[:, 0:1]))
                k.dma("Y", wdsp[pe][:, st:st + 1],
                      d["paird"][ex * CLOC + st * 128:
                                 ex * CLOC + (st + 1) * 128, 1:2],
                      allow_slow_non_contiguous=True)
            k.wait("G", k.V, k.V.val)
            k.wait("G", k.T, k.T.val)
            gx_evt = [0, 0]
            for st in range(8):
                bx = st % 2
                ins = nc.gpsimd.indirect_dma_start(
                    out=xg[:, st * 1024:(st + 1) * 1024],
                    out_offset=None, in_=d["x2tm"],
                    in_offset=bass.IndirectOffsetOnAxis(
                        ap=dispb[pe][:, st:st + 1], axis=0),
                    bounds_check=bregT, oob_is_err=False)
                gx_evt[bx] = k.GX[bx].inc(ins, 16)
            k.wait("T", k.GX[0], gx_evt[0])
            k.wait("T", k.GX[1], gx_evt[1])
            for st in range(8):
                for c in range(NC8):
                    k.wait("T", k.S, k.S.val)
                    tv = k.T.inc(nc.tensor.transpose(
                        psTb[:, :],
                        xg[:, st * 1024 + c * 128:st * 1024 + (c + 1) * 128],
                        k.idb[:, :]))
                    k.wait("S", k.T, tv)
                    k.S.inc(nc.scalar.copy(
                        XD[pe][:, c * CLOC + st * 128:
                               c * CLOC + (st + 1) * 128], psTb[:, :]))

        prep_expert(0, 0)
        for ex in range(NEXP):
            pe = ex % 2

            def evac_gate(ob, t, ps_, tv):
                k.wait("S", k.T, tv)
                ins = nc.scalar.activation(
                    He[:, ob * CLOC + t * 512:ob * CLOC + (t + 1) * 512],
                    ps_[:, :], AF.Silu)
                return (k.S, k.S.inc(ins))

            def evac_up(ob, t, ps_, tv):
                sl = slice(ob * CLOC + t * 512, ob * CLOC + (t + 1) * 512)
                k.wait("V", k.T, tv)
                k.wait("V", k.S, k.S.val)
                k.wait("V", k.V, k.V.val)
                ins = nc.vector.tensor_mul(He[:, sl], He[:, sl], ps_[:, :])
                return (k.V, k.V.inc(ins))

            for half in range(2):
                emit_mm_phase(k, d["rg"][ex][half * 16:(half + 1) * 16],
                              16, D, XD[pe], CLOC, wgu, psP, evac_gate)
                emit_mm_phase(k, d["ru"][ex][half * 16:(half + 1) * 16],
                              16, D, XD[pe], CLOC, wgu, psP, evac_up)
                if half == 0 and ex + 1 < NEXP:
                    prep_expert(ex + 1, 1 - pe)
                k.wait("G", k.T, k.T.val)
                ldrd = k.dma("G", rdres.rearrange("p (a n) -> p a n", a=16),
                             d["rdk"][ex][half * 16:(half + 1) * 16]
                             .rearrange("a p n -> p a n"), ctr=k.GW[0])
                k.wait("T", k.GW[0], ldrd)
                k.wait("T", k.S, k.S.val)
                k.wait("T", k.V, k.V.val)
                k.wait("V", k.Y, k.Y.val)
                for st in range(8):
                    b = st % 2
                    k.wait("V", k.GX[b], k.GX[b].val)
                    k.wait("T", k.V, k.V.val)
                    for oc2 in range(2):
                        for kc in range(16):
                            ins = nc.tensor.matmul(
                                psP[oc2][:, :],
                                He[:, kc * CLOC + st * 128:
                                   kc * CLOC + (st + 1) * 128],
                                rdres[:, kc * 1024 + oc2 * 512:
                                      kc * 1024 + oc2 * 512 + 512],
                                start=(kc == 0), stop=(kc == 15))
                        tv = k.T.inc(ins)
                        k.wait("V", k.T, tv)
                        vv = k.V.inc(nc.vector.tensor_scalar_mul(
                            eo[b][:, oc2 * 512:(oc2 + 1) * 512],
                            psP[oc2][:, :], wdsp[pe][:, st:st + 1]))
                    k.wait("G", k.V, vv)
                    ins = nc.gpsimd.indirect_dma_start(
                        out=d["routedtm"],
                        out_offset=bass.IndirectOffsetOnAxis(
                            ap=dispb[pe][:, st:st + 1], axis=0),
                        in_=eo[b][:, :], in_offset=None,
                        bounds_check=bregT, oob_is_err=False,
                        compute_op=ALU.add)
                    k.GX[b].inc(ins, 16)
    k.barrier()



def emit_moe_shared(k, X, Xb, H, d, ones128, ones1, idf, T):
    nc, es = k.nc, k.es
    ntile = T // 128
    k.barrier()
    with ExitStack() as ls:
        wgu = [ls.enter_context(nc.sbuf_tensor(f"s_wgu{i}", [128, D], BF16))
               for i in range(2)]
        wdn = [ls.enter_context(nc.sbuf_tensor(f"s_wdn{i}", [128, 2048], BF16))
               for i in range(2)]
        psP = [ls.enter_context(nc.psum_tensor(f"s_ps{i}", [128, 512], F32))
               for i in range(2)]
        with ExitStack() as sr:
            rin = [sr.enter_context(nc.sbuf_tensor(f"m_ri{i}", [128, 1024],
                                                   F32)) for i in range(2)]
            psTf = sr.enter_context(nc.psum_tensor("m_ptf", [128, 128], F32))
            for tt in range(ntile):
                b = tt % 2
                k.wait("G", k.T, k.T.val)
                k.dma("G", rin[b][:, :], d["routedtm"][tt * 128:(tt + 1) * 128, :])
                k.wait("T", k.G, k.G.val)
                for c in range(NC8):
                    k.wait("T", k.V, k.V.val)
                    tv = k.T.inc(nc.tensor.transpose(
                        psTf[:, :], rin[b][:, c * 128:(c + 1) * 128],
                        idf[:, :]))
                    k.wait("V", k.T, tv)
                    sl = slice(c * T + tt * 128, c * T + (tt + 1) * 128)
                    vv = k.V.inc(nc.vector.tensor_add(X[:, sl], X[:, sl],
                                                      psTf[:, :]))
        k.barrier()

        def evac_gate_s(ob, t, ps_, tv):
            k.wait("S", k.T, tv)
            ins = nc.scalar.activation(
                H[:, ob * T + t * 512:ob * T + (t + 1) * 512], ps_[:, :],
                AF.Silu)
            return (k.S, k.S.inc(ins))

        def evac_up_s(ob, t, ps_, tv):
            sl = slice(ob * T + t * 512, ob * T + (t + 1) * 512)
            k.wait("V", k.T, tv)
            k.wait("V", k.S, k.S.val)
            k.wait("V", k.V, k.V.val)
            ins = nc.vector.tensor_mul(H[:, sl], H[:, sl], ps_[:, :])
            return (k.V, k.V.inc(ins))

        def evac_down_s(ob, t, ps_, tv):
            sl = slice(ob * T + t * 512, ob * T + (t + 1) * 512)
            k.wait("V", k.T, tv)
            k.wait("V", k.V, k.V.val)
            ins = nc.vector.tensor_add(X[:, sl], X[:, sl], ps_[:, :])
            return (k.V, k.V.inc(ins))

        for q in range(4):
            emit_mm_phase(k, d["sg"][q * 16:(q + 1) * 16], 16, D, Xb, T,
                          wgu, psP, evac_gate_s)
            emit_mm_phase(k, d["su"][q * 16:(q + 1) * 16], 16, D, Xb, T,
                          wgu, psP, evac_up_s)
            emit_mm_phase(k, d["sd_h"][q // 2], NC8, 2048, H, T, wdn, psP,
                          evac_down_s, kc0=(q % 2) * 16)
    k.barrier()



def emit_l1_attn(k, X, Xb, d, ones128, ones1, T):
    nc, es = k.nc, k.es
    ncol = T // 512
    ntile = T // 128
    k.barrier()

    def norm_pipeline(src64, sl, ps1, nq1, psB, SQ64):
        k.wait("S", k.S, k.S.val)
        k.wait("S", k.T, k.T.val)
        k.wait("S", k.V, k.V.val)
        sv = k.S.inc(nc.scalar.activation(SQ64[:, :], src64[:, sl], AF.Square))
        k.wait("T", k.S, sv)
        tv = k.T.inc(nc.tensor.matmul(ps1[0:1, :], ones128[0:64, 0:1],
                                      SQ64[:, :], start=True, stop=True))
        k.wait("V", k.T, tv)
        vv = k.V.inc(nc.vector.tensor_copy(nq1[0:1, :], ps1[0:1, :]))
        k.wait("S", k.V, vv)
        sv = k.S.inc(nc.scalar.activation(nq1[0:1, :], nq1[0:1, :], AF.Ln))
        k.wait("S", k.S, sv)
        sv = k.S.inc(nc.scalar.activation(nq1[0:1, :], nq1[0:1, :], AF.Exp,
                                          scale=k.cp(7, 1)))
        k.wait("V", k.S, sv)
        vv = k.V.inc(nc.vector.tensor_scalar_add(nq1[0:1, :], nq1[0:1, :], 1e-6))
        k.wait("V", k.V, vv)
        vv = k.V.inc(nc.vector.reciprocal(nq1[0:1, :], nq1[0:1, :]))
        k.wait("T", k.V, vv)
        k.wait("T", k.T, k.T.val)
        tv = k.T.inc(nc.tensor.matmul(psB[:, :], ones1[0:1, 0:64], nq1[0:1, :],
                                      start=True, stop=True))
        k.wait("V", k.T, tv)
        vv = k.V.inc(nc.vector.tensor_mul(src64[:, sl], src64[:, sl], psB[:, :]))
        k.wait("T", k.V, vv)

    def rope_pipeline(src64, sl, rot, cosb, sinb, psR, tbf):
        tv = k.T.inc(nc.tensor.matmul(psR[0:64, :], rot[:, :], src64[:, sl],
                                      start=True, stop=True))
        k.wait("V", k.T, tv)
        vv = k.V.inc(nc.vector.tensor_mul(tbf[:, :], src64[:, sl], cosb[:, :]))
        k.wait("V", k.V, vv)
        vv = k.V.inc(nc.vector.tensor_mul(src64[:, sl], psR[0:64, :], sinb[:, :]))
        k.wait("V", k.V, vv)
        vv = k.V.inc(nc.vector.tensor_add(src64[:, sl], src64[:, sl], tbf[:, :]))
        k.wait("T", k.V, vv)

    with ExitStack() as ls:
        wkv = ls.enter_context(nc.sbuf_tensor("a_wkv", [128, 8 * 512], BF16))
        wbuf = [ls.enter_context(nc.sbuf_tensor(f"a_wb{i}", [128, 1024], BF16))
                for i in range(2)]
        K64 = ls.enter_context(nc.sbuf_tensor("a_k", [64, 4 * T], BF16))
        Vt = ls.enter_context(nc.sbuf_tensor("a_v", [128, ntile * 260], BF16))
        Q64 = ls.enter_context(nc.sbuf_tensor("a_q", [64, 16 * 512], BF16))
        st2 = [ls.enter_context(nc.sbuf_tensor(f"a_st{i}", [128, 8 * 512], BF16))
               for i in range(2)]
        SQ64 = ls.enter_context(nc.sbuf_tensor("a_sq", [64, 512], F32))
        nq1 = ls.enter_context(nc.sbuf_tensor("a_nq1", [1, 512], F32))
        rdn = [ls.enter_context(nc.sbuf_tensor(f"a_rd{i}", [1, 512], F32))
               for i in range(2)]
        cosb = ls.enter_context(nc.sbuf_tensor("a_cos", [64, 512], BF16))
        sinb = ls.enter_context(nc.sbuf_tensor("a_sin", [64, 512], BF16))
        rot = ls.enter_context(nc.sbuf_tensor("a_rot", [64, 64], BF16))
        tbf = ls.enter_context(nc.sbuf_tensor("a_tbf", [64, 512], BF16))
        ybf = [ls.enter_context(nc.sbuf_tensor(f"a_ybf{i}", [64, 512], BF16))
               for i in range(2)]

        for kc in range(8):
            k.dma("G", wkv[:, kc * 512:(kc + 1) * 512],
                  d["wkvT"][kc * 128:(kc + 1) * 128, :])
        k.dma("G", rot[:, :], d["rot64"])
        ldg = k.G.val
        k.wait("T", k.G, ldg)
        k.wait("T", k.V, k.V.val)
        k.wait("T", k.S, k.S.val)
        k.wait("V", k.G, ldg)

        with ExitStack() as hs:
            psPd = hs.enter_context(nc.psum_tensor("atPd", [128, 1024], F32))
            psP = [hs.enter_context(nc.psum_tensor(f"atA{i}", [128, 512], F32))
                   for i in range(2)]
            ps1 = hs.enter_context(nc.psum_tensor("at1", [1, 512], F32))
            psB = hs.enter_context(nc.psum_tensor("atB", [64, 512], F32))
            psPV = [hs.enter_context(nc.psum_tensor(f"atPV{i}", [65, 512], F32))
                    for i in range(2)]

            for g in range(NKV):
                for t in range(ncol):
                    pb = (g * ncol + t) % 2
                    k.wait("T", k.S, k.S.val)
                    for kc in range(8):
                        ins = nc.tensor.matmul(
                            psP[pb][0:64, :],
                            wkv[:, kc * 512 + g * 64:kc * 512 + (g + 1) * 64],
                            Xb[:, kc * T + t * 512:kc * T + (t + 1) * 512],
                            start=(kc == 0), stop=(kc == 7))
                    tv = k.T.inc(ins)
                    k.wait("S", k.T, tv)
                    k.S.inc(nc.scalar.copy(
                        K64[:, g * T + t * 512:g * T + (t + 1) * 512],
                        psP[pb][0:64, :]))
            k.V.inc(nc.vector.memset(Vt[:, :], 1.0))
            k.wait("T", k.V, k.V.val)
            for tt in range(ntile):
                pb = tt % 2
                k.wait("T", k.V, k.V.val)
                k.wait("T", k.S, k.S.val)
                for kc in range(8):
                    ins = nc.tensor.matmul(
                        psP[pb][:, 0:256],
                        Xb[:, kc * T + tt * 128:kc * T + (tt + 1) * 128],
                        wkv[:, kc * 512 + 256:kc * 512 + 512],
                        start=(kc == 0), stop=(kc == 7))
                tv = k.T.inc(ins)
                k.wait("V", k.T, tv)
                for g in range(NKV):
                    k.V.inc(nc.vector.tensor_copy(
                        Vt[:, tt * 260 + g * 65:tt * 260 + g * 65 + 64],
                        psP[pb][:, g * 64:(g + 1) * 64]))

            for t in range(ncol):
                k.wait("G", k.V, k.V.val)
                k.wait("G", k.T, k.T.val)
                k.dma("G", cosb[:, :], d["cos64"][:, t * 512:(t + 1) * 512])
                k.dma("G", sinb[:, :], d["sin64"][:, t * 512:(t + 1) * 512])
                k.wait("V", k.G, k.G.val)
                k.wait("T", k.G, k.G.val)
                for g in range(NKV):
                    sl = slice(g * T + t * 512, g * T + (t + 1) * 512)
                    norm_pipeline(K64, sl, ps1, nq1, psB, SQ64)
                    rope_pipeline(K64, sl, rot, cosb, sinb, psP[g % 2], tbf)

            for tq in range(ncol):
                ts = slice(tq * 512, (tq + 1) * 512)
                k.wait("G", k.V, k.V.val)
                k.wait("G", k.T, k.T.val)
                k.dma("G", cosb[:, :], d["cos64"][:, ts])
                k.dma("G", sinb[:, :], d["sin64"][:, ts])
                k.wait("V", k.G, k.G.val)
                k.wait("T", k.G, k.G.val)
                for oc in range(NC8):
                    b = oc % 2
                    k.wait("G", k.T, k.T.val)
                    k.dma("G", wbuf[b][:, :], d["wqT_t"][oc], ctr=k.GW[b])
                    k.wait("T", k.GW[b], k.GW[b].val)
                    for par in range(2):
                        h = 2 * oc + par
                        pb = h % 2
                        k.wait("T", k.S, k.S.val)
                        for kc in range(8):
                            ins = nc.tensor.matmul(
                                psP[pb][0:64, :],
                                wbuf[b][:, kc * 128 + par * 64:
                                        kc * 128 + par * 64 + 64],
                                Xb[:, kc * T + tq * 512:kc * T + (tq + 1) * 512],
                                start=(kc == 0), stop=(kc == 7))
                        tv = k.T.inc(ins)
                        k.wait("S", k.T, tv)
                        k.S.inc(nc.scalar.copy(Q64[:, h * 512:(h + 1) * 512],
                                               psP[pb][0:64, :]))
                for h in range(NHEAD):
                    sl = slice(h * 512, (h + 1) * 512)
                    norm_pipeline(Q64, sl, ps1, nq1, psB, SQ64)
                    rope_pipeline(Q64, sl, rot, cosb, sinb, psP[h % 2], tbf)

                exp_evt = [0, 0]
                pv_evt = [0, 0]
                msk_evt = [0, 0]
                evac_evt = [0, 0]
                for h in range(NHEAD):
                    g = h // 4
                    hb = h % 2
                    stb = st2[hb]
                    subs = [(j, tq * 512 - 512 + j * 128) for j in range(8)
                            if tq * 512 - 512 + j * 128 >= 0]
                    for i0 in range(0, len(subs), 2):
                        pair = subs[i0:i0 + 2]
                        for jj, (j, tkb) in enumerate(pair):
                            ins = nc.tensor.matmul(
                                psPd[:, jj * 512:(jj + 1) * 512],
                                K64[:, g * T + tkb:g * T + tkb + 128],
                                Q64[:, h * 512:(h + 1) * 512],
                                start=True, stop=True)
                        tv = k.T.inc(ins)
                        k.wait("S", k.T, tv)
                        k.wait("S", k.T, pv_evt[hb])
                        w = 512 * len(pair)
                        j0 = pair[0][0]
                        ins = nc.scalar.activation(
                            stb[:, j0 * 512:j0 * 512 + w],
                            psPd[:, 0:w], AF.Exp, scale=k.cp(6))
                        sv = k.S.inc(ins)
                        exp_evt[hb] = sv
                        k.wait("T", k.S, sv)
                        k.wait("P", k.S, sv)
                        k.wait("P", k.T, pv_evt[hb])
                        for (j, tkb) in pair:
                            if j < 4:
                                ins = nc.gpsimd.affine_select(
                                    stb[:, j * 512:(j + 1) * 512],
                                    stb[:, j * 512:(j + 1) * 512],
                                    pattern=[[-1, 512]], compare_op=ALU.is_ge,
                                    fill=0.0, base=511 - tq * 512 + tkb,
                                    channel_multiplier=1)
                            else:
                                ins = nc.gpsimd.affine_select(
                                    stb[:, j * 512:(j + 1) * 512],
                                    stb[:, j * 512:(j + 1) * 512],
                                    pattern=[[1, 512]], compare_op=ALU.is_ge,
                                    fill=0.0, base=tq * 512 - tkb,
                                    channel_multiplier=-1)
                            msk_evt[hb] = k.P.inc(ins)
                    k.wait("T", k.P, msk_evt[hb])
                    k.wait("T", k.V, evac_evt[hb])
                    for i, (j, tkb) in enumerate(subs):
                        tt = tkb // 128
                        ins = nc.tensor.matmul(
                            psPV[hb][:, :],
                            Vt[:, tt * 260 + g * 65:tt * 260 + (g + 1) * 65],
                            stb[:, j * 512:(j + 1) * 512], start=(i == 0),
                            stop=(i == len(subs) - 1))
                    tv = k.T.inc(ins)
                    pv_evt[hb] = tv
                    k.wait("V", k.T, tv)
                    vv = k.V.inc(nc.vector.reciprocal(rdn[hb][0:1, :],
                                                      psPV[hb][64:65, :]))
                    k.wait("T", k.V, vv)
                    bv = k.T.inc(nc.tensor.matmul(psB[:, :], ones1[0:1, 0:64],
                                                  rdn[hb][0:1, :], start=True,
                                                  stop=True))
                    k.wait("S", k.T, tv)
                    k.wait("S", k.GX[hb], k.GX[hb].val)
                    sv = k.S.inc(nc.scalar.copy(ybf[hb][:, :],
                                                psPV[hb][0:64, :]))
                    k.wait("V", k.S, sv)
                    k.wait("V", k.T, bv)
                    k.wait("V", k.GX[hb], k.GX[hb].val)
                    vv = k.V.inc(nc.vector.tensor_mul(ybf[hb][:, :],
                                                      ybf[hb][:, :],
                                                      psB[0:64, :]))
                    evac_evt[hb] = vv
                    k.wait("G", k.V, vv)
                    k.dma("G", d["yf"][h * 64:(h + 1) * 64, ts], ybf[hb][:, :],
                          ctr=k.GX[hb])
        k.barrier()

    with ExitStack() as os_:
        ybin = [os_.enter_context(nc.sbuf_tensor(f"a_yb{i}", [64, 512], BF16))
                for i in range(2)]
        wp8 = os_.enter_context(nc.sbuf_tensor("a_wp8", [64, 8 * 2048], BF16))
        psO = os_.enter_context(nc.psum_tensor("a_psO", [128, 4096], F32))
        for oc in range(NC8):
            k.dma("G", wp8[:, oc * 2048:(oc + 1) * 2048], d["wp64"][oc])
        k.wait("T", k.G, k.G.val)
        ld_evt = [0, 0]
        for t in range(ncol):
            for h in range(NHEAD):
                b = h % 2
                k.wait("G", k.T, k.T.val)
                k.dma("G", ybin[b][:, :],
                      d["yf"][h * 64:(h + 1) * 64, t * 512:(t + 1) * 512],
                      ctr=k.GX[b])
                ld_evt[b] = k.GX[b].val
                k.wait("T", k.GX[b], ld_evt[b])
                if h == 0:
                    k.wait("T", k.V, k.V.val)
                for oc in range(NC8):
                    ins = nc.tensor.matmul(
                        psO[:, oc * 512:(oc + 1) * 512],
                        wp8[:, oc * 2048 + h * 128:oc * 2048 + (h + 1) * 128],
                        ybin[b][:, :],
                        start=(h == 0), stop=(h == NHEAD - 1))
                k.T.inc(ins)
            tv = k.T.val
            k.wait("V", k.T, tv)
            for oc in range(NC8):
                sl = slice(oc * T + t * 512, oc * T + (t + 1) * 512)
                k.V.inc(nc.vector.tensor_add(X[:, sl], X[:, sl],
                                             psO[:, oc * 512:(oc + 1) * 512]))
    k.barrier()



def emit_l1_ffn_w12(k, Xb, H, d, wbig, psP, T, tok0, Th):
    nc = k.nc
    s1 = k.ffn_scales["s1"]
    s2 = k.ffn_scales["s2"]

    def evac_w1(ob, t, ps_, tv):
        k.wait("S", k.T, tv)
        ins = nc.scalar.activation(H[:, ob * Th + t * 512:ob * Th + (t + 1) * 512],
                                   ps_[:, :], AF.Silu, scale=s1[:, ob:ob + 1])
        return (k.S, k.S.inc(ins))
    emit_mm_phase(k, d["w1T"], 32, D, Xb, T, wbig, psP, evac_w1, tok0=tok0,
                  n_tcol=Th // 512)

    def evac_w2(ob, t, ps_, tv):
        sl = slice(ob * Th + t * 512, ob * Th + (t + 1) * 512)
        k.wait("V", k.T, tv)
        k.wait("V", k.S, k.S.val)
        k.wait("V", k.V, k.V.val)
        ins = nc.vector.tensor_scalar_mul(k.ffn_sq[:, 0:512], ps_[:, :],
                                          s2[:, ob:ob + 1])
        vv = k.V.inc(ins)
        k.wait("V", k.V, vv)
        ins = nc.vector.tensor_mul(H[:, sl], H[:, sl], k.ffn_sq[:, 0:512])
        return (k.V, k.V.inc(ins))
    emit_mm_phase(k, d["w2T"], 32, D, Xb, T, wbig, psP, evac_w2, tok0=tok0,
                  n_tcol=Th // 512)


def emit_l1_ffn_h(k, Hin, Hout, d_w, scale_col, wbig, psP, Th):
    nc = k.nc

    def evac(ob, t, ps_, tv):
        k.wait("S", k.T, tv)
        k.wait("S", k.V, k.V.val)
        ins = nc.scalar.activation(Hout[:, ob * Th + t * 512:ob * Th + (t + 1) * 512],
                                   ps_[:, :], AF.Gelu, scale=scale_col[:, ob:ob + 1])
        return (k.S, k.S.inc(ins))
    emit_mm_phase(k, d_w, 32, DH, Hin, Th, wbig, psP, evac)


def emit_l1_ffn_w3(k, Xn, H, d, wbig, psP, T, tok0, Th):
    nc = k.nc
    s3 = k.ffn_scales["s3"]

    def evac(ob, t, ps_, tv):
        sl = slice(ob * T + tok0 + t * 512, ob * T + tok0 + (t + 1) * 512)
        k.wait("V", k.T, tv)
        k.wait("V", k.S, k.S.val)
        k.wait("V", k.V, k.V.val)
        k.wait("V", k.Y, k.Y.val)
        ins = nc.vector.scalar_tensor_tensor(Xn[:, sl], ps_[:, :],
                                             s3[:, ob:ob + 1], Xn[:, sl],
                                             ALU.mult, ALU.add)
        return (k.V, k.V.inc(ins))
    emit_mm_phase(k, d["w3T"], NC8, DH, H, Th, wbig, psP, evac)


def declare_common(nc):
    d = {}
    d["scratch1"] = nc.dram_tensor("scratch1", [32, 8 * 2048], F32,
                                   kind="Internal").ap()
    return d


def build_launchA(T):
    nc = bass.Bass("TRN2", target_bir_lowering=False, debug=False)
    es = ExitStack()
    d = declare_common(nc)
    X_ = nc.dram_tensor
    d["x_in"] = X_("x_in", [D, T], F32, kind="ExternalInput").ap()
    d["w_in"] = X_("w_in", [D, 1072], F32, kind="ExternalInput").ap()
    d["bz"] = X_("bz", [128, 8], F32, kind="ExternalInput").ap()
    d["bbcd"] = X_("bbcd", [16, 3], F32, kind="ExternalInput").ap()
    d["alog"] = X_("alog", [16, 1], F32, kind="ExternalInput").ap()
    d["w_out"] = X_("w_out", [D, D], F32, kind="ExternalInput").ap()
    d["bout"] = X_("bout", [128, 8], F32, kind="ExternalInput").ap()
    d["rwT"] = X_("rwT", [D, 8], F32, kind="ExternalInput").ap()
    d["cpool"] = X_("cpool", [128, 8], F32, kind="ExternalInput").ap()
    d["x2_out"] = X_("x2_out", [D, T], F32, kind="ExternalOutput").ap()
    d["route_out"] = X_("route_out", [24, T], F32, kind="ExternalOutput").ap()
    d["counts_out"] = X_("counts_out", [8, 1], F32, kind="ExternalOutput").ap()
    k = Kb(nc, es)
    X = es.enter_context(nc.sbuf_tensor("X", [128, 8 * T], F32))
    Xb = es.enter_context(nc.sbuf_tensor("Xb", [128, 8 * T], BF16))
    ones128, ones1, idf = emit_consts(k, d)
    for c in range(NC8):
        k.dma("Y", X[:, c * T:(c + 1) * T],
                                  d["x_in"][c * 128:(c + 1) * 128, :])
    k.wait("V", k.Y, k.Y.val)
    for c in range(NC8):
        k.V.inc(nc.vector.tensor_copy(Xb[:, c * T:(c + 1) * T],
                                      X[:, c * T:(c + 1) * T]))
    emit_l0_attn(k, X, Xb, d, ones128, ones1, T)
    emit_ln(k, "0", X, Xb, ones128, ones1, T)
    route_sb = {
        "selE": es.enter_context(nc.sbuf_tensor("selE", [8, T], F32)),
        "wdE": es.enter_context(nc.sbuf_tensor("wdE", [8, T], F32)),
        "csum": es.enter_context(nc.sbuf_tensor("csum", [8, T], F32)),
        "counts": es.enter_context(nc.sbuf_tensor("counts", [8, 1], F32)),
    }
    emit_routing(k, X, Xb, d, ones128, ones1, idf, T, route_sb)
    for c in range(NC8):
        k.dma("Y", d["x2_out"][c * 128:(c + 1) * 128, :],
                                  X[:, c * T:(c + 1) * T])
    k.dma("Y", d["route_out"][0:8, :],
                              route_sb["selE"][:, :])
    k.dma("Y", d["route_out"][8:16, :],
                              route_sb["wdE"][:, :])
    k.dma("Y", d["route_out"][16:24, :],
                              route_sb["csum"][:, :])
    k.dma("Y", d["counts_out"][:, :],
                              route_sb["counts"][:, :])
    k.wait("Y", k.Y, k.Y.val)
    es.close()
    return nc


def build_launchB(T):
    nc = bass.Bass("TRN2", target_bir_lowering=False, debug=False)
    es = ExitStack()
    d = declare_common(nc)
    X_ = nc.dram_tensor
    d["x2_in"] = X_("x2_in", [D, T], F32, kind="ExternalInput").ap()
    d["route_in"] = X_("route_in", [24, T], F32, kind="ExternalInput").ap()
    d["counts_all"] = X_("counts_all", [8, 8], F32, kind="ExternalInput").ap()
    d["prevmask"] = X_("prevmask", [8, 1], F32, kind="ExternalInput").ap()
    d["rg"] = [X_(f"rg{e}", [IR // 128, 128, D], F32, kind="ExternalInput").ap()
               for e in range(NEXP)]
    d["ru"] = [X_(f"ru{e}", [IR // 128, 128, D], F32, kind="ExternalInput").ap()
               for e in range(NEXP)]
    d["sg"] = X_("sg", [ISH // 128, 128, D], F32, kind="ExternalInput").ap()
    d["su"] = X_("su", [ISH // 128, 128, D], F32, kind="ExternalInput").ap()
    d["sd_h"] = [X_(f"sd{h}", [NC8, 128, IR], F32, kind="ExternalInput").ap()
                 for h in range(2)]
    d["rdk"] = [X_(f"rdk{e}", [32, 128, 1024], F32, kind="ExternalInput").ap()
                for e in range(NEXP)]
    d["x2tm"] = X_("x2tm", [T, D], BF16, kind="Internal").ap()
    d["routedtm"] = X_("routedtm", [T, D], F32, kind="Internal").ap()
    d["paird"] = X_("paird", [NEXP * 1024, 2], F32, kind="Internal").ap()
    d["wqT_t"] = X_("wqT_t", [NC8, 128, D], F32, kind="ExternalInput").ap()
    d["wkvT"] = X_("wkvT", [D, 512], F32, kind="ExternalInput").ap()
    d["wp64"] = X_("wp64", [NC8, 64, 16 * 128], F32, kind="ExternalInput").ap()
    d["cos64"] = X_("cos64", [64, T], F32, kind="ExternalInput").ap()
    d["sin64"] = X_("sin64", [64, T], F32, kind="ExternalInput").ap()
    d["rot64"] = X_("rot64", [64, 64], F32, kind="ExternalInput").ap()
    d["yf"] = X_("yf", [D, T], BF16, kind="Internal").ap()
    d["w1T"] = X_("w1T", [32, 128, D], F32, kind="ExternalInput").ap()
    d["w2T"] = X_("w2T", [32, 128, D], F32, kind="ExternalInput").ap()
    d["h1T"] = X_("h1T", [32, 128, DH], F32, kind="ExternalInput").ap()
    d["h2T"] = X_("h2T", [32, 128, DH], F32, kind="ExternalInput").ap()
    d["w3T"] = X_("w3T", [8, 128, DH], F32, kind="ExternalInput").ap()
    d["s1"] = X_("s1", [128, 32], F32, kind="ExternalInput").ap()
    d["s2"] = X_("s2", [128, 32], F32, kind="ExternalInput").ap()
    d["h1s"] = X_("h1s", [128, 32], F32, kind="ExternalInput").ap()
    d["h2s"] = X_("h2s", [128, 32], F32, kind="ExternalInput").ap()
    d["s3"] = X_("s3", [128, 8], F32, kind="ExternalInput").ap()
    d["cpool"] = X_("cpool", [128, 8], F32, kind="ExternalInput").ap()
    d["xpark"] = X_("xpark", [D, T], F32, kind="Internal").ap()
    d["out"] = X_("out", [D, T], F32, kind="ExternalOutput").ap()

    k = Kb(nc, es)
    Xb = es.enter_context(nc.sbuf_tensor("Xb", [128, 8 * T], BF16))
    ones128, ones1, idf = emit_consts(k, d)
    k.ffn_scales = {}
    for name in ("s1", "s2", "h1s", "h2s", "s3"):
        t_ = es.enter_context(nc.sbuf_tensor(f"sc_{name}",
                                             [128, 8 if name == "s3" else 32],
                                             F32))
        k.dma("Y", t_[:, :], d[name])
        k.ffn_scales[name] = t_

    with ExitStack() as sx:
        X = sx.enter_context(nc.sbuf_tensor("X", [128, 8 * T], F32))
        for c in range(NC8):
            k.dma("Y", X[:, c * T:(c + 1) * T],
                                      d["x2_in"][c * 128:(c + 1) * 128, :])
        with ExitStack() as sw:
            weff_sb = sw.enter_context(nc.sbuf_tensor("weff", [8, T], F32))
            with ExitStack() as sr:
                route_sb = {
                    "selE": sr.enter_context(nc.sbuf_tensor("selE", [8, T], F32)),
                    "wdE": sr.enter_context(nc.sbuf_tensor("wdE", [8, T], F32)),
                    "csum": sr.enter_context(nc.sbuf_tensor("csum", [8, T], F32)),
                    "weff": weff_sb,
                }
                for i, key in enumerate(("selE", "wdE", "csum")):
                    k.dma("Y", 
                        route_sb[key][:, :],
                        d["route_in"][i * 8:(i + 1) * 8, :])
                k.wait("V", k.Y, k.Y.val)
                for c in range(NC8):
                    k.V.inc(nc.vector.tensor_copy(Xb[:, c * T:(c + 1) * T],
                                                  X[:, c * T:(c + 1) * T]))
                emit_weff(k, d, route_sb, T)
            with ExitStack() as sm:
                H = sm.enter_context(nc.sbuf_tensor("Hh", [128, 16 * T], BF16))
                emit_moe(k, X, Xb, H, d, {"weff": weff_sb}, ones128, ones1, T)
        emit_ln(k, "1", X, Xb, ones128, ones1, T)
        emit_l1_attn(k, X, Xb, d, ones128, ones1, T)
        emit_ln(k, "2", X, Xb, ones128, ones1, T)
        k.barrier()
        for c in range(NC8):
            k.dma("Y", d["xpark"][c * 128:(c + 1) * 128, :],
                                      X[:, c * T:(c + 1) * T])
        k.wait("Y", k.Y, k.Y.val)
        k.barrier()
    with ExitStack() as sf:
        Th = T // 2
        wbig = [sf.enter_context(nc.sbuf_tensor(f"f_wb{i}", [128, DH], BF16))
                for i in range(2)]
        k.ffn_sq = sf.enter_context(nc.sbuf_tensor("f_sq", [128, 512], F32))
        for halft in range(2):
            tok0 = halft * Th
            with ExitStack() as sh_:
                H = sh_.enter_context(nc.sbuf_tensor(f"Hf{halft}",
                                                     [128, 32 * Th], BF16))
                with ExitStack() as s3_:
                    H2 = s3_.enter_context(nc.sbuf_tensor(f"Hf2{halft}",
                                                          [128, 32 * Th], BF16))
                    with ExitStack() as ps:
                        psP = [ps.enter_context(
                            nc.psum_tensor(f"fA{halft}{i}", [128, 512], F32))
                            for i in range(2)]
                        emit_l1_ffn_w12(k, Xb, H, d, wbig, psP, T, tok0, Th)
                        k.barrier()
                        emit_l1_ffn_h(k, H, H2, d["h1T"], k.ffn_scales["h1s"],
                                      wbig, psP, Th)
                        k.barrier()
                        emit_l1_ffn_h(k, H2, H, d["h2T"], k.ffn_scales["h2s"],
                                      wbig, psP, Th)
                        k.barrier()
                Xnh = sh_.enter_context(nc.sbuf_tensor(f"Xnh{halft}",
                                                       [128, 8 * Th], F32))
                for c in range(NC8):
                    k.dma("Y", Xnh[:, c * Th:(c + 1) * Th],
                          d["xpark"][c * 128:(c + 1) * 128, tok0:tok0 + Th])
                k.wait("V", k.Y, k.Y.val)
                with ExitStack() as ps:
                    psP = [ps.enter_context(
                        nc.psum_tensor(f"fC{halft}{i}", [128, 512], F32))
                        for i in range(2)]
                    emit_l1_ffn_w3(k, Xnh, H, d, wbig, psP, Th, 0, Th)
                    k.barrier()
                for c in range(NC8):
                    k.dma("Y", d["xpark"][c * 128:(c + 1) * 128, tok0:tok0 + Th],
                          Xnh[:, c * Th:(c + 1) * Th])
                k.wait("Y", k.Y, k.Y.val)
                k.barrier()
        Xn = sf.enter_context(nc.sbuf_tensor("Xn", [128, 8 * T], F32))
        for c in range(NC8):
            k.dma("Y", Xn[:, c * T:(c + 1) * T],
                  d["xpark"][c * 128:(c + 1) * 128, :])
        k.wait("V", k.Y, k.Y.val)
        k.barrier()
        emit_ln(k, "3", Xn, Xb, ones128, ones1, T, out_bf=False)
        for c in range(NC8):
            k.dma("Y", d["out"][c * 128:(c + 1) * 128, :],
                  Xn[:, c * T:(c + 1) * T])
        k.wait("Y", k.Y, k.Y.val)
    es.close()
    return nc


def build_fused(T):
    nc = bass.Bass("TRN2", target_bir_lowering=False, debug=False)
    es = ExitStack()
    ntile = T // 128
    d = declare_common(nc)
    X_ = nc.dram_tensor
    d["x_in"] = X_("x_in", [D, T], F32, kind="ExternalInput").ap()
    d["x_all"] = X_("x_all", [8 * D, T], F32, kind="ExternalInput").ap()
    d["w_in"] = X_("w_in", [D, 1072], F32, kind="ExternalInput").ap()
    d["bz"] = X_("bz", [128, 8], F32, kind="ExternalInput").ap()
    d["bbcd"] = X_("bbcd", [16, 3], F32, kind="ExternalInput").ap()
    d["alog"] = X_("alog", [16, 1], F32, kind="ExternalInput").ap()
    d["w_out"] = X_("w_out", [D, D], F32, kind="ExternalInput").ap()
    d["wo_t"] = X_("wo_t", [NC8, 128, D], F32, kind="ExternalInput").ap()
    d["bout"] = X_("bout", [128, 8], F32, kind="ExternalInput").ap()
    d["rwT"] = X_("rwT", [D, 8], F32, kind="ExternalInput").ap()
    d["im8"] = X_("im8", [8, 8], F32, kind="ExternalInput").ap()
    d["im128"] = X_("im128", [128, 8], F32, kind="ExternalInput").ap()
    d["prevmask"] = X_("prevmask", [8, 1], F32, kind="ExternalInput").ap()
    d["cpool"] = X_("cpool", [128, 8], F32, kind="ExternalInput").ap()
    d["rg"] = [X_(f"rg{e}", [IR // 128, 128, D], F32, kind="ExternalInput").ap()
               for e in range(NEXP)]
    d["ru"] = [X_(f"ru{e}", [IR // 128, 128, D], F32, kind="ExternalInput").ap()
               for e in range(NEXP)]
    d["sg"] = X_("sg", [ISH // 128, 128, D], F32, kind="ExternalInput").ap()
    d["su"] = X_("su", [ISH // 128, 128, D], F32, kind="ExternalInput").ap()
    d["sd_h"] = [X_(f"sd{h}", [NC8, 128, IR], F32, kind="ExternalInput").ap()
                 for h in range(2)]
    d["rdk"] = [X_(f"rdk{e}", [32, 128, 1024], F32, kind="ExternalInput").ap()
                for e in range(NEXP)]
    d["x2tm"] = X_("x2tm", [T, D], BF16, kind="Internal").ap()
    d["routedtm"] = X_("routedtm", [T, D], F32, kind="Internal").ap()
    d["paird"] = X_("paird", [NEXP * 1024, 2], F32, kind="Internal").ap()
    d["wqT_t"] = X_("wqT_t", [NC8, 128, D], F32, kind="ExternalInput").ap()
    d["wkvT"] = X_("wkvT", [D, 512], F32, kind="ExternalInput").ap()
    d["wp64"] = X_("wp64", [NC8, 64, 16 * 128], F32, kind="ExternalInput").ap()
    d["cos64"] = X_("cos64", [64, T], F32, kind="ExternalInput").ap()
    d["sin64"] = X_("sin64", [64, T], F32, kind="ExternalInput").ap()
    d["rot64"] = X_("rot64", [64, 64], F32, kind="ExternalInput").ap()
    d["yf"] = X_("yf", [D, T], BF16, kind="Internal").ap()
    d["w1T"] = X_("w1T", [32, 128, D], F32, kind="ExternalInput").ap()
    d["w2T"] = X_("w2T", [32, 128, D], F32, kind="ExternalInput").ap()
    d["h1T"] = X_("h1T", [32, 128, DH], F32, kind="ExternalInput").ap()
    d["h2T"] = X_("h2T", [32, 128, DH], F32, kind="ExternalInput").ap()
    d["w3T"] = X_("w3T", [8, 128, DH], F32, kind="ExternalInput").ap()
    d["s1"] = X_("s1", [128, 32], F32, kind="ExternalInput").ap()
    d["s2"] = X_("s2", [128, 32], F32, kind="ExternalInput").ap()
    d["h1s"] = X_("h1s", [128, 32], F32, kind="ExternalInput").ap()
    d["h2s"] = X_("h2s", [128, 32], F32, kind="ExternalInput").ap()
    d["s3"] = X_("s3", [128, 8], F32, kind="ExternalInput").ap()
    d["xpark"] = X_("xpark", [D, T], F32, kind="Internal").ap()
    d["out"] = X_("out", [D, T], F32, kind="ExternalOutput").ap()

    k = Kb(nc, es)
    ones128, ones1, idf = emit_consts(k, d)
    k.ffn_scales = {}
    for name in ("s1", "s2", "h1s", "h2s", "s3"):
        t_ = es.enter_context(nc.sbuf_tensor(f"sc_{name}",
                                             [128, 8 if name == "s3" else 32],
                                             F32))
        k.dma("Y", t_[:, :], d[name])
        k.ffn_scales[name] = t_
    sart = ExitStack()
    art = {
        "selE": sart.enter_context(nc.sbuf_tensor("art_se", [8, T], F32,
                                                  side="right")),
        "msk1": sart.enter_context(nc.sbuf_tensor("art_k1", [128, 8 * ntile],
                                                  F32, side="right")),
        "msk2": sart.enter_context(nc.sbuf_tensor("art_k2", [128, 8 * ntile],
                                                  F32, side="right")),
        "dp": sart.enter_context(nc.sbuf_tensor("art_dp", [128, ntile], F32,
                                                side="right")),
        "off": sart.enter_context(nc.sbuf_tensor("art_off", [8, 1], F32,
                                                 side="right")),
    }
    rstd = sart.enter_context(nc.sbuf_tensor("rstd0", [1, T], F32,
                                             side="right"))
    rankI = es.enter_context(nc.sbuf_tensor("rankI", [128, 8 * (T // 128)],
                                            I32))
    weffT = es.enter_context(nc.sbuf_tensor("weffT", [128, 8 * (T // 128)],
                                            F32))

    emit_phase0(k, d, ones128, ones1, idf, T, art)

    Xb = es.enter_context(nc.sbuf_tensor("Xb", [128, 8 * T], BF16))
    with ExitStack() as sx:
        X = sx.enter_context(nc.sbuf_tensor("X", [128, 8 * T], F32))
        for c in range(NC8):
            k.dma("Y", X[:, c * T:(c + 1) * T],
                  d["x_in"][c * 128:(c + 1) * 128, :])
        k.wait("V", k.Y, k.Y.val)
        for c in range(NC8):
            k.V.inc(nc.vector.tensor_copy(Xb[:, c * T:(c + 1) * T],
                                          X[:, c * T:(c + 1) * T]))
        emit_l0_attn(k, X, Xb, d, ones128, ones1, T)
        emit_ln(k, "0", X, Xb, ones128, ones1, T, rstd_out=rstd)
        with ExitStack() as sw:
            weff_sb = sw.enter_context(nc.sbuf_tensor("weff", [8, T], F32))
            emit_routing2(k, X, Xb, d, ones128, ones1, idf, T, art, rstd,
                          weff_sb, rankI=rankI, weffT=weffT)
            sart.close()
        k.barrier()
        for c in range(NC8):
            k.dma("Y", d["xpark"][c * 128:(c + 1) * 128, :],
                  X[:, c * T:(c + 1) * T])
        k.wait("Y", k.Y, k.Y.val)
        k.barrier()
    emit_moe_routed(k, Xb, d, ones128, ones1, idf, rankI, weffT, T)
    with ExitStack() as sx:
        X = sx.enter_context(nc.sbuf_tensor("X2", [128, 8 * T], F32))
        for c in range(NC8):
            k.dma("Y", X[:, c * T:(c + 1) * T],
                  d["xpark"][c * 128:(c + 1) * 128, :])
        k.wait("V", k.Y, k.Y.val)
        k.barrier()
        with ExitStack() as sm:
            H = sm.enter_context(nc.sbuf_tensor("Hh", [128, 16 * T], BF16))
            emit_moe_shared(k, X, Xb, H, d, ones128, ones1, idf, T)
        emit_ln(k, "1", X, Xb, ones128, ones1, T)
        emit_l1_attn(k, X, Xb, d, ones128, ones1, T)
        emit_ln(k, "2", X, Xb, ones128, ones1, T)
        k.barrier()
        for c in range(NC8):
            k.dma("Y", d["xpark"][c * 128:(c + 1) * 128, :],
                  X[:, c * T:(c + 1) * T])
        k.wait("Y", k.Y, k.Y.val)
        k.barrier()
    with ExitStack() as sf:
        Th = T // 2
        wbig = [sf.enter_context(nc.sbuf_tensor(f"f_wb{i}", [128, DH], BF16))
                for i in range(2)]
        k.ffn_sq = sf.enter_context(nc.sbuf_tensor("f_sq", [128, 512], F32))
        for halft in range(2):
            tok0 = halft * Th
            with ExitStack() as sh_:
                H = sh_.enter_context(nc.sbuf_tensor(f"Hf{halft}",
                                                     [128, 32 * Th], BF16))
                with ExitStack() as s3_:
                    H2 = s3_.enter_context(nc.sbuf_tensor(f"Hf2{halft}",
                                                          [128, 32 * Th], BF16))
                    with ExitStack() as ps:
                        psP = [ps.enter_context(
                            nc.psum_tensor(f"fA{halft}{i}", [128, 512], F32))
                            for i in range(2)]
                        emit_l1_ffn_w12(k, Xb, H, d, wbig, psP, T, tok0, Th)
                        k.barrier()
                        emit_l1_ffn_h(k, H, H2, d["h1T"], k.ffn_scales["h1s"],
                                      wbig, psP, Th)
                        k.barrier()
                        emit_l1_ffn_h(k, H2, H, d["h2T"], k.ffn_scales["h2s"],
                                      wbig, psP, Th)
                        k.barrier()
                Xnh = sh_.enter_context(nc.sbuf_tensor(f"Xnh{halft}",
                                                       [128, 8 * Th], F32))
                for c in range(NC8):
                    k.dma("Y", Xnh[:, c * Th:(c + 1) * Th],
                          d["xpark"][c * 128:(c + 1) * 128, tok0:tok0 + Th])
                k.wait("V", k.Y, k.Y.val)
                with ExitStack() as ps:
                    psP = [ps.enter_context(
                        nc.psum_tensor(f"fC{halft}{i}", [128, 512], F32))
                        for i in range(2)]
                    emit_l1_ffn_w3(k, Xnh, H, d, wbig, psP, Th, 0, Th)
                    k.barrier()
                for c in range(NC8):
                    k.dma("Y", d["xpark"][c * 128:(c + 1) * 128, tok0:tok0 + Th],
                          Xnh[:, c * Th:(c + 1) * Th])
                k.wait("Y", k.Y, k.Y.val)
                k.barrier()
        Xn = sf.enter_context(nc.sbuf_tensor("Xn", [128, 8 * T], F32))
        for c in range(NC8):
            k.dma("Y", Xn[:, c * T:(c + 1) * T],
                  d["xpark"][c * 128:(c + 1) * 128, :])
        k.wait("V", k.Y, k.Y.val)
        k.barrier()
        emit_ln(k, "3", Xn, Xb, ones128, ones1, T, out_bf=False)
        for c in range(NC8):
            k.dma("Y", d["out"][c * 128:(c + 1) * 128, :],
                  Xn[:, c * T:(c + 1) * T])
        k.wait("Y", k.Y, k.Y.val)
    es.close()
    return nc


def _tile_oc(wKN, K, N):
    a = np.ascontiguousarray(wKN, np.float32).reshape(K // 128, 128, N // 128, 128)
    return np.ascontiguousarray(a.transpose(2, 1, 0, 3).reshape(N // 128, 128, K))


def _rs(v, n):
    return np.ascontiguousarray(np.asarray(v, np.float32).reshape(n, 128).T)


def _prep(inp):
    T = TFULL
    f = np.float32
    p = {}
    x = np.asarray(inp["L0_attn_x"], f)
    p["xs"] = [np.ascontiguousarray(x[b].T) for b in range(8)]
    w_in = np.asarray(inp["L0_attn_W_in"], f)
    p["w_in"] = np.ascontiguousarray(w_in[:, 1024:])
    b_in = np.asarray(inp["L0_attn_b_in"], f)
    p["bz"] = _rs(b_in[1024:2048], 8)
    p["bbcd"] = np.ascontiguousarray(
        np.stack([b_in[2048:2064], b_in[2064:2080], b_in[2080:2096]], axis=1))
    p["alog"] = np.asarray(inp["L0_attn_A_log"], f).reshape(16, 1).copy()
    p["w_out"] = np.ascontiguousarray(np.asarray(inp["L0_attn_W_out"], f))
    p["wo_t"] = _tile_oc(p["w_out"], D, D)
    p["bout"] = _rs(np.asarray(inp["L0_attn_b_out"], f), 8)
    p["rwT"] = np.ascontiguousarray(np.asarray(inp["L0_ffn_router_w"], f).T)
    rg = np.asarray(inp["L0_ffn_rg"], f)
    ru = np.asarray(inp["L0_ffn_ru"], f)
    rd = np.asarray(inp["L0_ffn_rd"], f)
    p["rg"] = [_tile_oc(rg[e], D, IR) for e in range(NEXP)]
    p["ru"] = [_tile_oc(ru[e], D, IR) for e in range(NEXP)]
    p["rd"] = [_tile_oc(rd[e], IR, D) for e in range(NEXP)]
    p["rdk"] = [np.ascontiguousarray(rd[e].reshape(32, 128, 1024))
                for e in range(NEXP)]
    p["sg"] = _tile_oc(np.asarray(inp["L0_ffn_sg"], f), D, ISH)
    p["su"] = _tile_oc(np.asarray(inp["L0_ffn_su"], f), D, ISH)
    sd = np.asarray(inp["L0_ffn_sd"], f)
    p["sd_h"] = [_tile_oc(sd[h * IR:(h + 1) * IR, :], IR, D) for h in range(2)]
    p["wqT_t"] = _tile_oc(np.asarray(inp["L1_attn_Wq"], f).T.copy(), D, D)
    p["wkvT"] = np.ascontiguousarray(np.asarray(inp["L1_attn_Wkv"], f).T)
    wpt = np.asarray(inp["L1_attn_Wproj"], f).T.copy()
    p["wp64"] = np.ascontiguousarray(
        wpt.reshape(16, 64, 8, 128).transpose(2, 1, 0, 3).reshape(8, 64, 2048))
    p["w1T"] = _tile_oc(np.asarray(inp["L1_ffn_w1_w"], f).T.copy(), D, DH)
    p["w2T"] = _tile_oc(np.asarray(inp["L1_ffn_w2_w"], f).T.copy(), D, DH)
    p["h1T"] = _tile_oc(np.asarray(inp["L1_ffn_h1_w"], f).T.copy(), DH, DH)
    p["h2T"] = _tile_oc(np.asarray(inp["L1_ffn_h2_w"], f).T.copy(), DH, DH)
    p["w3T"] = _tile_oc(np.asarray(inp["L1_ffn_w3_w"], f).T.copy(), DH, D)
    p["s1"] = _rs(inp["L1_ffn_w1_s"], 32)
    p["s2"] = _rs(inp["L1_ffn_w2_s"], 32)
    p["h1s"] = _rs(inp["L1_ffn_h1_s"], 32)
    p["h2s"] = _rs(inp["L1_ffn_h2_s"], 32)
    p["s3"] = _rs(inp["L1_ffn_w3_s"], 8)
    theta = 1.0 / (10000.0 ** (np.arange(0, HDIM, 2, dtype=np.float64) / HDIM))
    pos = np.arange(T, dtype=np.float64)
    fr = np.outer(pos, theta)
    cos = np.concatenate([np.cos(fr), np.cos(fr)], axis=1).astype(f)
    sin = np.concatenate([np.sin(fr), np.sin(fr)], axis=1).astype(f)
    p["cos64"] = np.ascontiguousarray(cos.T)
    p["sin64"] = np.ascontiguousarray(sin.T)
    R = np.zeros((HDIM, HDIM), f)
    for i in range(32):
        R[i, 32 + i] = -1.0
        R[32 + i, i] = 1.0
    p["rot64"] = np.ascontiguousarray(R.T)
    cp = np.zeros((128, 8), f)
    cp[:, 0] = 1e-5
    cp[:, 1] = -0.5
    cp[:, 2] = F32EPS
    cp[:, 3] = 1.0 / D
    cp[:, 4] = -1.0
    cp[:, 5] = 1.0
    cp[:, 6] = 0.125
    cp[:, 7] = 0.5
    p["cpool"] = cp
    p["prevmask"] = [np.ascontiguousarray(
        (np.arange(8) < c).astype(f).reshape(8, 1)) for c in range(8)]
    p["x_all"] = np.ascontiguousarray(np.concatenate(p["xs"], axis=0))
    im8 = []
    im128 = []
    for c in range(8):
        a = np.zeros((8, 8), f)
        a[:, c] = 1.0
        im8.append(a)
        b = np.zeros((128, 8), f)
        b[:, c] = 1.0
        im128.append(b)
    p["im8"] = im8
    p["im128"] = im128
    return p


def kernel(**inputs):
    p = _prep(inputs)
    T = TFULL
    nc = build_fused(T)
    in_maps = []
    for c in range(8):
        m = {"x_in": p["xs"][c], "x_all": p["x_all"], "w_in": p["w_in"],
             "bz": p["bz"], "bbcd": p["bbcd"], "alog": p["alog"],
             "w_out": p["w_out"], "wo_t": p["wo_t"], "bout": p["bout"],
             "rwT": p["rwT"],
             "im8": p["im8"][c], "im128": p["im128"][c],
             "prevmask": p["prevmask"][c], "cpool": p["cpool"],
             "sg": p["sg"], "su": p["su"], "wqT_t": p["wqT_t"],
             "wkvT": p["wkvT"], "wp64": p["wp64"], "cos64": p["cos64"],
             "sin64": p["sin64"], "rot64": p["rot64"],
             "w1T": p["w1T"], "w2T": p["w2T"], "h1T": p["h1T"],
             "h2T": p["h2T"], "w3T": p["w3T"], "s1": p["s1"], "s2": p["s2"],
             "h1s": p["h1s"], "h2s": p["h2s"], "s3": p["s3"]}
        for e in range(NEXP):
            m[f"rg{e}"] = p["rg"][e]
            m[f"ru{e}"] = p["ru"][e]
            m[f"rdk{e}"] = p["rdk"][e]
        for h in range(2):
            m[f"sd{h}"] = p["sd_h"][h]
        in_maps.append(m)
    res = run_bass_kernel_spmd(nc, in_maps, core_ids=list(range(8)))
    out = np.stack([np.ascontiguousarray(res.results[c]["out"].T)
                    for c in range(8)], axis=0)
    return out
